# version 10
# baseline (speedup 1.0000x reference)
import math
from contextlib import ExitStack

import numpy as np
import concourse.bass as bass
import concourse.mybir as mybir
from concourse.bass_utils import run_bass_kernel_spmd

F32 = mybir.dt.float32
BF16 = mybir.dt.bfloat16
AF = mybir.ActivationFunctionType
ALU = mybir.AluOpType
AX = mybir.AxisListType

NCORES = 8
LORA_SCALE = 4.0
NEXP = 4
LORA_R = 16
ER = NEXP * LORA_R
EPS = 1e-5
P = 128


class Sem:

    def __init__(self, ctx, nc, name, registry):
        self.sem = ctx.enter_context(nc.semaphore(name))
        self.n = 0
        registry.append(self)

    def inc(self, instr, k=1):
        instr.then_inc(self.sem, k)
        self.n += k
        return self.n

    def dma(self, instr):
        return self.inc(instr, 16)

    def wait(self, eng, val=None):
        eng.wait_ge(self.sem, self.n if val is None else val)


def build_program(D, S):
    T = S // NCORES
    KC = D // P
    HEADS = D // P
    OCH = 2 * D // P
    VOG = D // 512
    QT = T // P
    NQG = T // 512
    KT = S // P
    NPAIR = KT // 2
    NG = KC // 8
    NQTG = QT // 4
    HPV = 4
    assert T % 512 == 0 and D % 1024 == 0 and QT % 4 == 0

    nc = bass.Bass("TRN2", target_bir_lowering=False)

    x_d = nc.dram_tensor("x", [T, D], F32, kind="ExternalInput")
    wqk_d = nc.dram_tensor("w_qk", [OCH, P, D], F32, kind="ExternalInput")
    wv_d = nc.dram_tensor("w_v", [KC, P, D], F32, kind="ExternalInput")
    wo_d = nc.dram_tensor("w_o", [KC, P, D], F32, kind="ExternalInput")
    bqk_d = nc.dram_tensor("b_qk", [P, OCH], F32, kind="ExternalInput")
    bv_d = nc.dram_tensor("b_v", [1, D], F32, kind="ExternalInput")
    bo_d = nc.dram_tensor("b_o", [1, D], F32, kind="ExternalInput")
    wa_d = nc.dram_tensor("w_a", [KC, P, ER], F32, kind="ExternalInput")
    wb_d = nc.dram_tensor("w_b", [ER, D], F32, kind="ExternalInput")
    wr_d = nc.dram_tensor("w_r", [KC, P, NEXP], F32, kind="ExternalInput")
    id_d = nc.dram_tensor("ident", [P, P], F32, kind="ExternalInput")
    ind_d = nc.dram_tensor("ind4", [NEXP, ER], F32, kind="ExternalInput")

    out_d = nc.dram_tensor("out", [T, D], F32, kind="ExternalOutput")

    cc_in = nc.dram_tensor("cc_in", [2 * D, T], BF16, kind="Internal")
    cck_out = nc.dram_tensor("cck_out", [NCORES * D, T], BF16,
                             kind="Internal", addr_space="Shared")
    ccv_out = nc.dram_tensor("ccv_out", [NCORES * D, T], BF16,
                             kind="Internal", addr_space="Shared")
    qs_d = nc.dram_tensor("qs", [D, T], BF16, kind="Internal")
    x1_d = nc.dram_tensor("x1", [T, D], F32, kind="Internal")

    gctx = ExitStack()
    REG = []

    def sb(name, shape, dtype):
        return gctx.enter_context(nc.sbuf_tensor(name, shape, dtype))

    ident_bf = sb("ident_bf", [P, P], BF16)
    ident_f = sb("ident_f", [P, P], F32)
    bqk_sb = sb("bqk_sb", [P, OCH], F32)
    bv_sb = sb("bv_sb", [1, D], BF16)
    bo_sb = sb("bo_sb", [1, D], BF16)
    wa_sb = sb("wa_sb", [P, KC * ER], BF16)
    wb_sb = sb("wb_sb", [ER, D], BF16)
    wr_sb = sb("wr_sb", [P, KC * NEXP], BF16)
    ind_sb = sb("ind_sb", [NEXP, ER], F32)
    ones_c = sb("ones_c", [P, 1], BF16)
    ones_r = sb("ones_r", [1, P], BF16)
    ones_rf = sb("ones_rf", [1, P], F32)
    dump = sb("dump", [P, D], BF16)
    st = sb("st", [P, 32], F32)

    def mks(name):
        return Sem(gctx, nc, name, REG)

    def mkslots(name, k):
        return [mks(f"{name}{j}") for j in range(k)]

    s_small = mks("s_small")
    s_x = mkslots("s_x", 2)
    s_wq = mkslots("s_wq", 2)
    s_wv = mkslots("s_wv", 2)
    s_wo = mkslots("s_wo", 4)
    s_kv = mkslots("s_kv", 2)
    s_x1 = mkslots("s_x1", 2)
    s_x1b = mkslots("s_x1b", 2)
    s_xr = mkslots("s_xr", 4)
    s_cc = mks("s_cc")
    s_dveM = mks("s_dveM")
    s_dveLN = mks("s_dveLN")
    s_actLN = mks("s_actLN")
    s_peTR = mks("s_peTR")
    s_dveTR = mks("s_dveTR")
    s_peQK = mks("s_peQK")
    s_actQK = mks("s_actQK")
    s_stQK = mkslots("s_stQK", 4)
    s_peV = mks("s_peV")
    s_peV2 = mks("s_peV2")
    s_dveV = mks("s_dveV")
    s_stV = mkslots("s_stV", 2)
    s_peS = mks("s_peS")
    s_peP = mks("s_peP")
    s_peZ4 = mks("s_peZ4")
    s_dveQ = mks("s_dveQ")
    s_dveN = mks("s_dveN")
    s_actE = mks("s_actE")
    s_peH = mks("s_peH")
    s_dveZ = mks("s_dveZ")
    s_peZB = mks("s_peZB")
    s_dveC = mks("s_dveC")
    s_peO = mks("s_peO")
    s_dveX = mks("s_dveX")
    s_stX1 = mkslots("s_stX1", 4)
    s_dveL2 = mks("s_dveL2")
    s_actL2 = mks("s_actL2")
    s_peTR2 = mks("s_peTR2")
    s_dveTR2 = mks("s_dveTR2")
    s_peLG = mks("s_peLG")
    s_peMID = mks("s_peMID")
    s_peGT = mks("s_peGT")
    s_peGB = mks("s_peGB")
    s_peD = mks("s_peD")
    s_dveG = mks("s_dveG")
    s_actG = mks("s_actG")
    s_dveE2 = mks("s_dveE2")
    s_dveF = mks("s_dveF")
    s_outF = mkslots("s_outF", 2)

    block = gctx.enter_context(nc.Block())

    def barrier(eng, snap=None):
        vals = snap if snap is not None else [s.n for s in REG]
        for s, v_ in zip(REG, vals):
            if v_:
                eng.wait_ge(s.sem, v_)

    SC = float(1.0 / math.sqrt(P))

    def chain(eng, sem, instr_fn):
        if sem.n:
            eng.wait_ge(sem.sem, sem.n)
        return sem.inc(instr_fn())

    with gctx:
        def p0_gpsimd(g):
            s_small.dma(g.dma_start(out=ident_bf[:], in_=id_d.ap()))
            s_small.dma(g.dma_start(out=ident_f[:], in_=id_d.ap()))
            s_small.dma(g.dma_start(out=bqk_sb[:], in_=bqk_d.ap()))
            s_small.dma(g.dma_start(out=bv_sb[:], in_=bv_d.ap()))
            s_small.dma(g.dma_start(out=bo_sb[:], in_=bo_d.ap()))
            s_small.dma(g.dma_start(
                out=wa_sb.ap().rearrange("p (kc e) -> p kc e", kc=KC),
                in_=wa_d.ap().rearrange("kc p e -> p kc e")))
            s_small.dma(g.dma_start(out=wb_sb[:], in_=wb_d.ap()))
            s_small.dma(g.dma_start(
                out=wr_sb.ap().rearrange("p (kc e) -> p kc e", kc=KC),
                in_=wr_d.ap().rearrange("kc p e -> p kc e")))
            s_small.dma(g.dma_start(out=ind_sb[:], in_=ind_d.ap()))
        block.gpsimd(p0_gpsimd)

        def p0_vector(v):
            v.memset(ones_c[:], 1.0)
            v.memset(ones_r[:], 1.0)
            s_dveM.inc(v.memset(ones_rf[:], 1.0))
        block.vector(p0_vector)

        p1 = ExitStack()
        x_sb = p1.enter_context(nc.sbuf_tensor("x_sb", [P, 2 * D], F32))
        ln_tm = p1.enter_context(nc.sbuf_tensor("ln_tm", [P, D], BF16))
        ln_fm = p1.enter_context(nc.sbuf_tensor("ln_fm", [P, KC * T], BF16))
        wqk_sb = p1.enter_context(nc.sbuf_tensor("wqk_sb", [P, 2 * D], BF16))
        stage_qk = p1.enter_context(nc.sbuf_tensor("stage_qk", [P, 4 * 1024], BF16))
        wv_sb = p1.enter_context(nc.sbuf_tensor("wv_sb", [P, KC // 2 * 2 * 512], BF16))
        stage_v = p1.enter_context(nc.sbuf_tensor("stage_v", [P, 2 * 4 * 512], BF16))
        qkps = [p1.enter_context(nc.psum_tensor(f"qkps{i}", [P, 1024], F32))
                for i in range(2)]
        ps_ln = ExitStack()
        tps_ln = ps_ln.enter_context(nc.psum_tensor("tps_ln", [P, 2 * 1024], BF16))

        ln_fm_r = ln_fm.ap().rearrange("p (kc t) -> p kc t", kc=KC)

        def p1_sync(sy):
            for qt in range(QT):
                if qt >= 2:
                    sy.wait_ge(s_dveLN.sem, 7 * (qt - 2) + 7)
                    sy.wait_ge(s_actLN.sem, 3 * (qt - 2) + 1)
                s_x[qt % 2].dma(sy.dma_start(
                    out=x_sb[:, (qt % 2) * D:(qt % 2) * D + D],
                    in_=x_d.ap()[qt * P:(qt + 1) * P, :]))
        block.sync(p1_sync)

        def emit_ln_vec(v, qt, xap, sem_d, sem_a, s_petr, ln_tm_t, war_lag=2):
            b7 = 7 * qt
            chain(v, sem_d, lambda: v.reduce_sum(out=st[:, 0:1], in_=xap, axis=AX.X))
            chain(v, sem_d, lambda: v.tensor_scalar_mul(st[:, 1:2], st[:, 0:1], 1.0 / D))
            chain(v, sem_d, lambda: v.tensor_tensor(out=st[:, 2:3], in0=st[:, 1:2],
                                                    in1=st[:, 1:2], op=ALU.mult))
            v.wait_ge(sem_a.sem, 3 * qt + 1)
            chain(v, sem_d, lambda: v.tensor_scalar(out=st[:, 3:4], in0=st[:, 4:5],
                                                    scalar1=1.0 / D, scalar2=float(EPS),
                                                    op0=ALU.mult, op1=ALU.add))
            chain(v, sem_d, lambda: v.tensor_tensor(out=st[:, 5:6], in0=st[:, 3:4],
                                                    in1=st[:, 2:3], op=ALU.subtract))
            v.wait_ge(sem_a.sem, 3 * qt + 3)
            chain(v, sem_d, lambda: v.tensor_scalar(out=st[:, 7:8], in0=st[:, 1:2],
                                                    scalar1=st[:, 6:7], scalar2=-1.0,
                                                    op0=ALU.mult, op1=ALU.mult))
            if qt >= war_lag:
                v.wait_ge(s_petr.sem, NG * (qt - war_lag) + NG)
            nb = (qt % 2) * D if war_lag == 2 else 0
            chain(v, sem_d, lambda: v.tensor_scalar(
                out=ln_tm_t[:, nb:nb + D], in0=xap,
                scalar1=st[:, 6:7], scalar2=st[:, 7:8],
                op0=ALU.mult, op1=ALU.add))
            assert sem_d.n == b7 + 7

        def emit_ln_act(sc, qt, xap, sem_d, sem_a, wait_x):
            wait_x()
            if qt >= 1:
                sc.wait_ge(sem_d.sem, 7 * (qt - 1) + 4)
            chain(sc, sem_a, lambda: sc.activation(out=dump[:], in_=xap,
                                                   func=AF.Square,
                                                   accum_out=st[:, 4:5]))
            sc.wait_ge(sem_d.sem, 7 * qt + 5)
            chain(sc, sem_a, lambda: sc.activation(out=st[:, 8:9], in_=st[:, 5:6],
                                                   func=AF.Ln))
            chain(sc, sem_a, lambda: sc.activation(out=st[:, 6:7], in_=st[:, 8:9],
                                                   func=AF.Exp, scale=-0.5))
            assert sem_a.n == 3 * qt + 3

        def p1_vec_ln(v):
            for qt in range(QT):
                v.wait_ge(s_x[qt % 2].sem, 16 * (qt // 2 + 1))
                emit_ln_vec(v, qt, x_sb[:, (qt % 2) * D:(qt % 2) * D + D],
                            s_dveLN, s_actLN, s_peTR, ln_tm, war_lag=1)
                for g in range(NG):
                    G = NG * qt + g
                    v.wait_ge(s_peTR.sem, G + 1)
                    if s_dveTR.n:
                        v.wait_ge(s_dveTR.sem, s_dveTR.n)
                    s_dveTR.inc(v.tensor_copy(
                        out=ln_fm_r[:, g * 8:(g + 1) * 8, qt * P:(qt + 1) * P],
                        in_=tps_ln.ap()[:, (G % 2) * 1024:(G % 2) * 1024 + 1024]
                        .rearrange("p (kc t) -> p kc t", kc=8)))
        block.vector(p1_vec_ln)

        def p1_scalar_ln(sc):
            for qt in range(QT):
                emit_ln_act(sc, qt, x_sb[:, (qt % 2) * D:(qt % 2) * D + D],
                            s_dveLN, s_actLN,
                            lambda qt=qt: sc.wait_ge(s_x[qt % 2].sem,
                                                     16 * (qt // 2 + 1)))
        block.scalar(p1_scalar_ln)

        def p1_pe_tr(t):
            s_small.wait(t)
            t.wait_ge(s_dveM.sem, 1)
            for qt in range(QT):
                t.wait_ge(s_dveLN.sem, 7 * qt + 7)
                for g in range(NG):
                    G = NG * qt + g
                    if G >= 2:
                        t.wait_ge(s_dveTR.sem, G - 1)
                    for j in range(8):
                        kc = g * 8 + j
                        mm = t.transpose(
                            tps_ln[:, (G % 2) * 1024 + j * P:(G % 2) * 1024 + (j + 1) * P],
                            in_=ln_tm[:, kc * P:(kc + 1) * P],
                            identity=ident_bf[:])
                    s_peTR.inc(mm)
        block.tensor(p1_pe_tr)

        order = list(range(HEADS, OCH)) + list(range(0, HEADS))
        NKO = HEADS

        def emit_wqk_loads(g, i0, i1):
            for i in range(i0, i1):
                if i >= 2:
                    g.wait_ge(s_peQK.sem, i - 1)
                s_wq[i % 2].dma(g.dma_start(
                    out=wqk_sb[:, (i % 2) * D:(i % 2) * D + D],
                    in_=wqk_d.ap()[order[i]]))

        def emit_qk_compute(t, i0, i1):
            for i in range(i0, i1):
                t.wait_ge(s_wq[i % 2].sem, 16 * (i // 2 + 1))
                if i == 0:
                    t.wait_ge(s_dveTR.sem, NG * QT)
                if i >= 2:
                    t.wait_ge(s_actQK.sem, i - 1)
                for kc in range(KC):
                    for qg in range(NQG):
                        mm = t.matmul(
                            qkps[i % 2][:, qg * 512:(qg + 1) * 512],
                            lhsT=wqk_sb[:, (i % 2) * D + kc * P:(i % 2) * D + (kc + 1) * P],
                            rhs=ln_fm[:, kc * T + qg * 512:kc * T + qg * 512 + 512],
                            start=(kc == 0), stop=(kc == KC - 1))
                s_peQK.inc(mm)

        def emit_qk_evac(sc, i0, i1):
            for i in range(i0, i1):
                if i == 0:
                    s_small.wait(sc)
                sc.wait_ge(s_peQK.sem, i + 1)
                if i >= 4:
                    sc.wait_ge(s_stQK[i % 4].sem, 16 * (i // 4))
                och = order[i]
                chain(sc, s_actQK, lambda i=i, och=och: sc.activation(
                    out=stage_qk[:, (i % 4) * 1024:(i % 4) * 1024 + T],
                    in_=qkps[i % 2][:, 0:T], func=AF.Identity,
                    bias=bqk_sb[:, och:och + 1]))

        def emit_qk_store(sy, i0, i1):
            for i in range(i0, i1):
                sy.wait_ge(s_actQK.sem, i + 1)
                och = order[i]
                if och >= HEADS:
                    hk = och - HEADS
                    dst = cc_in.ap()[hk * P:(hk + 1) * P, :]
                else:
                    dst = qs_d.ap()[och * P:(och + 1) * P, :]
                s_stQK[i % 4].dma(sy.dma_start(
                    out=dst, in_=stage_qk[:, (i % 4) * 1024:(i % 4) * 1024 + T]))

        block.gpsimd(lambda g: emit_wqk_loads(g, 0, NKO))

        def p1_gps_cck(g):
            for s in s_stQK:
                if s.n:
                    g.wait_ge(s.sem, s.n)
            s_cc.inc(g.collective_compute(
                "AllGather", ALU.bypass,
                replica_groups=[list(range(NCORES))],
                ins=[cc_in.ap()[0:D, :]], outs=[cck_out.ap()]))

        block.tensor(lambda t: emit_qk_compute(t, 0, NKO))
        block.scalar(lambda sc: emit_qk_evac(sc, 0, NKO))
        block.sync(lambda sy: emit_qk_store(sy, 0, NKO))

        ps_ln.close()
        vps = [p1.enter_context(nc.psum_tensor(f"vps{i}", [P, 512], F32))
               for i in range(4)]
        KH = KC // 2

        def p1_gps_wv(g, v0, v1):
            for vog in range(v0, v1):
                for half in range(2):
                    if vog >= 1:
                        g.wait_ge(s_peV.sem,
                                  2 * NQTG * (vog - 1) + 2 * (NQTG - 1) + half + 1)
                    kc0 = half * KH
                    s_wv[half].dma(g.dma_start(
                        out=wv_sb.ap()[:, half * KH * 512:(half + 1) * KH * 512]
                        .rearrange("p (kc c) -> p kc c", kc=KH),
                        in_=wv_d.ap()[kc0:kc0 + KH, :, vog * 512:(vog + 1) * 512]
                        .rearrange("kc p c -> p kc c")))
        block.gpsimd(lambda g: p1_gps_wv(g, 0, 1))
        block.gpsimd(p1_gps_cck)
        block.gpsimd(lambda g: p1_gps_wv(g, 1, VOG))

        def p1_pe_v(t):
            for vog in range(VOG):
                for qtg in range(NQTG):
                    gi = NQTG * vog + qtg
                    if gi >= 1:
                        t.wait_ge(s_dveV.sem, 4 * gi)
                    for half in range(2):
                        t.wait_ge(s_wv[half].sem, 16 * (vog + 1))
                        for kcl in range(KH):
                            kc = half * KH + kcl
                            for q4 in range(4):
                                qt = qtg * 4 + q4
                                mm = t.matmul(
                                    vps[q4][:],
                                    lhsT=ln_fm[:, kc * T + qt * P:kc * T + (qt + 1) * P],
                                    rhs=wv_sb[:, (half * KH + kcl) * 512:
                                              (half * KH + kcl) * 512 + 512],
                                    start=(kc == 0), stop=False)
                        s_peV.inc(mm)
                    for q4 in range(4):
                        mm = t.matmul(vps[q4][:], lhsT=ones_r[:],
                                      rhs=bv_sb[0:1, vog * 512:(vog + 1) * 512],
                                      start=False, stop=True)
                        s_peV2.inc(mm)
        block.tensor(p1_pe_v)

        def p1_vec_v(v):
            for vog in range(VOG):
                for qtg in range(NQTG):
                    gi = NQTG * vog + qtg
                    for q4 in range(4):
                        v.wait_ge(s_peV2.sem, 4 * gi + q4 + 1)
                        if gi >= 2:
                            v.wait_ge(s_stV[gi % 2].sem, 16 * HPV * (gi // 2))
                        if s_dveV.n:
                            v.wait_ge(s_dveV.sem, s_dveV.n)
                        s_dveV.inc(v.tensor_copy(
                            out=stage_v[:, (gi % 2) * 2048 + q4 * 512:
                                        (gi % 2) * 2048 + (q4 + 1) * 512],
                            in_=vps[q4][:]))
        block.vector(p1_vec_v)

        def p1_sync_v(sy):
            for vog in range(VOG):
                for qtg in range(NQTG):
                    gi = NQTG * vog + qtg
                    sy.wait_ge(s_dveV.sem, 4 * gi + 4)
                    for j in range(HPV):
                        hv = vog * HPV + j
                        dst = cc_in.ap()[D + hv * P:D + (hv + 1) * P,
                                         qtg * 4 * P:(qtg + 1) * 4 * P] \
                            .rearrange("p (qt c) -> p qt c", qt=4)
                        src = stage_v.ap()[:, (gi % 2) * 2048:(gi % 2) * 2048 + 2048] \
                            .rearrange("p (qt c) -> p qt c", qt=4)[:, :, j * P:(j + 1) * P]
                        s_stV[gi % 2].dma(sy.dma_start(out=dst, in_=src))
        block.sync(p1_sync_v)

        def p1_gps_ccv(g):
            for s in s_stV:
                if s.n:
                    g.wait_ge(s.sem, s.n)
            s_cc.inc(g.collective_compute(
                "AllGather", ALU.bypass,
                replica_groups=[list(range(NCORES))],
                ins=[cc_in.ap()[D:2 * D, :]], outs=[ccv_out.ap()]))
        block.gpsimd(lambda g: emit_wqk_loads(g, NKO, NKO + 3))
        block.gpsimd(p1_gps_ccv)

        block.gpsimd(lambda g: emit_wqk_loads(g, NKO + 3, OCH))
        block.tensor(lambda t: emit_qk_compute(t, NKO, OCH))
        block.scalar(lambda sc: emit_qk_evac(sc, NKO, OCH))
        block.sync(lambda sy: emit_qk_store(sy, NKO, OCH))

        p1.close()

        p2 = ExitStack()
        ctx_sb = p2.enter_context(nc.sbuf_tensor("ctx_sb", [P, HEADS * T], BF16))
        sps = [p2.enter_context(nc.psum_tensor(f"sps{i}", [P, 1024], F32))
               for i in range(NQG)]
        ctxp = [p2.enter_context(nc.psum_tensor(f"ctxp{i}", [P, 512], F32))
                for i in range(NQG)]
        zzp = [p2.enter_context(nc.psum_tensor(f"zzp{i}", [P, 512], F32))
               for i in range(NQG)]
        p2a = ExitStack()
        ksb = p2a.enter_context(nc.sbuf_tensor("ksb", [P, 2 * S], BF16))
        vsb = p2a.enter_context(nc.sbuf_tensor("vsb", [P, 2 * S], BF16))
        qsb = p2a.enter_context(nc.sbuf_tensor("qsb", [P, 2 * T], BF16))
        esb = p2a.enter_context(nc.sbuf_tensor("esb", [P, 8 * 1024], BF16))
        eq_sb = p2a.enter_context(nc.sbuf_tensor("eq_sb", [P, NQG * 1024], BF16))
        e4_sb = p2a.enter_context(nc.sbuf_tensor("e4_sb", [P, NQG * 1024], BF16))
        zinv = p2a.enter_context(nc.sbuf_tensor("zinv", [1, NQG * 512], F32))
        zb_sb = p2a.enter_context(nc.sbuf_tensor("zb_sb", [P, NQG * 512], F32))

        NL = 1 + 2 * NCORES
        snap2 = [s.n for s in REG]

        def p2_sync(sy):
            barrier(sy, snap2)
            sy.wait_ge(s_cc.sem, 2)
            for h in range(HEADS):
                if h >= 2:
                    sy.wait_ge(s_peH.sem, h - 1)
                hb = h % 2
                s_kv[hb].dma(sy.dma_start(out=qsb[:, hb * T:hb * T + T],
                                          in_=qs_d.ap()[h * P:(h + 1) * P, :]))
                for r in range(NCORES):
                    s_kv[hb].dma(sy.dma_start(
                        out=ksb[:, hb * S + r * T:hb * S + (r + 1) * T],
                        in_=cck_out.ap()[r * D + h * P:r * D + (h + 1) * P, :]))
                for r in range(NCORES):
                    s_kv[hb].dma(sy.dma_start(
                        out=vsb.ap()[:, hb * S + r * T:hb * S + (r + 1) * T]
                        .rearrange("p (qt c) -> p qt c", qt=QT),
                        in_=ccv_out.ap()[r * D + h * P:r * D + (h + 1) * P, :]
                        .rearrange("p (qt c) -> p qt c", qt=QT)))
        block.sync(p2_sync)

        NQUAD = KT // 4
        QGS = list(range(NQG))

        def p2_tensor(t):
            barrier(t, snap2)
            for h in range(HEADS):
                hb = h % 2
                t.wait_ge(s_kv[h % 2].sem, 16 * NL * (h // 2 + 1))
                bE = NQG * NPAIR * h
                bP = NQG * NPAIR * h
                bZ = NQG * NQUAD * h
                if h >= 1:
                    for qg in QGS:
                        t.wait_ge(s_dveZ.sem, NQG * (h - 1) + qg + 1)
                        s_peZB.inc(t.matmul(
                            zzp[qg][:], lhsT=ones_rf[:],
                            rhs=zinv[0:1, qg * 512:(qg + 1) * 512],
                            start=True, stop=True))
                    t.wait_ge(s_actE.sem, bE)
                for p in range(NPAIR):
                    if p >= 1:
                        pv = p - 1
                        for qg in QGS:
                            t.wait_ge(s_actE.sem, bE + NQG * pv + qg + 1)
                            if pv == 0:
                                if h >= 1:
                                    t.wait_ge(s_dveC.sem, NQG * (h - 1) + qg + 1)
                            sl = (NQG * pv + qg) % 8
                            for j in range(2):
                                kt = 2 * pv + j
                                mm = t.matmul(
                                    ctxp[qg][:],
                                    lhsT=vsb[:, hb * S + kt * P:hb * S + (kt + 1) * P],
                                    rhs=esb[:, sl * 1024 + j * 512:sl * 1024 + (j + 1) * 512],
                                    start=(kt == 0), stop=False)
                            s_peP.inc(mm)
                            if pv % 2 == 1:
                                q = pv // 2
                                t.wait_ge(s_dveQ.sem, 3 * NQG * NQUAD * h + 3 * NQG * q + 3 * qg + 3)
                                if q == 0 and h >= 1:
                                    t.wait_ge(s_dveN.sem, 2 * NQG * (h - 1) + 2 * qg + 1)
                                s_peZ4.inc(t.matmul(
                                    zzp[qg][0:1, :], lhsT=ones_c[:],
                                    rhs=e4_sb[:, qg * 1024 + (q % 2) * 512:
                                              qg * 1024 + (q % 2) * 512 + 512],
                                    start=(q == 0), stop=(q == NQUAD - 1)))
                    for qg in QGS:
                        if p >= 1:
                            t.wait_ge(s_actE.sem, bE + NQG * (p - 1) + qg + 1)
                        for j in range(2):
                            kt = 2 * p + j
                            mm = t.matmul(
                                sps[qg][:, j * 512:(j + 1) * 512],
                                lhsT=ksb[:, hb * S + kt * P:hb * S + (kt + 1) * P],
                                rhs=qsb[:, hb * T + qg * 512:hb * T + qg * 512 + 512],
                                start=True, stop=True)
                        s_peS.inc(mm)
                pv = NPAIR - 1
                for qg in QGS:
                    t.wait_ge(s_actE.sem, bE + NQG * pv + qg + 1)
                    sl = (NQG * pv + qg) % 8
                    for j in range(2):
                        kt = 2 * pv + j
                        mm = t.matmul(
                            ctxp[qg][:],
                            lhsT=vsb[:, hb * S + kt * P:hb * S + (kt + 1) * P],
                            rhs=esb[:, sl * 1024 + j * 512:sl * 1024 + (j + 1) * 512],
                            start=False, stop=(j == 1))
                    s_peP.inc(mm)
                    q = pv // 2
                    t.wait_ge(s_dveQ.sem, 3 * NQG * NQUAD * h + 3 * NQG * q + 3 * qg + 3)
                    s_peZ4.inc(t.matmul(
                        zzp[qg][0:1, :], lhsT=ones_c[:],
                        rhs=e4_sb[:, qg * 1024 + (q % 2) * 512:
                                  qg * 1024 + (q % 2) * 512 + 512],
                        start=False, stop=True))
                t.wait_ge(s_peP.sem, bP + NQG * NPAIR)
                t.sem_inc(s_peH.sem, 1)
                s_peH.n += 1
            for qg in QGS:
                t.wait_ge(s_dveZ.sem, NQG * (HEADS - 1) + qg + 1)
                s_peZB.inc(t.matmul(zzp[qg][:], lhsT=ones_rf[:],
                                    rhs=zinv[0:1, qg * 512:(qg + 1) * 512],
                                    start=True, stop=True))
        block.tensor(p2_tensor)

        def p2_scalar(sc):
            barrier(sc, snap2)
            for h in range(HEADS):
                bS = NQG * NPAIR * h
                bP = NQG * NPAIR * h
                bQ = 3 * NQG * NQUAD * h
                for p in range(NPAIR):
                    for qg in QGS:
                        e = NQG * p + qg
                        sc.wait_ge(s_peS.sem, bS + e + 1)
                        if p >= 8 // NQG:
                            pp = p - 8 // NQG
                            sc.wait_ge(s_peP.sem, bP + NQG * pp + qg + 1)
                            sc.wait_ge(s_dveQ.sem,
                                       bQ + 3 * NQG * (pp // 2) + 3 * qg + (pp % 2) + 1)
                        elif h >= 1:
                            sc.wait_ge(s_peP.sem, bP)
                            sc.wait_ge(s_dveQ.sem, bQ)
                        chain(sc, s_actE, lambda qg=qg, e=e: sc.activation(
                            out=esb[:, (e % 8) * 1024:(e % 8) * 1024 + 1024],
                            in_=sps[qg][:], func=AF.Exp, scale=SC))
        block.scalar(p2_scalar)

        def p2_vector(v):
            barrier(v, snap2)
            for h in range(HEADS):
                bE = NQG * NPAIR * h
                bZ = NQG * NQUAD * h
                if h >= 1:
                    for qg in QGS:
                        v.wait_ge(s_peZB.sem, NQG * (h - 1) + qg + 1)
                        chain(v, s_dveN, lambda qg=qg: v.tensor_copy(
                            out=zb_sb[:, qg * 512:(qg + 1) * 512], in_=zzp[qg][:]))
                        chain(v, s_dveN, lambda qg=qg, h=h: v.tensor_tensor(
                            out=ctx_sb[:, (h - 1) * T + qg * 512:
                                       (h - 1) * T + qg * 512 + 512],
                            in0=ctx_sb[:, (h - 1) * T + qg * 512:
                                       (h - 1) * T + qg * 512 + 512],
                            in1=zb_sb[:, qg * 512:(qg + 1) * 512], op=ALU.mult))
                for q in range(NQUAD):
                    for qg in QGS:
                        p0, p1 = 2 * q, 2 * q + 1
                        e0 = (NQG * p0 + qg) % 8
                        e1 = (NQG * p1 + qg) % 8
                        v.wait_ge(s_actE.sem, bE + NQG * p0 + qg + 1)
                        war = bZ + NQG * (q - 2) + qg + 1
                        if war > 0:
                            v.wait_ge(s_peZ4.sem, war)
                        chain(v, s_dveQ, lambda qg=qg, e0=e0: v.tensor_add(
                            out=eq_sb[:, qg * 1024:qg * 1024 + 512],
                            in0=esb[:, e0 * 1024:e0 * 1024 + 512],
                            in1=esb[:, e0 * 1024 + 512:e0 * 1024 + 1024]))
                        v.wait_ge(s_actE.sem, bE + NQG * p1 + qg + 1)
                        chain(v, s_dveQ, lambda qg=qg, e1=e1: v.tensor_add(
                            out=eq_sb[:, qg * 1024 + 512:qg * 1024 + 1024],
                            in0=esb[:, e1 * 1024:e1 * 1024 + 512],
                            in1=esb[:, e1 * 1024 + 512:e1 * 1024 + 1024]))
                        chain(v, s_dveQ, lambda qg=qg, q=q: v.tensor_add(
                            out=e4_sb[:, qg * 1024 + (q % 2) * 512:
                                      qg * 1024 + (q % 2) * 512 + 512],
                            in0=eq_sb[:, qg * 1024:qg * 1024 + 512],
                            in1=eq_sb[:, qg * 1024 + 512:qg * 1024 + 1024]))
                for qg in QGS:
                    v.wait_ge(s_peZ4.sem, bZ + NQG * (NQUAD - 1) + qg + 1)
                    if s_dveZ.n:
                        v.wait_ge(s_dveZ.sem, s_dveZ.n)
                    s_dveZ.inc(v.reciprocal(
                        out=zinv[0:1, qg * 512:(qg + 1) * 512], in_=zzp[qg][0:1, :]))
                for qg in QGS:
                    v.wait_ge(s_peP.sem, NQG * NPAIR * h + NQG * (NPAIR - 1) + qg + 1)
                    chain(v, s_dveC, lambda h=h, qg=qg: v.tensor_copy(
                        out=ctx_sb[:, h * T + qg * 512:h * T + qg * 512 + 512],
                        in_=ctxp[qg][:]))
            h = HEADS
            for qg in QGS:
                v.wait_ge(s_peZB.sem, NQG * (h - 1) + qg + 1)
                chain(v, s_dveN, lambda qg=qg: v.tensor_copy(
                    out=zb_sb[:, qg * 512:(qg + 1) * 512], in_=zzp[qg][:]))
                chain(v, s_dveN, lambda qg=qg, h=h: v.tensor_tensor(
                    out=ctx_sb[:, (h - 1) * T + qg * 512:
                               (h - 1) * T + qg * 512 + 512],
                    in0=ctx_sb[:, (h - 1) * T + qg * 512:
                               (h - 1) * T + qg * 512 + 512],
                    in1=zb_sb[:, qg * 512:(qg + 1) * 512], op=ALU.mult))
        block.vector(p2_vector)

        p2a.close()
        wo_sb = p2.enter_context(nc.sbuf_tensor("wo_sb", [P, 2 * KH * 2 * 512], BF16))
        xr_sb = p2.enter_context(nc.sbuf_tensor("xr_sb", [P, 4 * 512], F32))
        x1s_sb = p2.enter_context(nc.sbuf_tensor("x1s_sb", [P, 4 * 512], F32))
        ops_extra = [p2.enter_context(nc.psum_tensor(f"ops_extra{i}", [P, 512], F32))
                     for i in range(4 * (2 - min(NQG, 2)))]
        if NQG >= 2:
            ops = [sps[0].ap()[:, 0:512], sps[0].ap()[:, 512:1024],
                   sps[1].ap()[:, 0:512], sps[1].ap()[:, 512:1024],
                   ctxp[0].ap()[:], ctxp[1].ap()[:], zzp[0].ap()[:], zzp[1].ap()[:]]
        else:
            ops = [sps[0].ap()[:, 0:512], sps[0].ap()[:, 512:1024],
                   ctxp[0].ap()[:], zzp[0].ap()[:]] + [o.ap()[:] for o in ops_extra]

        snap3 = [s.n for s in REG]

        def p3_gps(g):
            barrier(g, snap3)
            for og in range(VOG):
                for half in range(2):
                    if og >= 2:
                        g.wait_ge(s_peO.sem, QT * (og - 1))
                    kc0 = half * KH
                    base = (og % 2) * 2 * KH * 512
                    s_wo[(og % 2) * 2 + half].dma(g.dma_start(
                        out=wo_sb.ap()[:, base + half * KH * 512:base + (half + 1) * KH * 512]
                        .rearrange("p (kc c) -> p kc c", kc=KH),
                        in_=wo_d.ap()[kc0:kc0 + KH, :, og * 512:(og + 1) * 512]
                        .rearrange("kc p c -> p kc c")))
        block.gpsimd(p3_gps)

        def p3_sync(sy):
            barrier(sy, snap3)
            NI = VOG * QT

            def xr_load(i):
                og, qt = divmod(i, QT)
                s_xr[i % 4].dma(sy.dma_start(
                    out=xr_sb[:, (i % 4) * 512:(i % 4) * 512 + 512],
                    in_=x_d.ap()[qt * P:(qt + 1) * P, og * 512:(og + 1) * 512]))

            for i in range(min(4, NI)):
                xr_load(i)
            for i in range(NI):
                og, qt = divmod(i, QT)
                sy.wait_ge(s_dveX.sem, i + 1)
                s_stX1[i % 4].dma(sy.dma_start(
                    out=x1_d.ap()[qt * P:(qt + 1) * P, og * 512:(og + 1) * 512],
                    in_=x1s_sb[:, (i % 4) * 512:(i % 4) * 512 + 512]))
                if i + 4 < NI:
                    xr_load(i + 4)
        block.sync(p3_sync)

        def p3_tensor(t):
            barrier(t, snap3)
            for og in range(VOG):
                if og >= 1:
                    t.wait_ge(s_dveX.sem, QT * og)
                for half in range(2):
                    t.wait_ge(s_wo[(og % 2) * 2 + half].sem, 16 * (og // 2 + 1))
                    base = (og % 2) * 2 * KH * 512
                    for kcl in range(KH):
                        hc = half * KH + kcl
                        for qt in range(QT):
                            t.matmul(ops[qt],
                                     lhsT=ctx_sb[:, hc * T + qt * P:hc * T + (qt + 1) * P],
                                     rhs=wo_sb[:, base + (half * KH + kcl) * 512:
                                               base + (half * KH + kcl) * 512 + 512],
                                     start=(hc == 0), stop=False)
                for qt in range(QT):
                    mm = t.matmul(ops[qt], lhsT=ones_r[:],
                                  rhs=bo_sb[0:1, og * 512:(og + 1) * 512],
                                  start=False, stop=True)
                    s_peO.inc(mm)
        block.tensor(p3_tensor)

        def p3_vector(v):
            barrier(v, snap3)
            for og in range(VOG):
                for qt in range(QT):
                    i = og * QT + qt
                    v.wait_ge(s_peO.sem, i + 1)
                    v.wait_ge(s_xr[i % 4].sem, 16 * (i // 4 + 1))
                    if i >= 4:
                        v.wait_ge(s_stX1[i % 4].sem, 16 * (i // 4))
                    if s_dveX.n:
                        v.wait_ge(s_dveX.sem, s_dveX.n)
                    s_dveX.inc(v.tensor_tensor(
                        out=x1s_sb[:, (i % 4) * 512:(i % 4) * 512 + 512],
                        in0=ops[qt],
                        in1=xr_sb[:, (i % 4) * 512:(i % 4) * 512 + 512],
                        op=ALU.add))
        block.vector(p3_vector)

        p2.close()

        snap4 = [s.n for s in REG]
        p4 = ExitStack()
        ln2_fm = p4.enter_context(nc.sbuf_tensor("ln2_fm", [P, KC * T], BF16))
        x1_sb = p4.enter_context(nc.sbuf_tensor("x1_sb", [P, 2 * D], F32))
        ln2_tm = p4.enter_context(nc.sbuf_tensor("ln2_tm", [P, 2 * D], BF16))
        outs_sb = p4.enter_context(nc.sbuf_tensor("outs_sb", [P, 2 * D], F32))
        GA = QT * NEXP
        ga_sb = p4.enter_context(nc.sbuf_tensor("ga_sb", [P, GA], F32))
        gl_sb = p4.enter_context(nc.sbuf_tensor("gl_sb", [P, GA], F32))
        gx_sb = p4.enter_context(nc.sbuf_tensor("gx_sb", [P, GA], F32))
        gm_sb = p4.enter_context(nc.sbuf_tensor("gm_sb", [P, 4 * QT], F32))
        grep_sb = p4.enter_context(nc.sbuf_tensor("grep_sb", [P, 2 * GA], F32))
        gt_sb = p4.enter_context(nc.sbuf_tensor("gt_sb", [NEXP, QT * P], F32))
        mid_sb = p4.enter_context(nc.sbuf_tensor("mid_sb", [ER, QT * P], BF16))
        gbs_sb = p4.enter_context(nc.sbuf_tensor("gbs_sb", [ER, QT * P], F32))
        p4ps = ExitStack()
        tps2 = p4ps.enter_context(nc.psum_tensor("tps2", [P, 1024], BF16))
        ln2_fm_r = ln2_fm.ap().rearrange("p (kc t) -> p kc t", kc=KC)

        def p4a_sync(sy):
            barrier(sy, snap4)

            def x1_load(qt):
                s_x1[qt % 2].dma(sy.dma_start(
                    out=x1_sb[:, (qt % 2) * D:(qt % 2) * D + D],
                    in_=x1_d.ap()[qt * P:(qt + 1) * P, :]))

            for qt in range(min(2, QT)):
                x1_load(qt)
            for qt in range(2, QT):
                sy.wait_ge(s_dveL2.sem, 7 * (qt - 2) + 7)
                sy.wait_ge(s_actL2.sem, 3 * (qt - 2) + 1)
                x1_load(qt)
        block.sync(p4a_sync)

        def p4a_vec(v):
            barrier(v, snap4)
            for qt in range(QT):
                v.wait_ge(s_x1[qt % 2].sem, 16 * (qt // 2 + 1))
                emit_ln_vec(v, qt, x1_sb[:, (qt % 2) * D:(qt % 2) * D + D],
                            s_dveL2, s_actL2, s_peTR2, ln2_tm)
                for g in range(NG):
                    G = NG * qt + g
                    v.wait_ge(s_peTR2.sem, G + 1)
                    if s_dveTR2.n:
                        v.wait_ge(s_dveTR2.sem, s_dveTR2.n)
                    s_dveTR2.inc(v.tensor_copy(
                        out=ln2_fm_r[:, g * 8:(g + 1) * 8, qt * P:(qt + 1) * P],
                        in_=tps2.ap()[:, 0:1024]
                        .rearrange("p (kc t) -> p kc t", kc=8)))
        block.vector(p4a_vec)

        def p4a_scalar(sc):
            barrier(sc, snap4)
            for qt in range(QT):
                emit_ln_act(sc, qt, x1_sb[:, (qt % 2) * D:(qt % 2) * D + D],
                            s_dveL2, s_actL2,
                            lambda qt=qt: sc.wait_ge(s_x1[qt % 2].sem,
                                                     16 * (qt // 2 + 1)))
        block.scalar(p4a_scalar)

        def p4a_tensor(t):
            barrier(t, snap4)
            for qt in range(QT):
                t.wait_ge(s_dveL2.sem, 7 * qt + 7)
                for g in range(NG):
                    G = NG * qt + g
                    if G >= 1:
                        t.wait_ge(s_dveTR2.sem, G)
                    for j in range(8):
                        kc = g * 8 + j
                        mm = t.transpose(
                            tps2[:, j * P:(j + 1) * P],
                            in_=ln2_tm[:, (qt % 2) * D + kc * P:(qt % 2) * D + (kc + 1) * P],
                            identity=ident_bf[:])
                    s_peTR2.inc(mm)
        block.tensor(p4a_tensor)

        p4ps.close()
        lgps = p4.enter_context(nc.psum_tensor("lgps", [P, GA], F32))
        mid_all = p4.enter_context(nc.psum_tensor("mid_all", [ER, QT * P], F32))
        gtps = p4.enter_context(nc.psum_tensor("gtps", [NEXP, 2 * P], F32))
        gb_all = p4.enter_context(nc.psum_tensor("gb_all", [ER, QT * P], F32))
        dps = [p4.enter_context(nc.psum_tensor(f"dps{i}", [P, 512], F32))
               for i in range(2)]
        lg3 = lgps.ap().rearrange("p (qt e) -> p qt e", e=NEXP)

        def p4b_tensor(t):
            t.wait_ge(s_dveTR2.sem, NG * QT)
            for qt in range(QT):
                for kc in range(KC):
                    mm = t.matmul(lgps[:, qt * NEXP:(qt + 1) * NEXP],
                                  lhsT=ln2_fm[:, kc * T + qt * P:kc * T + (qt + 1) * P],
                                  rhs=wr_sb[:, kc * NEXP:(kc + 1) * NEXP],
                                  start=(kc == 0), stop=(kc == KC - 1))
            s_peLG.inc(mm)
            for qt in range(QT):
                for kc in range(KC):
                    mm = t.matmul(mid_all[:, qt * P:(qt + 1) * P],
                                  lhsT=wa_sb[:, kc * ER:(kc + 1) * ER],
                                  rhs=ln2_fm[:, kc * T + qt * P:kc * T + (qt + 1) * P],
                                  start=(kc == 0), stop=(kc == KC - 1))
            s_peMID.inc(mm)
            t.wait_ge(s_dveG.sem, 22)
            for qt in range(QT):
                if qt >= 1:
                    t.wait_ge(s_dveE2.sem, qt)
                s_peGT.inc(t.transpose(
                    gtps[:, 0:P],
                    in_=ga_sb[:, qt * NEXP:(qt + 1) * NEXP],
                    identity=ident_f[:]))
            for qt in range(QT):
                t.wait_ge(s_dveE2.sem, qt + 1)
                s_peGB.inc(t.matmul(
                    gb_all[:, qt * P:(qt + 1) * P], lhsT=ind_sb[:],
                    rhs=gt_sb[0:NEXP, qt * P:(qt + 1) * P],
                    start=True, stop=True))
            t.wait_ge(s_dveE2.sem, QT + 2)
            for qt in range(QT):
                for og in range(VOG):
                    i = VOG * qt + og
                    if i >= 2:
                        t.wait_ge(s_dveF.sem, i - 1)
                    s_peD.inc(t.matmul(dps[i % 2][:],
                                       lhsT=mid_sb[:, qt * P:(qt + 1) * P],
                                       rhs=wb_sb[:, og * 512:(og + 1) * 512],
                                       start=True, stop=True))
        block.tensor(p4b_tensor)

        def p4b_scalar(sc):
            sc.wait_ge(s_dveG.sem, 14)
            chain(sc, s_actG, lambda: sc.activation(out=gx_sb[:], in_=gl_sb[:],
                                                    func=AF.Exp))
        block.scalar(p4b_scalar)

        def p4b_vec(v):
            v.wait_ge(s_peLG.sem, 1)
            ch = lambda fn: chain(v, s_dveG, fn)
            ch(lambda: v.reduce_max(out=gm_sb[:, 0:QT], in_=lg3, axis=AX.X))
            for e in range(NEXP):
                ch(lambda e=e: v.tensor_copy(
                    out=bass.AP(grep_sb, e, [[2 * GA, P], [NEXP, QT]]),
                    in_=gm_sb[:, 0:QT]))
            ch(lambda: v.tensor_tensor(out=ga_sb[:], in0=lgps[:],
                                       in1=grep_sb[:, 0:GA], op=ALU.is_ge))
            ch(lambda: v.scalar_tensor_tensor(out=gl_sb[:], in0=ga_sb[:],
                                              scalar=-1e30, in1=lgps[:],
                                              op0=ALU.mult, op1=ALU.add))
            ch(lambda: v.reduce_max(out=gm_sb[:, QT:2 * QT],
                                    in_=gl_sb.ap().rearrange("p (qt e) -> p qt e", e=NEXP),
                                    axis=AX.X))
            for e in range(NEXP):
                ch(lambda e=e: v.tensor_copy(
                    out=bass.AP(grep_sb, GA + e, [[2 * GA, P], [NEXP, QT]]),
                    in_=gm_sb[:, QT:2 * QT]))
            ch(lambda: v.tensor_tensor(out=ga_sb[:], in0=lgps[:],
                                       in1=grep_sb[:, GA:2 * GA], op=ALU.is_ge))
            ch(lambda: v.tensor_tensor(out=gl_sb[:], in0=lgps[:],
                                       in1=grep_sb[:, 0:GA], op=ALU.subtract))
            assert s_dveG.n == 14, s_dveG.n
            v.wait_ge(s_actG.sem, 1)
            ch(lambda: v.tensor_tensor(out=gx_sb[:], in0=gx_sb[:], in1=ga_sb[:],
                                       op=ALU.mult))
            ch(lambda: v.reduce_sum(out=gm_sb[:, 2 * QT:3 * QT],
                                    in_=gx_sb.ap().rearrange("p (qt e) -> p qt e", e=NEXP),
                                    axis=AX.X))
            ch(lambda: v.reciprocal(out=gm_sb[:, 3 * QT:4 * QT],
                                    in_=gm_sb[:, 2 * QT:3 * QT]))
            for e in range(NEXP):
                ch(lambda e=e: v.tensor_copy(
                    out=bass.AP(grep_sb, e, [[2 * GA, P], [NEXP, QT]]),
                    in_=gm_sb[:, 3 * QT:4 * QT]))
            ch(lambda: v.tensor_tensor(out=ga_sb[:], in0=gx_sb[:],
                                       in1=grep_sb[:, 0:GA], op=ALU.mult))
            assert s_dveG.n == 22, s_dveG.n
            for qt in range(QT):
                v.wait_ge(s_peGT.sem, qt + 1)
                chain(v, s_dveE2, lambda qt=qt: v.tensor_copy(
                    out=gt_sb[:, qt * P:(qt + 1) * P],
                    in_=gtps[:, 0:P]))
            v.wait_ge(s_peGB.sem, QT)
            chain(v, s_dveE2, lambda: v.tensor_copy(out=gbs_sb[:], in_=gb_all[:]))
            v.wait_ge(s_peMID.sem, 1)
            chain(v, s_dveE2, lambda: v.tensor_tensor(
                out=mid_sb[:], in0=mid_all[:], in1=gbs_sb[:], op=ALU.mult))
            for qt in range(QT):
                for og in range(VOG):
                    i = VOG * qt + og
                    v.wait_ge(s_peD.sem, i + 1)
                    v.wait_ge(s_x1b[qt % 2].sem, 16 * (qt // 2 + 1))
                    if qt >= 2 and og == 0:
                        v.wait_ge(s_outF[qt % 2].sem, 16 * (qt // 2))
                    if s_dveF.n:
                        v.wait_ge(s_dveF.sem, s_dveF.n)
                    s_dveF.inc(v.tensor_tensor(
                        out=outs_sb[:, (qt % 2) * D + og * 512:(qt % 2) * D + (og + 1) * 512],
                        in0=dps[i % 2][:],
                        in1=x1_sb[:, (qt % 2) * D + og * 512:(qt % 2) * D + (og + 1) * 512],
                        op=ALU.add))
        block.vector(p4b_vec)

        def p4b_sync(sy):
            def x1b_load(qt):
                s_x1b[qt % 2].dma(sy.dma_start(
                    out=x1_sb[:, (qt % 2) * D:(qt % 2) * D + D],
                    in_=x1_d.ap()[qt * P:(qt + 1) * P, :]))

            for qt in range(min(2, QT)):
                sy.wait_ge(s_dveL2.sem, 7 * (QT - 2 + qt) + 7)
                sy.wait_ge(s_actL2.sem, 3 * (QT - 2 + qt) + 1)
                x1b_load(qt)
            for qt in range(QT):
                sy.wait_ge(s_dveF.sem, VOG * qt + VOG)
                s_outF[qt % 2].dma(sy.dma_start(
                    out=out_d.ap()[qt * P:(qt + 1) * P, :],
                    in_=outs_sb[:, (qt % 2) * D:(qt % 2) * D + D]))
                if qt + 2 < QT:
                    x1b_load(qt + 2)
            for s in s_outF:
                if s.n:
                    sy.wait_ge(s.sem, s.n)
        block.sync(p4b_sync)

        def p4_gps_end(g):
            for s in s_outF:
                if s.n:
                    g.wait_ge(s.sem, s.n)
        block.gpsimd(p4_gps_end)

        p4.close()

    return nc, dict(D=D, S=S, T=T)


def _host_prep(D, S, x, qkvw, qkvb, ow, ob, A, B, Wr):
    KC = D // P
    OCH = 2 * D // P
    T = S // NCORES
    x2 = np.ascontiguousarray(np.asarray(x, dtype=np.float32).reshape(S, D))
    qkvw = np.asarray(qkvw, dtype=np.float32)
    w_qk = np.ascontiguousarray(
        qkvw[:, :2 * D].reshape(KC, P, OCH, P).transpose(2, 1, 0, 3).reshape(OCH, P, D))
    w_v = np.ascontiguousarray(qkvw[:, 2 * D:].reshape(KC, P, D))
    qkvb = np.asarray(qkvb, dtype=np.float32)
    b_qk = np.ascontiguousarray(qkvb[:2 * D].reshape(OCH, P).T)
    b_v = np.ascontiguousarray(qkvb[2 * D:].reshape(1, D))
    w_o = np.ascontiguousarray(np.asarray(ow, dtype=np.float32).reshape(KC, P, D))
    b_o = np.ascontiguousarray(np.asarray(ob, dtype=np.float32).reshape(1, D))
    A = np.asarray(A, dtype=np.float32)
    w_a = np.ascontiguousarray(A.transpose(2, 0, 1).reshape(KC, P, ER))
    B = np.asarray(B, dtype=np.float32)
    w_b = np.ascontiguousarray(B.transpose(0, 2, 1).reshape(ER, D))
    Wr = np.asarray(Wr, dtype=np.float32)
    w_r = np.ascontiguousarray(Wr.T.reshape(KC, P, NEXP))
    ident = np.eye(P, dtype=np.float32)
    ind4 = np.zeros((NEXP, ER), dtype=np.float32)
    for e in range(NEXP):
        ind4[e, e * LORA_R:(e + 1) * LORA_R] = LORA_SCALE
    shared = dict(w_qk=w_qk, w_v=w_v, w_o=w_o, b_qk=b_qk, b_v=b_v, b_o=b_o,
                  w_a=w_a, w_b=w_b, w_r=w_r, ident=ident, ind4=ind4)
    in_maps = []
    for c in range(NCORES):
        m = dict(shared)
        m["x"] = np.ascontiguousarray(x2[c * T:(c + 1) * T, :])
        in_maps.append(m)
    return in_maps


def kernel(L0_attn_x, L0_attn_attn_qkvw, L0_attn_attn_qkvb, L0_attn_attn_ow,
           L0_attn_attn_ob, L0_ffn_A, L0_ffn_B, L0_ffn_Wr):
    D = 4096
    S = 8192
    in_maps = _host_prep(D, S, L0_attn_x, L0_attn_attn_qkvw, L0_attn_attn_qkvb,
                         L0_attn_attn_ow, L0_attn_attn_ob, L0_ffn_A, L0_ffn_B,
                         L0_ffn_Wr)
    nc, meta = build_program(D, S)
    res = run_bass_kernel_spmd(nc, in_maps, core_ids=list(range(NCORES)))
    out = np.concatenate([np.asarray(r["out"], dtype=np.float32)
                          for r in res.results], axis=0)
    return out.reshape(1, S, D)


# version 11
# speedup vs baseline: 1.3621x; 1.3621x over previous
import math
from contextlib import ExitStack

import numpy as np
import concourse.bass as bass
import concourse.mybir as mybir
from concourse.bass_utils import run_bass_kernel_spmd

F32 = mybir.dt.float32
BF16 = mybir.dt.bfloat16
AF = mybir.ActivationFunctionType
ALU = mybir.AluOpType
AX = mybir.AxisListType

NCORES = 8
LORA_SCALE = 4.0
NEXP = 4
LORA_R = 16
ER = NEXP * LORA_R
EPS = 1e-5
P = 128


class Sem:

    def __init__(self, ctx, nc, name, registry):
        self.sem = ctx.enter_context(nc.semaphore(name))
        self.n = 0
        registry.append(self)

    def inc(self, instr, k=1):
        instr.then_inc(self.sem, k)
        self.n += k
        return self.n

    def dma(self, instr):
        return self.inc(instr, 16)

    def wait(self, eng, val=None):
        eng.wait_ge(self.sem, self.n if val is None else val)


def build_program(D, S):
    T = S // NCORES
    KC = D // P
    HEADS = D // P
    OCH = 2 * D // P
    VOG = D // 512
    QT = T // P
    NQG = T // 512
    KT = S // P
    NPAIR = KT // 2
    NG = KC // 8
    NQTG = QT // 4
    HPV = 4
    assert T % 512 == 0 and D % 1024 == 0 and QT % 4 == 0

    nc = bass.Bass("TRN2", target_bir_lowering=False)

    x_d = nc.dram_tensor("x", [T, D], F32, kind="ExternalInput")
    wqk_d = nc.dram_tensor("w_qk", [OCH, P, D], F32, kind="ExternalInput")
    wv_d = nc.dram_tensor("w_v", [KC, P, D], F32, kind="ExternalInput")
    wo_d = nc.dram_tensor("w_o", [KC, P, D], F32, kind="ExternalInput")
    bqk_d = nc.dram_tensor("b_qk", [P, OCH], F32, kind="ExternalInput")
    bv_d = nc.dram_tensor("b_v", [1, D], F32, kind="ExternalInput")
    bo_d = nc.dram_tensor("b_o", [1, D], F32, kind="ExternalInput")
    wa_d = nc.dram_tensor("w_a", [KC, P, ER], F32, kind="ExternalInput")
    wb_d = nc.dram_tensor("w_b", [ER, D], F32, kind="ExternalInput")
    wr_d = nc.dram_tensor("w_r", [KC, P, NEXP], F32, kind="ExternalInput")
    id_d = nc.dram_tensor("ident", [P, P], F32, kind="ExternalInput")
    ind_d = nc.dram_tensor("ind4", [NEXP, ER], F32, kind="ExternalInput")

    out_d = nc.dram_tensor("out", [T, D], F32, kind="ExternalOutput")

    cc_in = nc.dram_tensor("cc_in", [2 * D, T], BF16, kind="Internal")
    cck_out = nc.dram_tensor("cck_out", [NCORES * D, T], BF16,
                             kind="Internal", addr_space="Shared")
    ccv_out = nc.dram_tensor("ccv_out", [NCORES * D, T], BF16,
                             kind="Internal", addr_space="Shared")
    qs_d = nc.dram_tensor("qs", [D, T], BF16, kind="Internal")
    x1_d = nc.dram_tensor("x1", [T, D], F32, kind="Internal")

    gctx = ExitStack()
    REG = []

    def sb(name, shape, dtype):
        return gctx.enter_context(nc.sbuf_tensor(name, shape, dtype))

    ident_bf = sb("ident_bf", [P, P], BF16)
    ident_f = sb("ident_f", [P, P], F32)
    bqk_sb = sb("bqk_sb", [P, OCH], F32)
    bv_sb = sb("bv_sb", [1, D], BF16)
    bo_sb = sb("bo_sb", [1, D], BF16)
    wa_sb = sb("wa_sb", [P, KC * ER], BF16)
    wb_sb = sb("wb_sb", [ER, D], BF16)
    wr_sb = sb("wr_sb", [P, KC * NEXP], BF16)
    ind_sb = sb("ind_sb", [NEXP, ER], F32)
    ones_c = sb("ones_c", [P, 1], BF16)
    ones_r = sb("ones_r", [1, P], BF16)
    ones_rf = sb("ones_rf", [1, P], F32)
    dump = sb("dump", [P, D], BF16)
    st = sb("st", [P, 32], F32)

    def mks(name):
        return Sem(gctx, nc, name, REG)

    def mkslots(name, k):
        return [mks(f"{name}{j}") for j in range(k)]

    s_small = mks("s_small")
    s_x = mkslots("s_x", 2)
    s_wq = mkslots("s_wq", 2)
    s_wv = mkslots("s_wv", 2)
    s_wo = mkslots("s_wo", 4)
    s_kv = mkslots("s_kv", 2)
    s_x1 = mkslots("s_x1", 2)
    s_x1b = mkslots("s_x1b", 2)
    s_xr = mkslots("s_xr", 4)
    s_cc = mks("s_cc")
    s_dveM = mks("s_dveM")
    s_dveLN = mks("s_dveLN")
    s_actLN = mks("s_actLN")
    s_peTR = mks("s_peTR")
    s_dveTR = mks("s_dveTR")
    s_peQK = mks("s_peQK")
    s_actQK = mks("s_actQK")
    s_stQK = mkslots("s_stQK", 4)
    s_peV = mks("s_peV")
    s_peV2 = mks("s_peV2")
    s_dveV = mks("s_dveV")
    s_stV = mkslots("s_stV", 2)
    s_peS = mks("s_peS")
    s_peP = mks("s_peP")
    s_peZ4 = mks("s_peZ4")
    s_dveQ = mks("s_dveQ")
    s_dveN = mks("s_dveN")
    s_actE = mks("s_actE")
    s_peH = mks("s_peH")
    s_dveZ = mks("s_dveZ")
    s_peZB = mks("s_peZB")
    s_dveC = mks("s_dveC")
    s_peO = mks("s_peO")
    s_dveX = mks("s_dveX")
    s_stX1 = mkslots("s_stX1", 4)
    s_dveL2 = mks("s_dveL2")
    s_actL2 = mks("s_actL2")
    s_peTR2 = mks("s_peTR2")
    s_dveTR2 = mks("s_dveTR2")
    s_peLG = mks("s_peLG")
    s_peMID = mks("s_peMID")
    s_peGT = mks("s_peGT")
    s_peGB = mks("s_peGB")
    s_peD = mks("s_peD")
    s_dveG = mks("s_dveG")
    s_actG = mks("s_actG")
    s_dveE2 = mks("s_dveE2")
    s_dveF = mks("s_dveF")
    s_outF = mkslots("s_outF", 2)

    block = gctx.enter_context(nc.Block())

    def barrier(eng, snap=None):
        vals = snap if snap is not None else [s.n for s in REG]
        for s, v_ in zip(REG, vals):
            if v_:
                eng.wait_ge(s.sem, v_)

    SC = float(1.0 / math.sqrt(P))

    def chain(eng, sem, instr_fn):
        if sem.n:
            eng.wait_ge(sem.sem, sem.n)
        return sem.inc(instr_fn())

    with gctx:
        def p0_gpsimd(g):
            s_small.dma(g.dma_start(out=ident_bf[:], in_=id_d.ap()))
            s_small.dma(g.dma_start(out=ident_f[:], in_=id_d.ap()))
            s_small.dma(g.dma_start(out=bqk_sb[:], in_=bqk_d.ap()))
            s_small.dma(g.dma_start(out=bv_sb[:], in_=bv_d.ap()))
            s_small.dma(g.dma_start(out=bo_sb[:], in_=bo_d.ap()))
            s_small.dma(g.dma_start(
                out=wa_sb.ap().rearrange("p (kc e) -> p kc e", kc=KC),
                in_=wa_d.ap().rearrange("kc p e -> p kc e")))
            s_small.dma(g.dma_start(out=wb_sb[:], in_=wb_d.ap()))
            s_small.dma(g.dma_start(
                out=wr_sb.ap().rearrange("p (kc e) -> p kc e", kc=KC),
                in_=wr_d.ap().rearrange("kc p e -> p kc e")))
            s_small.dma(g.dma_start(out=ind_sb[:], in_=ind_d.ap()))
        block.gpsimd(p0_gpsimd)

        def p0_vector(v):
            v.memset(ones_c[:], 1.0)
            v.memset(ones_r[:], 1.0)
            s_dveM.inc(v.memset(ones_rf[:], 1.0))
        block.vector(p0_vector)

        p1 = ExitStack()
        x_sb = p1.enter_context(nc.sbuf_tensor("x_sb", [P, 2 * D], F32))
        ln_tm = p1.enter_context(nc.sbuf_tensor("ln_tm", [P, D], BF16))
        ln_fm = p1.enter_context(nc.sbuf_tensor("ln_fm", [P, KC * T], BF16))
        wqk_sb = p1.enter_context(nc.sbuf_tensor("wqk_sb", [P, 2 * D], BF16))
        stage_qk = p1.enter_context(nc.sbuf_tensor("stage_qk", [P, 4 * 1024], BF16))
        wv_sb = p1.enter_context(nc.sbuf_tensor("wv_sb", [P, KC // 2 * 2 * 512], BF16))
        stage_v = p1.enter_context(nc.sbuf_tensor("stage_v", [P, 2 * 4 * 512], BF16))
        qkps = [p1.enter_context(nc.psum_tensor(f"qkps{i}", [P, 1024], F32))
                for i in range(2)]
        ps_ln = ExitStack()
        tps_ln = ps_ln.enter_context(nc.psum_tensor("tps_ln", [P, 2 * 1024], BF16))

        ln_fm_r = ln_fm.ap().rearrange("p (kc t) -> p kc t", kc=KC)

        def p1_sync(sy):
            for qt in range(QT):
                if qt >= 2:
                    sy.wait_ge(s_dveLN.sem, 7 * (qt - 2) + 7)
                    sy.wait_ge(s_actLN.sem, 3 * (qt - 2) + 1)
                s_x[qt % 2].dma(sy.dma_start(
                    out=x_sb[:, (qt % 2) * D:(qt % 2) * D + D],
                    in_=x_d.ap()[qt * P:(qt + 1) * P, :]))
        block.sync(p1_sync)

        def emit_ln_vec(v, qt, xap, sem_d, sem_a, s_petr, ln_tm_t, war_lag=2):
            b7 = 7 * qt
            chain(v, sem_d, lambda: v.reduce_sum(out=st[:, 0:1], in_=xap, axis=AX.X))
            chain(v, sem_d, lambda: v.tensor_scalar_mul(st[:, 1:2], st[:, 0:1], 1.0 / D))
            chain(v, sem_d, lambda: v.tensor_tensor(out=st[:, 2:3], in0=st[:, 1:2],
                                                    in1=st[:, 1:2], op=ALU.mult))
            v.wait_ge(sem_a.sem, 3 * qt + 1)
            chain(v, sem_d, lambda: v.tensor_scalar(out=st[:, 3:4], in0=st[:, 4:5],
                                                    scalar1=1.0 / D, scalar2=float(EPS),
                                                    op0=ALU.mult, op1=ALU.add))
            chain(v, sem_d, lambda: v.tensor_tensor(out=st[:, 5:6], in0=st[:, 3:4],
                                                    in1=st[:, 2:3], op=ALU.subtract))
            v.wait_ge(sem_a.sem, 3 * qt + 3)
            chain(v, sem_d, lambda: v.tensor_scalar(out=st[:, 7:8], in0=st[:, 1:2],
                                                    scalar1=st[:, 6:7], scalar2=-1.0,
                                                    op0=ALU.mult, op1=ALU.mult))
            if qt >= war_lag:
                v.wait_ge(s_petr.sem, NG * (qt - war_lag) + NG)
            nb = (qt % 2) * D if war_lag == 2 else 0
            chain(v, sem_d, lambda: v.tensor_scalar(
                out=ln_tm_t[:, nb:nb + D], in0=xap,
                scalar1=st[:, 6:7], scalar2=st[:, 7:8],
                op0=ALU.mult, op1=ALU.add))
            assert sem_d.n == b7 + 7

        def emit_ln_act(sc, qt, xap, sem_d, sem_a, wait_x):
            wait_x()
            if qt >= 1:
                sc.wait_ge(sem_d.sem, 7 * (qt - 1) + 4)
            chain(sc, sem_a, lambda: sc.activation(out=dump[:], in_=xap,
                                                   func=AF.Square,
                                                   accum_out=st[:, 4:5]))
            sc.wait_ge(sem_d.sem, 7 * qt + 5)
            chain(sc, sem_a, lambda: sc.activation(out=st[:, 8:9], in_=st[:, 5:6],
                                                   func=AF.Ln))
            chain(sc, sem_a, lambda: sc.activation(out=st[:, 6:7], in_=st[:, 8:9],
                                                   func=AF.Exp, scale=-0.5))
            assert sem_a.n == 3 * qt + 3

        def p1_vec_ln(v):
            for qt in range(QT):
                v.wait_ge(s_x[qt % 2].sem, 16 * (qt // 2 + 1))
                emit_ln_vec(v, qt, x_sb[:, (qt % 2) * D:(qt % 2) * D + D],
                            s_dveLN, s_actLN, s_peTR, ln_tm, war_lag=1)
                for g in range(NG):
                    G = NG * qt + g
                    v.wait_ge(s_peTR.sem, G + 1)
                    if s_dveTR.n:
                        v.wait_ge(s_dveTR.sem, s_dveTR.n)
                    s_dveTR.inc(v.tensor_copy(
                        out=ln_fm_r[:, g * 8:(g + 1) * 8, qt * P:(qt + 1) * P],
                        in_=tps_ln.ap()[:, (G % 2) * 1024:(G % 2) * 1024 + 1024]
                        .rearrange("p (kc t) -> p kc t", kc=8)))
        block.vector(p1_vec_ln)

        def p1_scalar_ln(sc):
            for qt in range(QT):
                emit_ln_act(sc, qt, x_sb[:, (qt % 2) * D:(qt % 2) * D + D],
                            s_dveLN, s_actLN,
                            lambda qt=qt: sc.wait_ge(s_x[qt % 2].sem,
                                                     16 * (qt // 2 + 1)))
        block.scalar(p1_scalar_ln)

        def p1_pe_tr(t):
            s_small.wait(t)
            t.wait_ge(s_dveM.sem, 1)
            for qt in range(QT):
                t.wait_ge(s_dveLN.sem, 7 * qt + 7)
                for g in range(NG):
                    G = NG * qt + g
                    if G >= 2:
                        t.wait_ge(s_dveTR.sem, G - 1)
                    for j in range(8):
                        kc = g * 8 + j
                        mm = t.transpose(
                            tps_ln[:, (G % 2) * 1024 + j * P:(G % 2) * 1024 + (j + 1) * P],
                            in_=ln_tm[:, kc * P:(kc + 1) * P],
                            identity=ident_bf[:])
                    s_peTR.inc(mm)
        block.tensor(p1_pe_tr)

        order = list(range(HEADS, OCH)) + list(range(0, HEADS))
        NKO = HEADS

        def emit_wqk_loads(g, i0, i1):
            for i in range(i0, i1):
                if i >= 2:
                    g.wait_ge(s_peQK.sem, i - 1)
                s_wq[i % 2].dma(g.dma_start(
                    out=wqk_sb[:, (i % 2) * D:(i % 2) * D + D],
                    in_=wqk_d.ap()[order[i]]))

        def emit_qk_compute(t, i0, i1):
            for i in range(i0, i1):
                t.wait_ge(s_wq[i % 2].sem, 16 * (i // 2 + 1))
                if i == 0:
                    t.wait_ge(s_dveTR.sem, NG * QT)
                if i >= 2:
                    t.wait_ge(s_actQK.sem, i - 1)
                for kc in range(KC):
                    for qg in range(NQG):
                        mm = t.matmul(
                            qkps[i % 2][:, qg * 512:(qg + 1) * 512],
                            lhsT=wqk_sb[:, (i % 2) * D + kc * P:(i % 2) * D + (kc + 1) * P],
                            rhs=ln_fm[:, kc * T + qg * 512:kc * T + qg * 512 + 512],
                            start=(kc == 0), stop=(kc == KC - 1))
                s_peQK.inc(mm)

        def emit_qk_evac(sc, i0, i1):
            for i in range(i0, i1):
                if i == 0:
                    s_small.wait(sc)
                sc.wait_ge(s_peQK.sem, i + 1)
                if i >= 4:
                    sc.wait_ge(s_stQK[i % 4].sem, 16 * (i // 4))
                och = order[i]
                chain(sc, s_actQK, lambda i=i, och=och: sc.activation(
                    out=stage_qk[:, (i % 4) * 1024:(i % 4) * 1024 + T],
                    in_=qkps[i % 2][:, 0:T], func=AF.Identity,
                    bias=bqk_sb[:, och:och + 1]))

        def emit_qk_store(sy, i0, i1):
            for i in range(i0, i1):
                sy.wait_ge(s_actQK.sem, i + 1)
                och = order[i]
                if och >= HEADS:
                    hk = och - HEADS
                    dst = cc_in.ap()[hk * P:(hk + 1) * P, :]
                else:
                    dst = qs_d.ap()[och * P:(och + 1) * P, :]
                s_stQK[i % 4].dma(sy.dma_start(
                    out=dst, in_=stage_qk[:, (i % 4) * 1024:(i % 4) * 1024 + T]))

        block.gpsimd(lambda g: emit_wqk_loads(g, 0, NKO))

        def p1_gps_cck(g):
            for s in s_stQK:
                if s.n:
                    g.wait_ge(s.sem, s.n)
            s_cc.inc(g.collective_compute(
                "AllGather", ALU.bypass,
                replica_groups=[list(range(NCORES))],
                ins=[cc_in.ap()[0:D, :]], outs=[cck_out.ap()]))

        block.tensor(lambda t: emit_qk_compute(t, 0, NKO))
        block.scalar(lambda sc: emit_qk_evac(sc, 0, NKO))
        block.sync(lambda sy: emit_qk_store(sy, 0, NKO))

        ps_ln.close()
        vps = [p1.enter_context(nc.psum_tensor(f"vps{i}", [P, 512], F32))
               for i in range(4)]
        KH = KC // 2

        def p1_gps_wv(g, v0, v1):
            for vog in range(v0, v1):
                for half in range(2):
                    if vog >= 1:
                        g.wait_ge(s_peV.sem,
                                  2 * NQTG * (vog - 1) + 2 * (NQTG - 1) + half + 1)
                    kc0 = half * KH
                    s_wv[half].dma(g.dma_start(
                        out=wv_sb.ap()[:, half * KH * 512:(half + 1) * KH * 512]
                        .rearrange("p (kc c) -> p kc c", kc=KH),
                        in_=wv_d.ap()[kc0:kc0 + KH, :, vog * 512:(vog + 1) * 512]
                        .rearrange("kc p c -> p kc c")))
        block.gpsimd(lambda g: p1_gps_wv(g, 0, 1))
        block.gpsimd(p1_gps_cck)
        block.gpsimd(lambda g: p1_gps_wv(g, 1, VOG))

        def p1_pe_v(t):
            for vog in range(VOG):
                for qtg in range(NQTG):
                    gi = NQTG * vog + qtg
                    if gi >= 1:
                        t.wait_ge(s_dveV.sem, 4 * gi)
                    for half in range(2):
                        t.wait_ge(s_wv[half].sem, 16 * (vog + 1))
                        for kcl in range(KH):
                            kc = half * KH + kcl
                            for q4 in range(4):
                                qt = qtg * 4 + q4
                                mm = t.matmul(
                                    vps[q4][:],
                                    lhsT=ln_fm[:, kc * T + qt * P:kc * T + (qt + 1) * P],
                                    rhs=wv_sb[:, (half * KH + kcl) * 512:
                                              (half * KH + kcl) * 512 + 512],
                                    start=(kc == 0), stop=False)
                        s_peV.inc(mm)
                    for q4 in range(4):
                        mm = t.matmul(vps[q4][:], lhsT=ones_r[:],
                                      rhs=bv_sb[0:1, vog * 512:(vog + 1) * 512],
                                      start=False, stop=True)
                        s_peV2.inc(mm)
        block.tensor(p1_pe_v)

        def p1_vec_v(v):
            for vog in range(VOG):
                for qtg in range(NQTG):
                    gi = NQTG * vog + qtg
                    for q4 in range(4):
                        v.wait_ge(s_peV2.sem, 4 * gi + q4 + 1)
                        if gi >= 2:
                            v.wait_ge(s_stV[gi % 2].sem, 16 * HPV * (gi // 2))
                        if s_dveV.n:
                            v.wait_ge(s_dveV.sem, s_dveV.n)
                        s_dveV.inc(v.tensor_copy(
                            out=stage_v[:, (gi % 2) * 2048 + q4 * 512:
                                        (gi % 2) * 2048 + (q4 + 1) * 512],
                            in_=vps[q4][:]))
        block.vector(p1_vec_v)

        def p1_sync_v(sy):
            for vog in range(VOG):
                for qtg in range(NQTG):
                    gi = NQTG * vog + qtg
                    sy.wait_ge(s_dveV.sem, 4 * gi + 4)
                    for j in range(HPV):
                        hv = vog * HPV + j
                        dst = cc_in.ap()[D + hv * P:D + (hv + 1) * P,
                                         qtg * 4 * P:(qtg + 1) * 4 * P] \
                            .rearrange("p (qt c) -> p qt c", qt=4)
                        src = stage_v.ap()[:, (gi % 2) * 2048:(gi % 2) * 2048 + 2048] \
                            .rearrange("p (qt c) -> p qt c", qt=4)[:, :, j * P:(j + 1) * P]
                        s_stV[gi % 2].dma(sy.dma_start(out=dst, in_=src))
        block.sync(p1_sync_v)

        def p1_gps_ccv(g):
            for s in s_stV:
                if s.n:
                    g.wait_ge(s.sem, s.n)
            s_cc.inc(g.collective_compute(
                "AllGather", ALU.bypass,
                replica_groups=[list(range(NCORES))],
                ins=[cc_in.ap()[D:2 * D, :]], outs=[ccv_out.ap()]))
        block.gpsimd(lambda g: emit_wqk_loads(g, NKO, NKO + 3))
        block.gpsimd(p1_gps_ccv)

        block.gpsimd(lambda g: emit_wqk_loads(g, NKO + 3, OCH))
        block.tensor(lambda t: emit_qk_compute(t, NKO, OCH))
        block.scalar(lambda sc: emit_qk_evac(sc, NKO, OCH))
        block.sync(lambda sy: emit_qk_store(sy, NKO, OCH))

        p1.close()

        p2 = ExitStack()
        ctx_sb = p2.enter_context(nc.sbuf_tensor("ctx_sb", [P, HEADS * T], BF16))
        sps = [p2.enter_context(nc.psum_tensor(f"sps{i}", [P, 1024], F32))
               for i in range(NQG)]
        ctxp = [p2.enter_context(nc.psum_tensor(f"ctxp{i}", [P, 512], F32))
                for i in range(NQG)]
        zzp = [p2.enter_context(nc.psum_tensor(f"zzp{i}", [P, 512], F32))
               for i in range(NQG)]
        p2a = ExitStack()
        ksb = p2a.enter_context(nc.sbuf_tensor("ksb", [P, 2 * S], BF16))
        vsb = p2a.enter_context(nc.sbuf_tensor("vsb", [P, 2 * S], BF16))
        qsb = p2a.enter_context(nc.sbuf_tensor("qsb", [P, 2 * T], BF16))
        esb = p2a.enter_context(nc.sbuf_tensor("esb", [P, 8 * 1024], BF16))
        eq_sb = p2a.enter_context(nc.sbuf_tensor("eq_sb", [P, NQG * 1024], BF16))
        e4_sb = p2a.enter_context(nc.sbuf_tensor("e4_sb", [P, NQG * 1024], BF16))
        zinv = p2a.enter_context(nc.sbuf_tensor("zinv", [1, NQG * 512], F32))
        zb_sb = p2a.enter_context(nc.sbuf_tensor("zb_sb", [P, NQG * 512], F32))

        NL = 1 + 2 * NCORES
        snap2 = [s.n for s in REG]

        def p2_sync(sy):
            barrier(sy, snap2)
            sy.wait_ge(s_cc.sem, 2)
            for h in range(HEADS):
                if h >= 2:
                    sy.wait_ge(s_peH.sem, h - 1)
                hb = h % 2
                s_kv[hb].dma(sy.dma_start(out=qsb[:, hb * T:hb * T + T],
                                          in_=qs_d.ap()[h * P:(h + 1) * P, :]))
                for r in range(NCORES):
                    s_kv[hb].dma(sy.dma_start(
                        out=ksb[:, hb * S + r * T:hb * S + (r + 1) * T],
                        in_=cck_out.ap()[r * D + h * P:r * D + (h + 1) * P, :]))
                for r in range(NCORES):
                    s_kv[hb].dma(sy.dma_start(
                        out=vsb.ap()[:, hb * S + r * T:hb * S + (r + 1) * T]
                        .rearrange("p (qt c) -> p qt c", qt=QT),
                        in_=ccv_out.ap()[r * D + h * P:r * D + (h + 1) * P, :]
                        .rearrange("p (qt c) -> p qt c", qt=QT)))
        block.sync(p2_sync)

        NQUAD = KT // 4
        QGS = list(range(NQG))

        def p2_tensor(t):
            barrier(t, snap2)
            for h in range(HEADS):
                hb = h % 2
                t.wait_ge(s_kv[h % 2].sem, 16 * NL * (h // 2 + 1))
                bE = NQG * NPAIR * h
                bP = NQG * NPAIR * h
                bZ = NQG * NQUAD * h
                if h >= 1:
                    for qg in QGS:
                        t.wait_ge(s_dveZ.sem, NQG * (h - 1) + qg + 1)
                        s_peZB.inc(t.matmul(
                            zzp[qg][:], lhsT=ones_rf[:],
                            rhs=zinv[0:1, qg * 512:(qg + 1) * 512],
                            start=True, stop=True))
                    t.wait_ge(s_actE.sem, bE)
                for p in range(NPAIR):
                    for qg in QGS:
                        if p >= 1:
                            t.wait_ge(s_actE.sem, bE + NQG * (p - 1) + qg + 1)
                        for j in range(2):
                            kt = 2 * p + j
                            mm = t.matmul(
                                sps[qg][:, j * 512:(j + 1) * 512],
                                lhsT=ksb[:, hb * S + kt * P:hb * S + (kt + 1) * P],
                                rhs=qsb[:, hb * T + qg * 512:hb * T + qg * 512 + 512],
                                start=True, stop=True)
                        s_peS.inc(mm)
                    if p >= 1:
                        pv = p - 1
                        for qg in QGS:
                            if pv == 0:
                                if h >= 1:
                                    t.wait_ge(s_dveC.sem, NQG * (h - 1) + qg + 1)
                            sl = (NQG * pv + qg) % 8
                            for j in range(2):
                                kt = 2 * pv + j
                                mm = t.matmul(
                                    ctxp[qg][:],
                                    lhsT=vsb[:, hb * S + kt * P:hb * S + (kt + 1) * P],
                                    rhs=esb[:, sl * 1024 + j * 512:sl * 1024 + (j + 1) * 512],
                                    start=(kt == 0), stop=False)
                            s_peP.inc(mm)
                            if pv % 2 == 1:
                                q = pv // 2
                                t.wait_ge(s_dveQ.sem, 3 * NQG * NQUAD * h + 3 * NQG * q + 3 * qg + 3)
                                if q == 0 and h >= 1:
                                    t.wait_ge(s_dveN.sem, 2 * NQG * (h - 1) + 2 * qg + 1)
                                s_peZ4.inc(t.matmul(
                                    zzp[qg][0:1, :], lhsT=ones_c[:],
                                    rhs=e4_sb[:, qg * 1024 + (q % 2) * 512:
                                              qg * 1024 + (q % 2) * 512 + 512],
                                    start=(q == 0), stop=(q == NQUAD - 1)))
                pv = NPAIR - 1
                for qg in QGS:
                    t.wait_ge(s_actE.sem, bE + NQG * pv + qg + 1)
                    sl = (NQG * pv + qg) % 8
                    for j in range(2):
                        kt = 2 * pv + j
                        mm = t.matmul(
                            ctxp[qg][:],
                            lhsT=vsb[:, hb * S + kt * P:hb * S + (kt + 1) * P],
                            rhs=esb[:, sl * 1024 + j * 512:sl * 1024 + (j + 1) * 512],
                            start=False, stop=(j == 1))
                    s_peP.inc(mm)
                    q = pv // 2
                    t.wait_ge(s_dveQ.sem, 3 * NQG * NQUAD * h + 3 * NQG * q + 3 * qg + 3)
                    s_peZ4.inc(t.matmul(
                        zzp[qg][0:1, :], lhsT=ones_c[:],
                        rhs=e4_sb[:, qg * 1024 + (q % 2) * 512:
                                  qg * 1024 + (q % 2) * 512 + 512],
                        start=False, stop=True))
                t.wait_ge(s_peP.sem, bP + NQG * NPAIR)
                t.sem_inc(s_peH.sem, 1)
                s_peH.n += 1
            for qg in QGS:
                t.wait_ge(s_dveZ.sem, NQG * (HEADS - 1) + qg + 1)
                s_peZB.inc(t.matmul(zzp[qg][:], lhsT=ones_rf[:],
                                    rhs=zinv[0:1, qg * 512:(qg + 1) * 512],
                                    start=True, stop=True))
        block.tensor(p2_tensor)

        def p2_scalar(sc):
            barrier(sc, snap2)
            for h in range(HEADS):
                bS = NQG * NPAIR * h
                bP = NQG * NPAIR * h
                bQ = 3 * NQG * NQUAD * h
                for p in range(NPAIR):
                    for qg in QGS:
                        e = NQG * p + qg
                        sc.wait_ge(s_peS.sem, bS + e + 1)
                        if p >= 8 // NQG:
                            pp = p - 8 // NQG
                            sc.wait_ge(s_peP.sem, bP + NQG * pp + qg + 1)
                            sc.wait_ge(s_dveQ.sem,
                                       bQ + 3 * NQG * (pp // 2) + 3 * qg + (pp % 2) + 1)
                        elif h >= 1:
                            sc.wait_ge(s_peP.sem, bP)
                            sc.wait_ge(s_dveQ.sem, bQ)
                        chain(sc, s_actE, lambda qg=qg, e=e: sc.activation(
                            out=esb[:, (e % 8) * 1024:(e % 8) * 1024 + 1024],
                            in_=sps[qg][:], func=AF.Exp, scale=SC))
        block.scalar(p2_scalar)

        def p2_vector(v):
            barrier(v, snap2)
            for h in range(HEADS):
                bE = NQG * NPAIR * h
                bZ = NQG * NQUAD * h
                if h >= 1:
                    for qg in QGS:
                        v.wait_ge(s_peZB.sem, NQG * (h - 1) + qg + 1)
                        chain(v, s_dveN, lambda qg=qg: v.tensor_copy(
                            out=zb_sb[:, qg * 512:(qg + 1) * 512], in_=zzp[qg][:]))
                        chain(v, s_dveN, lambda qg=qg, h=h: v.tensor_tensor(
                            out=ctx_sb[:, (h - 1) * T + qg * 512:
                                       (h - 1) * T + qg * 512 + 512],
                            in0=ctx_sb[:, (h - 1) * T + qg * 512:
                                       (h - 1) * T + qg * 512 + 512],
                            in1=zb_sb[:, qg * 512:(qg + 1) * 512], op=ALU.mult))
                for q in range(NQUAD):
                    for qg in QGS:
                        p0, p1 = 2 * q, 2 * q + 1
                        e0 = (NQG * p0 + qg) % 8
                        e1 = (NQG * p1 + qg) % 8
                        v.wait_ge(s_actE.sem, bE + NQG * p0 + qg + 1)
                        war = bZ + NQG * (q - 2) + qg + 1
                        if war > 0:
                            v.wait_ge(s_peZ4.sem, war)
                        chain(v, s_dveQ, lambda qg=qg, e0=e0: v.tensor_add(
                            out=eq_sb[:, qg * 1024:qg * 1024 + 512],
                            in0=esb[:, e0 * 1024:e0 * 1024 + 512],
                            in1=esb[:, e0 * 1024 + 512:e0 * 1024 + 1024]))
                        v.wait_ge(s_actE.sem, bE + NQG * p1 + qg + 1)
                        chain(v, s_dveQ, lambda qg=qg, e1=e1: v.tensor_add(
                            out=eq_sb[:, qg * 1024 + 512:qg * 1024 + 1024],
                            in0=esb[:, e1 * 1024:e1 * 1024 + 512],
                            in1=esb[:, e1 * 1024 + 512:e1 * 1024 + 1024]))
                        chain(v, s_dveQ, lambda qg=qg, q=q: v.tensor_add(
                            out=e4_sb[:, qg * 1024 + (q % 2) * 512:
                                      qg * 1024 + (q % 2) * 512 + 512],
                            in0=eq_sb[:, qg * 1024:qg * 1024 + 512],
                            in1=eq_sb[:, qg * 1024 + 512:qg * 1024 + 1024]))
                for qg in QGS:
                    v.wait_ge(s_peZ4.sem, bZ + NQG * (NQUAD - 1) + qg + 1)
                    if s_dveZ.n:
                        v.wait_ge(s_dveZ.sem, s_dveZ.n)
                    s_dveZ.inc(v.reciprocal(
                        out=zinv[0:1, qg * 512:(qg + 1) * 512], in_=zzp[qg][0:1, :]))
                for qg in QGS:
                    v.wait_ge(s_peP.sem, NQG * NPAIR * h + NQG * (NPAIR - 1) + qg + 1)
                    chain(v, s_dveC, lambda h=h, qg=qg: v.tensor_copy(
                        out=ctx_sb[:, h * T + qg * 512:h * T + qg * 512 + 512],
                        in_=ctxp[qg][:]))
            h = HEADS
            for qg in QGS:
                v.wait_ge(s_peZB.sem, NQG * (h - 1) + qg + 1)
                chain(v, s_dveN, lambda qg=qg: v.tensor_copy(
                    out=zb_sb[:, qg * 512:(qg + 1) * 512], in_=zzp[qg][:]))
                chain(v, s_dveN, lambda qg=qg, h=h: v.tensor_tensor(
                    out=ctx_sb[:, (h - 1) * T + qg * 512:
                               (h - 1) * T + qg * 512 + 512],
                    in0=ctx_sb[:, (h - 1) * T + qg * 512:
                               (h - 1) * T + qg * 512 + 512],
                    in1=zb_sb[:, qg * 512:(qg + 1) * 512], op=ALU.mult))
        block.vector(p2_vector)

        p2a.close()
        wo_sb = p2.enter_context(nc.sbuf_tensor("wo_sb", [P, 2 * KH * 2 * 512], BF16))
        xr_sb = p2.enter_context(nc.sbuf_tensor("xr_sb", [P, 4 * 512], F32))
        x1s_sb = p2.enter_context(nc.sbuf_tensor("x1s_sb", [P, 4 * 512], F32))
        ops_extra = [p2.enter_context(nc.psum_tensor(f"ops_extra{i}", [P, 512], F32))
                     for i in range(4 * (2 - min(NQG, 2)))]
        if NQG >= 2:
            ops = [sps[0].ap()[:, 0:512], sps[0].ap()[:, 512:1024],
                   sps[1].ap()[:, 0:512], sps[1].ap()[:, 512:1024],
                   ctxp[0].ap()[:], ctxp[1].ap()[:], zzp[0].ap()[:], zzp[1].ap()[:]]
        else:
            ops = [sps[0].ap()[:, 0:512], sps[0].ap()[:, 512:1024],
                   ctxp[0].ap()[:], zzp[0].ap()[:]] + [o.ap()[:] for o in ops_extra]

        snap3 = [s.n for s in REG]

        def p3_gps(g):
            barrier(g, snap3)
            for og in range(VOG):
                for half in range(2):
                    if og >= 2:
                        g.wait_ge(s_peO.sem, QT * (og - 1))
                    kc0 = half * KH
                    base = (og % 2) * 2 * KH * 512
                    s_wo[(og % 2) * 2 + half].dma(g.dma_start(
                        out=wo_sb.ap()[:, base + half * KH * 512:base + (half + 1) * KH * 512]
                        .rearrange("p (kc c) -> p kc c", kc=KH),
                        in_=wo_d.ap()[kc0:kc0 + KH, :, og * 512:(og + 1) * 512]
                        .rearrange("kc p c -> p kc c")))
        block.gpsimd(p3_gps)

        def p3_sync(sy):
            barrier(sy, snap3)
            NI = VOG * QT

            def xr_load(i):
                og, qt = divmod(i, QT)
                s_xr[i % 4].dma(sy.dma_start(
                    out=xr_sb[:, (i % 4) * 512:(i % 4) * 512 + 512],
                    in_=x_d.ap()[qt * P:(qt + 1) * P, og * 512:(og + 1) * 512]))

            for i in range(min(4, NI)):
                xr_load(i)
            for i in range(NI):
                og, qt = divmod(i, QT)
                sy.wait_ge(s_dveX.sem, i + 1)
                s_stX1[i % 4].dma(sy.dma_start(
                    out=x1_d.ap()[qt * P:(qt + 1) * P, og * 512:(og + 1) * 512],
                    in_=x1s_sb[:, (i % 4) * 512:(i % 4) * 512 + 512]))
                if i + 4 < NI:
                    xr_load(i + 4)
        block.sync(p3_sync)

        def p3_tensor(t):
            barrier(t, snap3)
            for og in range(VOG):
                if og >= 1:
                    t.wait_ge(s_dveX.sem, QT * og)
                for half in range(2):
                    t.wait_ge(s_wo[(og % 2) * 2 + half].sem, 16 * (og // 2 + 1))
                    base = (og % 2) * 2 * KH * 512
                    for kcl in range(KH):
                        hc = half * KH + kcl
                        for qt in range(QT):
                            t.matmul(ops[qt],
                                     lhsT=ctx_sb[:, hc * T + qt * P:hc * T + (qt + 1) * P],
                                     rhs=wo_sb[:, base + (half * KH + kcl) * 512:
                                               base + (half * KH + kcl) * 512 + 512],
                                     start=(hc == 0), stop=False)
                for qt in range(QT):
                    mm = t.matmul(ops[qt], lhsT=ones_r[:],
                                  rhs=bo_sb[0:1, og * 512:(og + 1) * 512],
                                  start=False, stop=True)
                    s_peO.inc(mm)
        block.tensor(p3_tensor)

        def p3_vector(v):
            barrier(v, snap3)
            for og in range(VOG):
                for qt in range(QT):
                    i = og * QT + qt
                    v.wait_ge(s_peO.sem, i + 1)
                    v.wait_ge(s_xr[i % 4].sem, 16 * (i // 4 + 1))
                    if i >= 4:
                        v.wait_ge(s_stX1[i % 4].sem, 16 * (i // 4))
                    if s_dveX.n:
                        v.wait_ge(s_dveX.sem, s_dveX.n)
                    s_dveX.inc(v.tensor_tensor(
                        out=x1s_sb[:, (i % 4) * 512:(i % 4) * 512 + 512],
                        in0=ops[qt],
                        in1=xr_sb[:, (i % 4) * 512:(i % 4) * 512 + 512],
                        op=ALU.add))
        block.vector(p3_vector)

        p2.close()

        snap4 = [s.n for s in REG]
        p4 = ExitStack()
        ln2_fm = p4.enter_context(nc.sbuf_tensor("ln2_fm", [P, KC * T], BF16))
        x1_sb = p4.enter_context(nc.sbuf_tensor("x1_sb", [P, 2 * D], F32))
        ln2_tm = p4.enter_context(nc.sbuf_tensor("ln2_tm", [P, 2 * D], BF16))
        outs_sb = p4.enter_context(nc.sbuf_tensor("outs_sb", [P, 2 * D], F32))
        GA = QT * NEXP
        ga_sb = p4.enter_context(nc.sbuf_tensor("ga_sb", [P, GA], F32))
        gl_sb = p4.enter_context(nc.sbuf_tensor("gl_sb", [P, GA], F32))
        gx_sb = p4.enter_context(nc.sbuf_tensor("gx_sb", [P, GA], F32))
        gm_sb = p4.enter_context(nc.sbuf_tensor("gm_sb", [P, 4 * QT], F32))
        grep_sb = p4.enter_context(nc.sbuf_tensor("grep_sb", [P, 2 * GA], F32))
        gt_sb = p4.enter_context(nc.sbuf_tensor("gt_sb", [NEXP, QT * P], F32))
        mid_sb = p4.enter_context(nc.sbuf_tensor("mid_sb", [ER, QT * P], BF16))
        gbs_sb = p4.enter_context(nc.sbuf_tensor("gbs_sb", [ER, QT * P], F32))
        p4ps = ExitStack()
        tps2 = p4ps.enter_context(nc.psum_tensor("tps2", [P, 1024], BF16))
        ln2_fm_r = ln2_fm.ap().rearrange("p (kc t) -> p kc t", kc=KC)

        def p4a_sync(sy):
            barrier(sy, snap4)

            def x1_load(qt):
                s_x1[qt % 2].dma(sy.dma_start(
                    out=x1_sb[:, (qt % 2) * D:(qt % 2) * D + D],
                    in_=x1_d.ap()[qt * P:(qt + 1) * P, :]))

            for qt in range(min(2, QT)):
                x1_load(qt)
            for qt in range(2, QT):
                sy.wait_ge(s_dveL2.sem, 7 * (qt - 2) + 7)
                sy.wait_ge(s_actL2.sem, 3 * (qt - 2) + 1)
                x1_load(qt)
        block.sync(p4a_sync)

        def p4a_vec(v):
            barrier(v, snap4)
            for qt in range(QT):
                v.wait_ge(s_x1[qt % 2].sem, 16 * (qt // 2 + 1))
                emit_ln_vec(v, qt, x1_sb[:, (qt % 2) * D:(qt % 2) * D + D],
                            s_dveL2, s_actL2, s_peTR2, ln2_tm)
                for g in range(NG):
                    G = NG * qt + g
                    v.wait_ge(s_peTR2.sem, G + 1)
                    if s_dveTR2.n:
                        v.wait_ge(s_dveTR2.sem, s_dveTR2.n)
                    s_dveTR2.inc(v.tensor_copy(
                        out=ln2_fm_r[:, g * 8:(g + 1) * 8, qt * P:(qt + 1) * P],
                        in_=tps2.ap()[:, 0:1024]
                        .rearrange("p (kc t) -> p kc t", kc=8)))
        block.vector(p4a_vec)

        def p4a_scalar(sc):
            barrier(sc, snap4)
            for qt in range(QT):
                emit_ln_act(sc, qt, x1_sb[:, (qt % 2) * D:(qt % 2) * D + D],
                            s_dveL2, s_actL2,
                            lambda qt=qt: sc.wait_ge(s_x1[qt % 2].sem,
                                                     16 * (qt // 2 + 1)))
        block.scalar(p4a_scalar)

        def p4a_tensor(t):
            barrier(t, snap4)
            for qt in range(QT):
                t.wait_ge(s_dveL2.sem, 7 * qt + 7)
                for g in range(NG):
                    G = NG * qt + g
                    if G >= 1:
                        t.wait_ge(s_dveTR2.sem, G)
                    for j in range(8):
                        kc = g * 8 + j
                        mm = t.transpose(
                            tps2[:, j * P:(j + 1) * P],
                            in_=ln2_tm[:, (qt % 2) * D + kc * P:(qt % 2) * D + (kc + 1) * P],
                            identity=ident_bf[:])
                    s_peTR2.inc(mm)
        block.tensor(p4a_tensor)

        p4ps.close()
        lgps = p4.enter_context(nc.psum_tensor("lgps", [P, GA], F32))
        mid_all = p4.enter_context(nc.psum_tensor("mid_all", [ER, QT * P], F32))
        gtps = p4.enter_context(nc.psum_tensor("gtps", [NEXP, 2 * P], F32))
        gb_all = p4.enter_context(nc.psum_tensor("gb_all", [ER, QT * P], F32))
        dps = [p4.enter_context(nc.psum_tensor(f"dps{i}", [P, 512], F32))
               for i in range(2)]
        lg3 = lgps.ap().rearrange("p (qt e) -> p qt e", e=NEXP)

        def p4b_tensor(t):
            t.wait_ge(s_dveTR2.sem, NG * QT)
            for qt in range(QT):
                for kc in range(KC):
                    mm = t.matmul(lgps[:, qt * NEXP:(qt + 1) * NEXP],
                                  lhsT=ln2_fm[:, kc * T + qt * P:kc * T + (qt + 1) * P],
                                  rhs=wr_sb[:, kc * NEXP:(kc + 1) * NEXP],
                                  start=(kc == 0), stop=(kc == KC - 1))
            s_peLG.inc(mm)
            for qt in range(QT):
                for kc in range(KC):
                    mm = t.matmul(mid_all[:, qt * P:(qt + 1) * P],
                                  lhsT=wa_sb[:, kc * ER:(kc + 1) * ER],
                                  rhs=ln2_fm[:, kc * T + qt * P:kc * T + (qt + 1) * P],
                                  start=(kc == 0), stop=(kc == KC - 1))
            s_peMID.inc(mm)
            t.wait_ge(s_dveG.sem, 22)
            for qt in range(QT):
                if qt >= 1:
                    t.wait_ge(s_dveE2.sem, qt)
                s_peGT.inc(t.transpose(
                    gtps[:, 0:P],
                    in_=ga_sb[:, qt * NEXP:(qt + 1) * NEXP],
                    identity=ident_f[:]))
            for qt in range(QT):
                t.wait_ge(s_dveE2.sem, qt + 1)
                s_peGB.inc(t.matmul(
                    gb_all[:, qt * P:(qt + 1) * P], lhsT=ind_sb[:],
                    rhs=gt_sb[0:NEXP, qt * P:(qt + 1) * P],
                    start=True, stop=True))
            t.wait_ge(s_dveE2.sem, QT + 2)
            for qt in range(QT):
                for og in range(VOG):
                    i = VOG * qt + og
                    if i >= 2:
                        t.wait_ge(s_dveF.sem, i - 1)
                    s_peD.inc(t.matmul(dps[i % 2][:],
                                       lhsT=mid_sb[:, qt * P:(qt + 1) * P],
                                       rhs=wb_sb[:, og * 512:(og + 1) * 512],
                                       start=True, stop=True))
        block.tensor(p4b_tensor)

        def p4b_scalar(sc):
            sc.wait_ge(s_dveG.sem, 14)
            chain(sc, s_actG, lambda: sc.activation(out=gx_sb[:], in_=gl_sb[:],
                                                    func=AF.Exp))
        block.scalar(p4b_scalar)

        def p4b_vec(v):
            v.wait_ge(s_peLG.sem, 1)
            ch = lambda fn: chain(v, s_dveG, fn)
            ch(lambda: v.reduce_max(out=gm_sb[:, 0:QT], in_=lg3, axis=AX.X))
            for e in range(NEXP):
                ch(lambda e=e: v.tensor_copy(
                    out=bass.AP(grep_sb, e, [[2 * GA, P], [NEXP, QT]]),
                    in_=gm_sb[:, 0:QT]))
            ch(lambda: v.tensor_tensor(out=ga_sb[:], in0=lgps[:],
                                       in1=grep_sb[:, 0:GA], op=ALU.is_ge))
            ch(lambda: v.scalar_tensor_tensor(out=gl_sb[:], in0=ga_sb[:],
                                              scalar=-1e30, in1=lgps[:],
                                              op0=ALU.mult, op1=ALU.add))
            ch(lambda: v.reduce_max(out=gm_sb[:, QT:2 * QT],
                                    in_=gl_sb.ap().rearrange("p (qt e) -> p qt e", e=NEXP),
                                    axis=AX.X))
            for e in range(NEXP):
                ch(lambda e=e: v.tensor_copy(
                    out=bass.AP(grep_sb, GA + e, [[2 * GA, P], [NEXP, QT]]),
                    in_=gm_sb[:, QT:2 * QT]))
            ch(lambda: v.tensor_tensor(out=ga_sb[:], in0=lgps[:],
                                       in1=grep_sb[:, GA:2 * GA], op=ALU.is_ge))
            ch(lambda: v.tensor_tensor(out=gl_sb[:], in0=lgps[:],
                                       in1=grep_sb[:, 0:GA], op=ALU.subtract))
            assert s_dveG.n == 14, s_dveG.n
            v.wait_ge(s_actG.sem, 1)
            ch(lambda: v.tensor_tensor(out=gx_sb[:], in0=gx_sb[:], in1=ga_sb[:],
                                       op=ALU.mult))
            ch(lambda: v.reduce_sum(out=gm_sb[:, 2 * QT:3 * QT],
                                    in_=gx_sb.ap().rearrange("p (qt e) -> p qt e", e=NEXP),
                                    axis=AX.X))
            ch(lambda: v.reciprocal(out=gm_sb[:, 3 * QT:4 * QT],
                                    in_=gm_sb[:, 2 * QT:3 * QT]))
            for e in range(NEXP):
                ch(lambda e=e: v.tensor_copy(
                    out=bass.AP(grep_sb, e, [[2 * GA, P], [NEXP, QT]]),
                    in_=gm_sb[:, 3 * QT:4 * QT]))
            ch(lambda: v.tensor_tensor(out=ga_sb[:], in0=gx_sb[:],
                                       in1=grep_sb[:, 0:GA], op=ALU.mult))
            assert s_dveG.n == 22, s_dveG.n
            for qt in range(QT):
                v.wait_ge(s_peGT.sem, qt + 1)
                chain(v, s_dveE2, lambda qt=qt: v.tensor_copy(
                    out=gt_sb[:, qt * P:(qt + 1) * P],
                    in_=gtps[:, 0:P]))
            v.wait_ge(s_peGB.sem, QT)
            chain(v, s_dveE2, lambda: v.tensor_copy(out=gbs_sb[:], in_=gb_all[:]))
            v.wait_ge(s_peMID.sem, 1)
            chain(v, s_dveE2, lambda: v.tensor_tensor(
                out=mid_sb[:], in0=mid_all[:], in1=gbs_sb[:], op=ALU.mult))
            for qt in range(QT):
                for og in range(VOG):
                    i = VOG * qt + og
                    v.wait_ge(s_peD.sem, i + 1)
                    v.wait_ge(s_x1b[qt % 2].sem, 16 * (qt // 2 + 1))
                    if qt >= 2 and og == 0:
                        v.wait_ge(s_outF[qt % 2].sem, 16 * (qt // 2))
                    if s_dveF.n:
                        v.wait_ge(s_dveF.sem, s_dveF.n)
                    s_dveF.inc(v.tensor_tensor(
                        out=outs_sb[:, (qt % 2) * D + og * 512:(qt % 2) * D + (og + 1) * 512],
                        in0=dps[i % 2][:],
                        in1=x1_sb[:, (qt % 2) * D + og * 512:(qt % 2) * D + (og + 1) * 512],
                        op=ALU.add))
        block.vector(p4b_vec)

        def p4b_sync(sy):
            def x1b_load(qt):
                s_x1b[qt % 2].dma(sy.dma_start(
                    out=x1_sb[:, (qt % 2) * D:(qt % 2) * D + D],
                    in_=x1_d.ap()[qt * P:(qt + 1) * P, :]))

            for qt in range(min(2, QT)):
                sy.wait_ge(s_dveL2.sem, 7 * (QT - 2 + qt) + 7)
                sy.wait_ge(s_actL2.sem, 3 * (QT - 2 + qt) + 1)
                x1b_load(qt)
            for qt in range(QT):
                sy.wait_ge(s_dveF.sem, VOG * qt + VOG)
                s_outF[qt % 2].dma(sy.dma_start(
                    out=out_d.ap()[qt * P:(qt + 1) * P, :],
                    in_=outs_sb[:, (qt % 2) * D:(qt % 2) * D + D]))
                if qt + 2 < QT:
                    x1b_load(qt + 2)
            for s in s_outF:
                if s.n:
                    sy.wait_ge(s.sem, s.n)
        block.sync(p4b_sync)

        def p4_gps_end(g):
            for s in s_outF:
                if s.n:
                    g.wait_ge(s.sem, s.n)
        block.gpsimd(p4_gps_end)

        p4.close()

    return nc, dict(D=D, S=S, T=T)


def _host_prep(D, S, x, qkvw, qkvb, ow, ob, A, B, Wr):
    KC = D // P
    OCH = 2 * D // P
    T = S // NCORES
    x2 = np.ascontiguousarray(np.asarray(x, dtype=np.float32).reshape(S, D))
    qkvw = np.asarray(qkvw, dtype=np.float32)
    w_qk = np.ascontiguousarray(
        qkvw[:, :2 * D].reshape(KC, P, OCH, P).transpose(2, 1, 0, 3).reshape(OCH, P, D))
    w_v = np.ascontiguousarray(qkvw[:, 2 * D:].reshape(KC, P, D))
    qkvb = np.asarray(qkvb, dtype=np.float32)
    b_qk = np.ascontiguousarray(qkvb[:2 * D].reshape(OCH, P).T)
    b_v = np.ascontiguousarray(qkvb[2 * D:].reshape(1, D))
    w_o = np.ascontiguousarray(np.asarray(ow, dtype=np.float32).reshape(KC, P, D))
    b_o = np.ascontiguousarray(np.asarray(ob, dtype=np.float32).reshape(1, D))
    A = np.asarray(A, dtype=np.float32)
    w_a = np.ascontiguousarray(A.transpose(2, 0, 1).reshape(KC, P, ER))
    B = np.asarray(B, dtype=np.float32)
    w_b = np.ascontiguousarray(B.transpose(0, 2, 1).reshape(ER, D))
    Wr = np.asarray(Wr, dtype=np.float32)
    w_r = np.ascontiguousarray(Wr.T.reshape(KC, P, NEXP))
    ident = np.eye(P, dtype=np.float32)
    ind4 = np.zeros((NEXP, ER), dtype=np.float32)
    for e in range(NEXP):
        ind4[e, e * LORA_R:(e + 1) * LORA_R] = LORA_SCALE
    shared = dict(w_qk=w_qk, w_v=w_v, w_o=w_o, b_qk=b_qk, b_v=b_v, b_o=b_o,
                  w_a=w_a, w_b=w_b, w_r=w_r, ident=ident, ind4=ind4)
    in_maps = []
    for c in range(NCORES):
        m = dict(shared)
        m["x"] = np.ascontiguousarray(x2[c * T:(c + 1) * T, :])
        in_maps.append(m)
    return in_maps


def kernel(L0_attn_x, L0_attn_attn_qkvw, L0_attn_attn_qkvb, L0_attn_attn_ow,
           L0_attn_attn_ob, L0_ffn_A, L0_ffn_B, L0_ffn_Wr):
    D = 4096
    S = 8192
    in_maps = _host_prep(D, S, L0_attn_x, L0_attn_attn_qkvw, L0_attn_attn_qkvb,
                         L0_attn_attn_ow, L0_attn_attn_ob, L0_ffn_A, L0_ffn_B,
                         L0_ffn_Wr)
    nc, meta = build_program(D, S)
    res = run_bass_kernel_spmd(nc, in_maps, core_ids=list(range(NCORES)))
    out = np.concatenate([np.asarray(r["out"], dtype=np.float32)
                          for r in res.results], axis=0)
    return out.reshape(1, S, D)


# version 12
# speedup vs baseline: 1.3888x; 1.0196x over previous
import math
from contextlib import ExitStack

import numpy as np
import concourse.bass as bass
import concourse.mybir as mybir
from concourse.bass_utils import run_bass_kernel_spmd

F32 = mybir.dt.float32
BF16 = mybir.dt.bfloat16
AF = mybir.ActivationFunctionType
ALU = mybir.AluOpType
AX = mybir.AxisListType

NCORES = 8
LORA_SCALE = 4.0
NEXP = 4
LORA_R = 16
ER = NEXP * LORA_R
EPS = 1e-5
P = 128


class Sem:

    def __init__(self, ctx, nc, name, registry):
        self.sem = ctx.enter_context(nc.semaphore(name))
        self.n = 0
        registry.append(self)

    def inc(self, instr, k=1):
        instr.then_inc(self.sem, k)
        self.n += k
        return self.n

    def dma(self, instr):
        return self.inc(instr, 16)

    def wait(self, eng, val=None):
        eng.wait_ge(self.sem, self.n if val is None else val)


def build_program(D, S):
    T = S // NCORES
    KC = D // P
    HEADS = D // P
    OCH = 2 * D // P
    VOG = D // 512
    QT = T // P
    NQG = T // 512
    KT = S // P
    NPAIR = KT // 2
    NG = KC // 8
    NQTG = QT // 4
    HPV = 4
    assert T % 512 == 0 and D % 1024 == 0 and QT % 4 == 0

    nc = bass.Bass("TRN2", target_bir_lowering=False)

    x_d = nc.dram_tensor("x", [T, D], F32, kind="ExternalInput")
    wqk_d = nc.dram_tensor("w_qk", [OCH, P, D], F32, kind="ExternalInput")
    wv_d = nc.dram_tensor("w_v", [KC, P, D], F32, kind="ExternalInput")
    wo_d = nc.dram_tensor("w_o", [KC, P, D], F32, kind="ExternalInput")
    bqk_d = nc.dram_tensor("b_qk", [P, OCH], F32, kind="ExternalInput")
    bv_d = nc.dram_tensor("b_v", [1, D], F32, kind="ExternalInput")
    bo_d = nc.dram_tensor("b_o", [1, D], F32, kind="ExternalInput")
    wa_d = nc.dram_tensor("w_a", [KC, P, ER], F32, kind="ExternalInput")
    wb_d = nc.dram_tensor("w_b", [ER, D], F32, kind="ExternalInput")
    wr_d = nc.dram_tensor("w_r", [KC, P, NEXP], F32, kind="ExternalInput")
    id_d = nc.dram_tensor("ident", [P, P], F32, kind="ExternalInput")
    ind_d = nc.dram_tensor("ind4", [NEXP, ER], F32, kind="ExternalInput")

    out_d = nc.dram_tensor("out", [T, D], F32, kind="ExternalOutput")

    cc_in = nc.dram_tensor("cc_in", [2 * D, T], BF16, kind="Internal")
    cck_out = nc.dram_tensor("cck_out", [NCORES * D, T], BF16,
                             kind="Internal", addr_space="Shared")
    ccv_out = nc.dram_tensor("ccv_out", [NCORES * D, T], BF16,
                             kind="Internal", addr_space="Shared")
    qs_d = nc.dram_tensor("qs", [D, T], BF16, kind="Internal")
    x1_d = nc.dram_tensor("x1", [T, D], F32, kind="Internal")

    gctx = ExitStack()
    REG = []

    def sb(name, shape, dtype):
        return gctx.enter_context(nc.sbuf_tensor(name, shape, dtype))

    ident_bf = sb("ident_bf", [P, P], BF16)
    ident_f = sb("ident_f", [P, P], F32)
    bqk_sb = sb("bqk_sb", [P, OCH], F32)
    bv_sb = sb("bv_sb", [1, D], BF16)
    bo_sb = sb("bo_sb", [1, D], BF16)
    wa_sb = sb("wa_sb", [P, KC * ER], BF16)
    wb_sb = sb("wb_sb", [ER, D], BF16)
    wr_sb = sb("wr_sb", [P, KC * NEXP], BF16)
    ind_sb = sb("ind_sb", [NEXP, ER], F32)
    ones_c = sb("ones_c", [P, P], BF16)
    ones_r = sb("ones_r", [1, P], BF16)
    ones_rf = sb("ones_rf", [1, P], F32)
    dump = sb("dump", [P, D], BF16)
    st = sb("st", [P, 32], F32)

    def mks(name):
        return Sem(gctx, nc, name, REG)

    def mkslots(name, k):
        return [mks(f"{name}{j}") for j in range(k)]

    s_small = mks("s_small")
    s_x = mkslots("s_x", 2)
    s_wq = mkslots("s_wq", 2)
    s_wv = mkslots("s_wv", 2)
    s_wo = mkslots("s_wo", 4)
    s_kv = mkslots("s_kv", 2)
    s_x1 = mkslots("s_x1", 2)
    s_x1b = mkslots("s_x1b", 2)
    s_xr = mkslots("s_xr", 4)
    s_cc = mks("s_cc")
    s_dveM = mks("s_dveM")
    s_dveLN = mks("s_dveLN")
    s_actLN = mks("s_actLN")
    s_peTR = mks("s_peTR")
    s_dveTR = mks("s_dveTR")
    s_peQK = mks("s_peQK")
    s_actQK = mks("s_actQK")
    s_stQK = mkslots("s_stQK", 4)
    s_peV = mks("s_peV")
    s_peV2 = mks("s_peV2")
    s_dveV = mks("s_dveV")
    s_stV = mkslots("s_stV", 2)
    s_peS = mks("s_peS")
    s_peP = mks("s_peP")
    s_peZ4 = mks("s_peZ4")
    s_dveQ = mks("s_dveQ")
    s_dveN = mks("s_dveN")
    s_actE = mks("s_actE")
    s_peH = mks("s_peH")
    s_dveZ = mks("s_dveZ")
    s_peZB = mks("s_peZB")
    s_dveC = mks("s_dveC")
    s_peO = mks("s_peO")
    s_dveX = mks("s_dveX")
    s_stX1 = mkslots("s_stX1", 4)
    s_dveL2 = mks("s_dveL2")
    s_actL2 = mks("s_actL2")
    s_peTR2 = mks("s_peTR2")
    s_dveTR2 = mks("s_dveTR2")
    s_peLG = mks("s_peLG")
    s_peMID = mks("s_peMID")
    s_peGT = mks("s_peGT")
    s_peGB = mks("s_peGB")
    s_peD = mks("s_peD")
    s_dveG = mks("s_dveG")
    s_actG = mks("s_actG")
    s_dveE2 = mks("s_dveE2")
    s_dveF = mks("s_dveF")
    s_outF = mkslots("s_outF", 2)

    block = gctx.enter_context(nc.Block())

    def barrier(eng, snap=None):
        vals = snap if snap is not None else [s.n for s in REG]
        for s, v_ in zip(REG, vals):
            if v_:
                eng.wait_ge(s.sem, v_)

    SC = float(1.0 / math.sqrt(P))

    def chain(eng, sem, instr_fn):
        if sem.n:
            eng.wait_ge(sem.sem, sem.n)
        return sem.inc(instr_fn())

    with gctx:
        def p0_gpsimd(g):
            s_small.dma(g.dma_start(out=ident_bf[:], in_=id_d.ap()))
            s_small.dma(g.dma_start(out=ident_f[:], in_=id_d.ap()))
            s_small.dma(g.dma_start(out=bqk_sb[:], in_=bqk_d.ap()))
            s_small.dma(g.dma_start(out=bv_sb[:], in_=bv_d.ap()))
            s_small.dma(g.dma_start(out=bo_sb[:], in_=bo_d.ap()))
            s_small.dma(g.dma_start(
                out=wa_sb.ap().rearrange("p (kc e) -> p kc e", kc=KC),
                in_=wa_d.ap().rearrange("kc p e -> p kc e")))
            s_small.dma(g.dma_start(out=wb_sb[:], in_=wb_d.ap()))
            s_small.dma(g.dma_start(
                out=wr_sb.ap().rearrange("p (kc e) -> p kc e", kc=KC),
                in_=wr_d.ap().rearrange("kc p e -> p kc e")))
            s_small.dma(g.dma_start(out=ind_sb[:], in_=ind_d.ap()))
        block.gpsimd(p0_gpsimd)

        def p0_vector(v):
            v.memset(ones_c[:], 1.0)
            v.memset(ones_r[:], 1.0)
            s_dveM.inc(v.memset(ones_rf[:], 1.0))
        block.vector(p0_vector)

        p1 = ExitStack()
        x_sb = p1.enter_context(nc.sbuf_tensor("x_sb", [P, 2 * D], F32))
        ln_tm = p1.enter_context(nc.sbuf_tensor("ln_tm", [P, D], BF16))
        ln_fm = p1.enter_context(nc.sbuf_tensor("ln_fm", [P, KC * T], BF16))
        wqk_sb = p1.enter_context(nc.sbuf_tensor("wqk_sb", [P, 2 * D], BF16))
        stage_qk = p1.enter_context(nc.sbuf_tensor("stage_qk", [P, 4 * 1024], BF16))
        wv_sb = p1.enter_context(nc.sbuf_tensor("wv_sb", [P, KC // 2 * 2 * 512], BF16))
        stage_v = p1.enter_context(nc.sbuf_tensor("stage_v", [P, 2 * 4 * 512], BF16))
        qkps = [p1.enter_context(nc.psum_tensor(f"qkps{i}", [P, 1024], F32))
                for i in range(2)]
        ps_ln = ExitStack()
        tps_ln = ps_ln.enter_context(nc.psum_tensor("tps_ln", [P, 2 * 1024], BF16))

        ln_fm_r = ln_fm.ap().rearrange("p (kc t) -> p kc t", kc=KC)

        def p1_sync(sy):
            for qt in range(QT):
                if qt >= 2:
                    sy.wait_ge(s_dveLN.sem, 7 * (qt - 2) + 7)
                    sy.wait_ge(s_actLN.sem, 3 * (qt - 2) + 1)
                s_x[qt % 2].dma(sy.dma_start(
                    out=x_sb[:, (qt % 2) * D:(qt % 2) * D + D],
                    in_=x_d.ap()[qt * P:(qt + 1) * P, :]))
        block.sync(p1_sync)

        def emit_ln_vec(v, qt, xap, sem_d, sem_a, s_petr, ln_tm_t, war_lag=2):
            b7 = 7 * qt
            chain(v, sem_d, lambda: v.reduce_sum(out=st[:, 0:1], in_=xap, axis=AX.X))
            chain(v, sem_d, lambda: v.tensor_scalar_mul(st[:, 1:2], st[:, 0:1], 1.0 / D))
            chain(v, sem_d, lambda: v.tensor_tensor(out=st[:, 2:3], in0=st[:, 1:2],
                                                    in1=st[:, 1:2], op=ALU.mult))
            v.wait_ge(sem_a.sem, 3 * qt + 1)
            chain(v, sem_d, lambda: v.tensor_scalar(out=st[:, 3:4], in0=st[:, 4:5],
                                                    scalar1=1.0 / D, scalar2=float(EPS),
                                                    op0=ALU.mult, op1=ALU.add))
            chain(v, sem_d, lambda: v.tensor_tensor(out=st[:, 5:6], in0=st[:, 3:4],
                                                    in1=st[:, 2:3], op=ALU.subtract))
            v.wait_ge(sem_a.sem, 3 * qt + 3)
            chain(v, sem_d, lambda: v.tensor_scalar(out=st[:, 7:8], in0=st[:, 1:2],
                                                    scalar1=st[:, 6:7], scalar2=-1.0,
                                                    op0=ALU.mult, op1=ALU.mult))
            if qt >= war_lag:
                v.wait_ge(s_petr.sem, NG * (qt - war_lag) + NG)
            nb = (qt % 2) * D if war_lag == 2 else 0
            chain(v, sem_d, lambda: v.tensor_scalar(
                out=ln_tm_t[:, nb:nb + D], in0=xap,
                scalar1=st[:, 6:7], scalar2=st[:, 7:8],
                op0=ALU.mult, op1=ALU.add))
            assert sem_d.n == b7 + 7

        def emit_ln_act(sc, qt, xap, sem_d, sem_a, wait_x):
            wait_x()
            if qt >= 1:
                sc.wait_ge(sem_d.sem, 7 * (qt - 1) + 4)
            chain(sc, sem_a, lambda: sc.activation(out=dump[:], in_=xap,
                                                   func=AF.Square,
                                                   accum_out=st[:, 4:5]))
            sc.wait_ge(sem_d.sem, 7 * qt + 5)
            chain(sc, sem_a, lambda: sc.activation(out=st[:, 8:9], in_=st[:, 5:6],
                                                   func=AF.Ln))
            chain(sc, sem_a, lambda: sc.activation(out=st[:, 6:7], in_=st[:, 8:9],
                                                   func=AF.Exp, scale=-0.5))
            assert sem_a.n == 3 * qt + 3

        def p1_vec_ln(v):
            for qt in range(QT):
                v.wait_ge(s_x[qt % 2].sem, 16 * (qt // 2 + 1))
                emit_ln_vec(v, qt, x_sb[:, (qt % 2) * D:(qt % 2) * D + D],
                            s_dveLN, s_actLN, s_peTR, ln_tm, war_lag=1)
                for g in range(NG):
                    G = NG * qt + g
                    v.wait_ge(s_peTR.sem, G + 1)
                    if s_dveTR.n:
                        v.wait_ge(s_dveTR.sem, s_dveTR.n)
                    s_dveTR.inc(v.tensor_copy(
                        out=ln_fm_r[:, g * 8:(g + 1) * 8, qt * P:(qt + 1) * P],
                        in_=tps_ln.ap()[:, (G % 2) * 1024:(G % 2) * 1024 + 1024]
                        .rearrange("p (kc t) -> p kc t", kc=8)))
        block.vector(p1_vec_ln)

        def p1_scalar_ln(sc):
            for qt in range(QT):
                emit_ln_act(sc, qt, x_sb[:, (qt % 2) * D:(qt % 2) * D + D],
                            s_dveLN, s_actLN,
                            lambda qt=qt: sc.wait_ge(s_x[qt % 2].sem,
                                                     16 * (qt // 2 + 1)))
        block.scalar(p1_scalar_ln)

        def p1_pe_tr(t):
            s_small.wait(t)
            t.wait_ge(s_dveM.sem, 1)
            for qt in range(QT):
                t.wait_ge(s_dveLN.sem, 7 * qt + 7)
                for g in range(NG):
                    G = NG * qt + g
                    if G >= 2:
                        t.wait_ge(s_dveTR.sem, G - 1)
                    for j in range(8):
                        kc = g * 8 + j
                        mm = t.transpose(
                            tps_ln[:, (G % 2) * 1024 + j * P:(G % 2) * 1024 + (j + 1) * P],
                            in_=ln_tm[:, kc * P:(kc + 1) * P],
                            identity=ident_bf[:])
                    s_peTR.inc(mm)
        block.tensor(p1_pe_tr)

        order = list(range(HEADS, OCH)) + list(range(0, HEADS))
        NKO = HEADS

        def emit_wqk_loads(g, i0, i1):
            for i in range(i0, i1):
                if i >= 2:
                    g.wait_ge(s_peQK.sem, i - 1)
                s_wq[i % 2].dma(g.dma_start(
                    out=wqk_sb[:, (i % 2) * D:(i % 2) * D + D],
                    in_=wqk_d.ap()[order[i]]))

        def emit_qk_compute(t, i0, i1):
            for i in range(i0, i1):
                t.wait_ge(s_wq[i % 2].sem, 16 * (i // 2 + 1))
                if i == 0:
                    t.wait_ge(s_dveTR.sem, NG * QT)
                if i >= 2:
                    t.wait_ge(s_actQK.sem, i - 1)
                for kc in range(KC):
                    for qg in range(NQG):
                        mm = t.matmul(
                            qkps[i % 2][:, qg * 512:(qg + 1) * 512],
                            lhsT=wqk_sb[:, (i % 2) * D + kc * P:(i % 2) * D + (kc + 1) * P],
                            rhs=ln_fm[:, kc * T + qg * 512:kc * T + qg * 512 + 512],
                            start=(kc == 0), stop=(kc == KC - 1))
                s_peQK.inc(mm)

        def emit_qk_evac(sc, i0, i1):
            for i in range(i0, i1):
                if i == 0:
                    s_small.wait(sc)
                sc.wait_ge(s_peQK.sem, i + 1)
                if i >= 4:
                    sc.wait_ge(s_stQK[i % 4].sem, 16 * (i // 4))
                och = order[i]
                chain(sc, s_actQK, lambda i=i, och=och: sc.activation(
                    out=stage_qk[:, (i % 4) * 1024:(i % 4) * 1024 + T],
                    in_=qkps[i % 2][:, 0:T], func=AF.Identity,
                    bias=bqk_sb[:, och:och + 1]))

        def emit_qk_store(sy, i0, i1):
            for i in range(i0, i1):
                sy.wait_ge(s_actQK.sem, i + 1)
                och = order[i]
                if och >= HEADS:
                    hk = och - HEADS
                    dst = cc_in.ap()[hk * P:(hk + 1) * P, :]
                else:
                    dst = qs_d.ap()[och * P:(och + 1) * P, :]
                s_stQK[i % 4].dma(sy.dma_start(
                    out=dst, in_=stage_qk[:, (i % 4) * 1024:(i % 4) * 1024 + T]))

        block.gpsimd(lambda g: emit_wqk_loads(g, 0, NKO))

        def p1_gps_cck(g):
            for s in s_stQK:
                if s.n:
                    g.wait_ge(s.sem, s.n)
            s_cc.inc(g.collective_compute(
                "AllGather", ALU.bypass,
                replica_groups=[list(range(NCORES))],
                ins=[cc_in.ap()[0:D, :]], outs=[cck_out.ap()]))

        block.tensor(lambda t: emit_qk_compute(t, 0, NKO))
        block.scalar(lambda sc: emit_qk_evac(sc, 0, NKO))
        block.sync(lambda sy: emit_qk_store(sy, 0, NKO))

        ps_ln.close()
        vps = [p1.enter_context(nc.psum_tensor(f"vps{i}", [P, 512], F32))
               for i in range(4)]
        KH = KC // 2

        def p1_gps_wv(g, v0, v1):
            for vog in range(v0, v1):
                for half in range(2):
                    if vog >= 1:
                        g.wait_ge(s_peV.sem,
                                  2 * NQTG * (vog - 1) + 2 * (NQTG - 1) + half + 1)
                    kc0 = half * KH
                    s_wv[half].dma(g.dma_start(
                        out=wv_sb.ap()[:, half * KH * 512:(half + 1) * KH * 512]
                        .rearrange("p (kc c) -> p kc c", kc=KH),
                        in_=wv_d.ap()[kc0:kc0 + KH, :, vog * 512:(vog + 1) * 512]
                        .rearrange("kc p c -> p kc c")))
        block.gpsimd(lambda g: p1_gps_wv(g, 0, 1))
        block.gpsimd(p1_gps_cck)
        block.gpsimd(lambda g: p1_gps_wv(g, 1, VOG))

        def p1_pe_v(t):
            for vog in range(VOG):
                for qtg in range(NQTG):
                    gi = NQTG * vog + qtg
                    if gi >= 1:
                        t.wait_ge(s_dveV.sem, 4 * gi)
                    for half in range(2):
                        t.wait_ge(s_wv[half].sem, 16 * (vog + 1))
                        for kcl in range(KH):
                            kc = half * KH + kcl
                            for q4 in range(4):
                                qt = qtg * 4 + q4
                                mm = t.matmul(
                                    vps[q4][:],
                                    lhsT=ln_fm[:, kc * T + qt * P:kc * T + (qt + 1) * P],
                                    rhs=wv_sb[:, (half * KH + kcl) * 512:
                                              (half * KH + kcl) * 512 + 512],
                                    start=(kc == 0), stop=False)
                        s_peV.inc(mm)
                    for q4 in range(4):
                        mm = t.matmul(vps[q4][:], lhsT=ones_r[:],
                                      rhs=bv_sb[0:1, vog * 512:(vog + 1) * 512],
                                      start=False, stop=True)
                        s_peV2.inc(mm)
        block.tensor(p1_pe_v)

        def p1_vec_v(v):
            for vog in range(VOG):
                for qtg in range(NQTG):
                    gi = NQTG * vog + qtg
                    for q4 in range(4):
                        v.wait_ge(s_peV2.sem, 4 * gi + q4 + 1)
                        if gi >= 2:
                            v.wait_ge(s_stV[gi % 2].sem, 16 * HPV * (gi // 2))
                        if s_dveV.n:
                            v.wait_ge(s_dveV.sem, s_dveV.n)
                        s_dveV.inc(v.tensor_copy(
                            out=stage_v[:, (gi % 2) * 2048 + q4 * 512:
                                        (gi % 2) * 2048 + (q4 + 1) * 512],
                            in_=vps[q4][:]))
        block.vector(p1_vec_v)

        def p1_sync_v(sy):
            for vog in range(VOG):
                for qtg in range(NQTG):
                    gi = NQTG * vog + qtg
                    sy.wait_ge(s_dveV.sem, 4 * gi + 4)
                    for j in range(HPV):
                        hv = vog * HPV + j
                        dst = cc_in.ap()[D + hv * P:D + (hv + 1) * P,
                                         qtg * 4 * P:(qtg + 1) * 4 * P] \
                            .rearrange("p (qt c) -> p qt c", qt=4)
                        src = stage_v.ap()[:, (gi % 2) * 2048:(gi % 2) * 2048 + 2048] \
                            .rearrange("p (qt c) -> p qt c", qt=4)[:, :, j * P:(j + 1) * P]
                        s_stV[gi % 2].dma(sy.dma_start(out=dst, in_=src))
        block.sync(p1_sync_v)

        def p1_gps_ccv(g):
            for s in s_stV:
                if s.n:
                    g.wait_ge(s.sem, s.n)
            s_cc.inc(g.collective_compute(
                "AllGather", ALU.bypass,
                replica_groups=[list(range(NCORES))],
                ins=[cc_in.ap()[D:2 * D, :]], outs=[ccv_out.ap()]))
        block.gpsimd(lambda g: emit_wqk_loads(g, NKO, NKO + 3))
        block.gpsimd(p1_gps_ccv)

        block.gpsimd(lambda g: emit_wqk_loads(g, NKO + 3, OCH))
        block.tensor(lambda t: emit_qk_compute(t, NKO, OCH))
        block.scalar(lambda sc: emit_qk_evac(sc, NKO, OCH))
        block.sync(lambda sy: emit_qk_store(sy, NKO, OCH))

        p1.close()

        p2 = ExitStack()
        ctx_sb = p2.enter_context(nc.sbuf_tensor("ctx_sb", [P, HEADS * T], BF16))
        sps = [p2.enter_context(nc.psum_tensor(f"sps{i}", [P, 1024], F32))
               for i in range(NQG)]
        ctxp = [p2.enter_context(nc.psum_tensor(f"ctxp{i}", [P, 512], F32))
                for i in range(NQG)]
        zzp = [p2.enter_context(nc.psum_tensor(f"zzp{i}", [P, 512], F32))
               for i in range(NQG)]
        p2a = ExitStack()
        ksb = p2a.enter_context(nc.sbuf_tensor("ksb", [P, 2 * S], BF16))
        vsb = p2a.enter_context(nc.sbuf_tensor("vsb", [P, 2 * S], BF16))
        qsb = p2a.enter_context(nc.sbuf_tensor("qsb", [P, 2 * T], BF16))
        esb = p2a.enter_context(nc.sbuf_tensor("esb", [P, 8 * 1024], BF16))
        eq_sb = p2a.enter_context(nc.sbuf_tensor("eq_sb", [P, NQG * 1024], BF16))
        e4_sb = p2a.enter_context(nc.sbuf_tensor("e4_sb", [P, NQG * 1024], BF16))
        zb_sb = p2a.enter_context(nc.sbuf_tensor("zb_sb", [P, NQG * 512], F32))

        NL = 1 + 2 * NCORES
        snap2 = [s.n for s in REG]

        def p2_sync(sy):
            barrier(sy, snap2)
            sy.wait_ge(s_cc.sem, 2)
            for h in range(HEADS):
                if h >= 2:
                    sy.wait_ge(s_peH.sem, h - 1)
                hb = h % 2
                s_kv[hb].dma(sy.dma_start(out=qsb[:, hb * T:hb * T + T],
                                          in_=qs_d.ap()[h * P:(h + 1) * P, :]))
                for r in range(NCORES):
                    s_kv[hb].dma(sy.dma_start(
                        out=ksb[:, hb * S + r * T:hb * S + (r + 1) * T],
                        in_=cck_out.ap()[r * D + h * P:r * D + (h + 1) * P, :]))
                for r in range(NCORES):
                    s_kv[hb].dma(sy.dma_start(
                        out=vsb.ap()[:, hb * S + r * T:hb * S + (r + 1) * T]
                        .rearrange("p (qt c) -> p qt c", qt=QT),
                        in_=ccv_out.ap()[r * D + h * P:r * D + (h + 1) * P, :]
                        .rearrange("p (qt c) -> p qt c", qt=QT)))
        block.sync(p2_sync)

        NQUAD = KT // 4
        QGS = list(range(NQG))

        def p2_tensor(t):
            barrier(t, snap2)
            for h in range(HEADS):
                hb = h % 2
                t.wait_ge(s_kv[h % 2].sem, 16 * NL * (h // 2 + 1))
                bE = NQG * NPAIR * h
                bP = NQG * NPAIR * h
                bZ = NQG * NQUAD * h
                if h >= 1:
                    t.wait_ge(s_actE.sem, bE)
                for p in range(NPAIR):
                    for qg in QGS:
                        if p >= 1:
                            t.wait_ge(s_actE.sem, bE + NQG * (p - 1) + qg + 1)
                        for j in range(2):
                            kt = 2 * p + j
                            mm = t.matmul(
                                sps[qg][:, j * 512:(j + 1) * 512],
                                lhsT=ksb[:, hb * S + kt * P:hb * S + (kt + 1) * P],
                                rhs=qsb[:, hb * T + qg * 512:hb * T + qg * 512 + 512],
                                start=True, stop=True)
                        s_peS.inc(mm)
                    if p >= 1:
                        pv = p - 1
                        for qg in QGS:
                            if pv == 0:
                                if h >= 1:
                                    t.wait_ge(s_dveC.sem, NQG * (h - 1) + qg + 1)
                            sl = (NQG * pv + qg) % 8
                            for j in range(2):
                                kt = 2 * pv + j
                                mm = t.matmul(
                                    ctxp[qg][:],
                                    lhsT=vsb[:, hb * S + kt * P:hb * S + (kt + 1) * P],
                                    rhs=esb[:, sl * 1024 + j * 512:sl * 1024 + (j + 1) * 512],
                                    start=(kt == 0), stop=False)
                            s_peP.inc(mm)
                            if pv % 2 == 1:
                                q = pv // 2
                                t.wait_ge(s_dveQ.sem, 3 * NQG * NQUAD * h + 3 * NQG * q + 3 * qg + 3)
                                if q == 0 and h >= 1:
                                    t.wait_ge(s_dveZ.sem, NQG * (h - 1) + qg + 1)
                                s_peZ4.inc(t.matmul(
                                    zzp[qg][:], lhsT=ones_c[:],
                                    rhs=e4_sb[:, qg * 1024 + (q % 2) * 512:
                                              qg * 1024 + (q % 2) * 512 + 512],
                                    start=(q == 0), stop=(q == NQUAD - 1)))
                pv = NPAIR - 1
                for qg in QGS:
                    t.wait_ge(s_actE.sem, bE + NQG * pv + qg + 1)
                    sl = (NQG * pv + qg) % 8
                    for j in range(2):
                        kt = 2 * pv + j
                        mm = t.matmul(
                            ctxp[qg][:],
                            lhsT=vsb[:, hb * S + kt * P:hb * S + (kt + 1) * P],
                            rhs=esb[:, sl * 1024 + j * 512:sl * 1024 + (j + 1) * 512],
                            start=False, stop=(j == 1))
                    s_peP.inc(mm)
                    q = pv // 2
                    t.wait_ge(s_dveQ.sem, 3 * NQG * NQUAD * h + 3 * NQG * q + 3 * qg + 3)
                    s_peZ4.inc(t.matmul(
                        zzp[qg][:], lhsT=ones_c[:],
                        rhs=e4_sb[:, qg * 1024 + (q % 2) * 512:
                                  qg * 1024 + (q % 2) * 512 + 512],
                        start=False, stop=True))
                t.wait_ge(s_peP.sem, bP + NQG * NPAIR)
                t.sem_inc(s_peH.sem, 1)
                s_peH.n += 1
        block.tensor(p2_tensor)

        def p2_scalar(sc):
            barrier(sc, snap2)
            for h in range(HEADS):
                bS = NQG * NPAIR * h
                bP = NQG * NPAIR * h
                bQ = 3 * NQG * NQUAD * h
                for p in range(NPAIR):
                    for qg in QGS:
                        e = NQG * p + qg
                        sc.wait_ge(s_peS.sem, bS + e + 1)
                        if p >= 8 // NQG:
                            pp = p - 8 // NQG
                            sc.wait_ge(s_peP.sem, bP + NQG * pp + qg + 1)
                            sc.wait_ge(s_dveQ.sem,
                                       bQ + 3 * NQG * (pp // 2) + 3 * qg + (pp % 2) + 1)
                        elif h >= 1:
                            sc.wait_ge(s_peP.sem, bP)
                            sc.wait_ge(s_dveQ.sem, bQ)
                        chain(sc, s_actE, lambda qg=qg, e=e: sc.activation(
                            out=esb[:, (e % 8) * 1024:(e % 8) * 1024 + 1024],
                            in_=sps[qg][:], func=AF.Exp, scale=SC))
        block.scalar(p2_scalar)

        def p2_vector(v):
            barrier(v, snap2)
            for h in range(HEADS):
                bE = NQG * NPAIR * h
                bZ = NQG * NQUAD * h
                for q in range(NQUAD):
                    for qg in QGS:
                        p0, p1 = 2 * q, 2 * q + 1
                        e0 = (NQG * p0 + qg) % 8
                        e1 = (NQG * p1 + qg) % 8
                        v.wait_ge(s_actE.sem, bE + NQG * p0 + qg + 1)
                        war = bZ + NQG * (q - 2) + qg + 1
                        if war > 0:
                            v.wait_ge(s_peZ4.sem, war)
                        chain(v, s_dveQ, lambda qg=qg, e0=e0: v.tensor_add(
                            out=eq_sb[:, qg * 1024:qg * 1024 + 512],
                            in0=esb[:, e0 * 1024:e0 * 1024 + 512],
                            in1=esb[:, e0 * 1024 + 512:e0 * 1024 + 1024]))
                        v.wait_ge(s_actE.sem, bE + NQG * p1 + qg + 1)
                        chain(v, s_dveQ, lambda qg=qg, e1=e1: v.tensor_add(
                            out=eq_sb[:, qg * 1024 + 512:qg * 1024 + 1024],
                            in0=esb[:, e1 * 1024:e1 * 1024 + 512],
                            in1=esb[:, e1 * 1024 + 512:e1 * 1024 + 1024]))
                        chain(v, s_dveQ, lambda qg=qg, q=q: v.tensor_add(
                            out=e4_sb[:, qg * 1024 + (q % 2) * 512:
                                      qg * 1024 + (q % 2) * 512 + 512],
                            in0=eq_sb[:, qg * 1024:qg * 1024 + 512],
                            in1=eq_sb[:, qg * 1024 + 512:qg * 1024 + 1024]))
                for qg in QGS:
                    v.wait_ge(s_peZ4.sem, bZ + NQG * (NQUAD - 1) + qg + 1)
                    if s_dveZ.n:
                        v.wait_ge(s_dveZ.sem, s_dveZ.n)
                    s_dveZ.inc(v.reciprocal(
                        out=zb_sb[:, qg * 512:(qg + 1) * 512], in_=zzp[qg][:]))
                for qg in QGS:
                    v.wait_ge(s_peP.sem, NQG * NPAIR * h + NQG * (NPAIR - 1) + qg + 1)
                    chain(v, s_dveC, lambda h=h, qg=qg: v.tensor_copy(
                        out=ctx_sb[:, h * T + qg * 512:h * T + qg * 512 + 512],
                        in_=ctxp[qg][:]))
                for qg in QGS:
                    v.wait_ge(s_dveZ.sem, NQG * h + qg + 1)
                    v.wait_ge(s_dveC.sem, NQG * h + qg + 1)
                    chain(v, s_dveN, lambda qg=qg, h=h: v.tensor_tensor(
                        out=ctx_sb[:, h * T + qg * 512:h * T + qg * 512 + 512],
                        in0=ctx_sb[:, h * T + qg * 512:h * T + qg * 512 + 512],
                        in1=zb_sb[:, qg * 512:(qg + 1) * 512], op=ALU.mult))
        block.vector(p2_vector)

        p2a.close()
        wo_sb = p2.enter_context(nc.sbuf_tensor("wo_sb", [P, 2 * KH * 2 * 512], BF16))
        xr_sb = p2.enter_context(nc.sbuf_tensor("xr_sb", [P, 4 * 512], F32))
        x1s_sb = p2.enter_context(nc.sbuf_tensor("x1s_sb", [P, 4 * 512], F32))
        ops_extra = [p2.enter_context(nc.psum_tensor(f"ops_extra{i}", [P, 512], F32))
                     for i in range(4 * (2 - min(NQG, 2)))]
        if NQG >= 2:
            ops = [sps[0].ap()[:, 0:512], sps[0].ap()[:, 512:1024],
                   sps[1].ap()[:, 0:512], sps[1].ap()[:, 512:1024],
                   ctxp[0].ap()[:], ctxp[1].ap()[:], zzp[0].ap()[:], zzp[1].ap()[:]]
        else:
            ops = [sps[0].ap()[:, 0:512], sps[0].ap()[:, 512:1024],
                   ctxp[0].ap()[:], zzp[0].ap()[:]] + [o.ap()[:] for o in ops_extra]

        snap3 = [s.n for s in REG]

        def p3_gps(g):
            barrier(g, snap3)
            for og in range(VOG):
                for half in range(2):
                    if og >= 2:
                        g.wait_ge(s_peO.sem, QT * (og - 1))
                    kc0 = half * KH
                    base = (og % 2) * 2 * KH * 512
                    s_wo[(og % 2) * 2 + half].dma(g.dma_start(
                        out=wo_sb.ap()[:, base + half * KH * 512:base + (half + 1) * KH * 512]
                        .rearrange("p (kc c) -> p kc c", kc=KH),
                        in_=wo_d.ap()[kc0:kc0 + KH, :, og * 512:(og + 1) * 512]
                        .rearrange("kc p c -> p kc c")))
        block.gpsimd(p3_gps)

        def p3_sync(sy):
            barrier(sy, snap3)
            NI = VOG * QT

            def xr_load(i):
                og, qt = divmod(i, QT)
                s_xr[i % 4].dma(sy.dma_start(
                    out=xr_sb[:, (i % 4) * 512:(i % 4) * 512 + 512],
                    in_=x_d.ap()[qt * P:(qt + 1) * P, og * 512:(og + 1) * 512]))

            for i in range(min(4, NI)):
                xr_load(i)
            for i in range(NI):
                og, qt = divmod(i, QT)
                sy.wait_ge(s_dveX.sem, i + 1)
                s_stX1[i % 4].dma(sy.dma_start(
                    out=x1_d.ap()[qt * P:(qt + 1) * P, og * 512:(og + 1) * 512],
                    in_=x1s_sb[:, (i % 4) * 512:(i % 4) * 512 + 512]))
                if i + 4 < NI:
                    xr_load(i + 4)
        block.sync(p3_sync)

        def p3_tensor(t):
            barrier(t, snap3)
            for og in range(VOG):
                if og >= 1:
                    t.wait_ge(s_dveX.sem, QT * og)
                for half in range(2):
                    t.wait_ge(s_wo[(og % 2) * 2 + half].sem, 16 * (og // 2 + 1))
                    base = (og % 2) * 2 * KH * 512
                    for kcl in range(KH):
                        hc = half * KH + kcl
                        for qt in range(QT):
                            t.matmul(ops[qt],
                                     lhsT=ctx_sb[:, hc * T + qt * P:hc * T + (qt + 1) * P],
                                     rhs=wo_sb[:, base + (half * KH + kcl) * 512:
                                               base + (half * KH + kcl) * 512 + 512],
                                     start=(hc == 0), stop=False)
                for qt in range(QT):
                    mm = t.matmul(ops[qt], lhsT=ones_r[:],
                                  rhs=bo_sb[0:1, og * 512:(og + 1) * 512],
                                  start=False, stop=True)
                    s_peO.inc(mm)
        block.tensor(p3_tensor)

        def p3_vector(v):
            barrier(v, snap3)
            for og in range(VOG):
                for qt in range(QT):
                    i = og * QT + qt
                    v.wait_ge(s_peO.sem, i + 1)
                    v.wait_ge(s_xr[i % 4].sem, 16 * (i // 4 + 1))
                    if i >= 4:
                        v.wait_ge(s_stX1[i % 4].sem, 16 * (i // 4))
                    if s_dveX.n:
                        v.wait_ge(s_dveX.sem, s_dveX.n)
                    s_dveX.inc(v.tensor_tensor(
                        out=x1s_sb[:, (i % 4) * 512:(i % 4) * 512 + 512],
                        in0=ops[qt],
                        in1=xr_sb[:, (i % 4) * 512:(i % 4) * 512 + 512],
                        op=ALU.add))
        block.vector(p3_vector)

        p2.close()

        snap4 = [s.n for s in REG]
        p4 = ExitStack()
        ln2_fm = p4.enter_context(nc.sbuf_tensor("ln2_fm", [P, KC * T], BF16))
        x1_sb = p4.enter_context(nc.sbuf_tensor("x1_sb", [P, 2 * D], F32))
        ln2_tm = p4.enter_context(nc.sbuf_tensor("ln2_tm", [P, 2 * D], BF16))
        outs_sb = p4.enter_context(nc.sbuf_tensor("outs_sb", [P, 2 * D], F32))
        GA = QT * NEXP
        ga_sb = p4.enter_context(nc.sbuf_tensor("ga_sb", [P, GA], F32))
        gl_sb = p4.enter_context(nc.sbuf_tensor("gl_sb", [P, GA], F32))
        gx_sb = p4.enter_context(nc.sbuf_tensor("gx_sb", [P, GA], F32))
        gm_sb = p4.enter_context(nc.sbuf_tensor("gm_sb", [P, 4 * QT], F32))
        grep_sb = p4.enter_context(nc.sbuf_tensor("grep_sb", [P, 2 * GA], F32))
        gt_sb = p4.enter_context(nc.sbuf_tensor("gt_sb", [NEXP, QT * P], F32))
        mid_sb = p4.enter_context(nc.sbuf_tensor("mid_sb", [ER, QT * P], BF16))
        gbs_sb = p4.enter_context(nc.sbuf_tensor("gbs_sb", [ER, QT * P], F32))
        p4ps = ExitStack()
        tps2 = p4ps.enter_context(nc.psum_tensor("tps2", [P, 1024], BF16))
        ln2_fm_r = ln2_fm.ap().rearrange("p (kc t) -> p kc t", kc=KC)

        def p4a_sync(sy):
            barrier(sy, snap4)

            def x1_load(qt):
                s_x1[qt % 2].dma(sy.dma_start(
                    out=x1_sb[:, (qt % 2) * D:(qt % 2) * D + D],
                    in_=x1_d.ap()[qt * P:(qt + 1) * P, :]))

            for qt in range(min(2, QT)):
                x1_load(qt)
            for qt in range(2, QT):
                sy.wait_ge(s_dveL2.sem, 7 * (qt - 2) + 7)
                sy.wait_ge(s_actL2.sem, 3 * (qt - 2) + 1)
                x1_load(qt)
        block.sync(p4a_sync)

        def p4a_vec(v):
            barrier(v, snap4)
            for qt in range(QT):
                v.wait_ge(s_x1[qt % 2].sem, 16 * (qt // 2 + 1))
                emit_ln_vec(v, qt, x1_sb[:, (qt % 2) * D:(qt % 2) * D + D],
                            s_dveL2, s_actL2, s_peTR2, ln2_tm)
                for g in range(NG):
                    G = NG * qt + g
                    v.wait_ge(s_peTR2.sem, G + 1)
                    if s_dveTR2.n:
                        v.wait_ge(s_dveTR2.sem, s_dveTR2.n)
                    s_dveTR2.inc(v.tensor_copy(
                        out=ln2_fm_r[:, g * 8:(g + 1) * 8, qt * P:(qt + 1) * P],
                        in_=tps2.ap()[:, 0:1024]
                        .rearrange("p (kc t) -> p kc t", kc=8)))
        block.vector(p4a_vec)

        def p4a_scalar(sc):
            barrier(sc, snap4)
            for qt in range(QT):
                emit_ln_act(sc, qt, x1_sb[:, (qt % 2) * D:(qt % 2) * D + D],
                            s_dveL2, s_actL2,
                            lambda qt=qt: sc.wait_ge(s_x1[qt % 2].sem,
                                                     16 * (qt // 2 + 1)))
        block.scalar(p4a_scalar)

        def p4a_tensor(t):
            barrier(t, snap4)
            for qt in range(QT):
                t.wait_ge(s_dveL2.sem, 7 * qt + 7)
                for g in range(NG):
                    G = NG * qt + g
                    if G >= 1:
                        t.wait_ge(s_dveTR2.sem, G)
                    for j in range(8):
                        kc = g * 8 + j
                        mm = t.transpose(
                            tps2[:, j * P:(j + 1) * P],
                            in_=ln2_tm[:, (qt % 2) * D + kc * P:(qt % 2) * D + (kc + 1) * P],
                            identity=ident_bf[:])
                    s_peTR2.inc(mm)
        block.tensor(p4a_tensor)

        p4ps.close()
        lgps = p4.enter_context(nc.psum_tensor("lgps", [P, GA], F32))
        mid_all = p4.enter_context(nc.psum_tensor("mid_all", [ER, QT * P], F32))
        gtps = p4.enter_context(nc.psum_tensor("gtps", [NEXP, 2 * P], F32))
        gb_all = p4.enter_context(nc.psum_tensor("gb_all", [ER, QT * P], F32))
        dps = [p4.enter_context(nc.psum_tensor(f"dps{i}", [P, 512], F32))
               for i in range(2)]
        lg3 = lgps.ap().rearrange("p (qt e) -> p qt e", e=NEXP)

        def p4b_tensor(t):
            t.wait_ge(s_dveTR2.sem, NG * QT)
            for qt in range(QT):
                for kc in range(KC):
                    mm = t.matmul(lgps[:, qt * NEXP:(qt + 1) * NEXP],
                                  lhsT=ln2_fm[:, kc * T + qt * P:kc * T + (qt + 1) * P],
                                  rhs=wr_sb[:, kc * NEXP:(kc + 1) * NEXP],
                                  start=(kc == 0), stop=(kc == KC - 1))
            s_peLG.inc(mm)
            for qt in range(QT):
                for kc in range(KC):
                    mm = t.matmul(mid_all[:, qt * P:(qt + 1) * P],
                                  lhsT=wa_sb[:, kc * ER:(kc + 1) * ER],
                                  rhs=ln2_fm[:, kc * T + qt * P:kc * T + (qt + 1) * P],
                                  start=(kc == 0), stop=(kc == KC - 1))
            s_peMID.inc(mm)
            t.wait_ge(s_dveG.sem, 22)
            for qt in range(QT):
                if qt >= 1:
                    t.wait_ge(s_dveE2.sem, qt)
                s_peGT.inc(t.transpose(
                    gtps[:, 0:P],
                    in_=ga_sb[:, qt * NEXP:(qt + 1) * NEXP],
                    identity=ident_f[:]))
            for qt in range(QT):
                t.wait_ge(s_dveE2.sem, qt + 1)
                s_peGB.inc(t.matmul(
                    gb_all[:, qt * P:(qt + 1) * P], lhsT=ind_sb[:],
                    rhs=gt_sb[0:NEXP, qt * P:(qt + 1) * P],
                    start=True, stop=True))
            t.wait_ge(s_dveE2.sem, QT + 2)
            for qt in range(QT):
                for og in range(VOG):
                    i = VOG * qt + og
                    if i >= 2:
                        t.wait_ge(s_dveF.sem, i - 1)
                    s_peD.inc(t.matmul(dps[i % 2][:],
                                       lhsT=mid_sb[:, qt * P:(qt + 1) * P],
                                       rhs=wb_sb[:, og * 512:(og + 1) * 512],
                                       start=True, stop=True))
        block.tensor(p4b_tensor)

        def p4b_scalar(sc):
            sc.wait_ge(s_dveG.sem, 14)
            chain(sc, s_actG, lambda: sc.activation(out=gx_sb[:], in_=gl_sb[:],
                                                    func=AF.Exp))
        block.scalar(p4b_scalar)

        def p4b_vec(v):
            v.wait_ge(s_peLG.sem, 1)
            ch = lambda fn: chain(v, s_dveG, fn)
            ch(lambda: v.reduce_max(out=gm_sb[:, 0:QT], in_=lg3, axis=AX.X))
            for e in range(NEXP):
                ch(lambda e=e: v.tensor_copy(
                    out=bass.AP(grep_sb, e, [[2 * GA, P], [NEXP, QT]]),
                    in_=gm_sb[:, 0:QT]))
            ch(lambda: v.tensor_tensor(out=ga_sb[:], in0=lgps[:],
                                       in1=grep_sb[:, 0:GA], op=ALU.is_ge))
            ch(lambda: v.scalar_tensor_tensor(out=gl_sb[:], in0=ga_sb[:],
                                              scalar=-1e30, in1=lgps[:],
                                              op0=ALU.mult, op1=ALU.add))
            ch(lambda: v.reduce_max(out=gm_sb[:, QT:2 * QT],
                                    in_=gl_sb.ap().rearrange("p (qt e) -> p qt e", e=NEXP),
                                    axis=AX.X))
            for e in range(NEXP):
                ch(lambda e=e: v.tensor_copy(
                    out=bass.AP(grep_sb, GA + e, [[2 * GA, P], [NEXP, QT]]),
                    in_=gm_sb[:, QT:2 * QT]))
            ch(lambda: v.tensor_tensor(out=ga_sb[:], in0=lgps[:],
                                       in1=grep_sb[:, GA:2 * GA], op=ALU.is_ge))
            ch(lambda: v.tensor_tensor(out=gl_sb[:], in0=lgps[:],
                                       in1=grep_sb[:, 0:GA], op=ALU.subtract))
            assert s_dveG.n == 14, s_dveG.n
            v.wait_ge(s_actG.sem, 1)
            ch(lambda: v.tensor_tensor(out=gx_sb[:], in0=gx_sb[:], in1=ga_sb[:],
                                       op=ALU.mult))
            ch(lambda: v.reduce_sum(out=gm_sb[:, 2 * QT:3 * QT],
                                    in_=gx_sb.ap().rearrange("p (qt e) -> p qt e", e=NEXP),
                                    axis=AX.X))
            ch(lambda: v.reciprocal(out=gm_sb[:, 3 * QT:4 * QT],
                                    in_=gm_sb[:, 2 * QT:3 * QT]))
            for e in range(NEXP):
                ch(lambda e=e: v.tensor_copy(
                    out=bass.AP(grep_sb, e, [[2 * GA, P], [NEXP, QT]]),
                    in_=gm_sb[:, 3 * QT:4 * QT]))
            ch(lambda: v.tensor_tensor(out=ga_sb[:], in0=gx_sb[:],
                                       in1=grep_sb[:, 0:GA], op=ALU.mult))
            assert s_dveG.n == 22, s_dveG.n
            for qt in range(QT):
                v.wait_ge(s_peGT.sem, qt + 1)
                chain(v, s_dveE2, lambda qt=qt: v.tensor_copy(
                    out=gt_sb[:, qt * P:(qt + 1) * P],
                    in_=gtps[:, 0:P]))
            v.wait_ge(s_peGB.sem, QT)
            chain(v, s_dveE2, lambda: v.tensor_copy(out=gbs_sb[:], in_=gb_all[:]))
            v.wait_ge(s_peMID.sem, 1)
            chain(v, s_dveE2, lambda: v.tensor_tensor(
                out=mid_sb[:], in0=mid_all[:], in1=gbs_sb[:], op=ALU.mult))
            for qt in range(QT):
                for og in range(VOG):
                    i = VOG * qt + og
                    v.wait_ge(s_peD.sem, i + 1)
                    v.wait_ge(s_x1b[qt % 2].sem, 16 * (qt // 2 + 1))
                    if qt >= 2 and og == 0:
                        v.wait_ge(s_outF[qt % 2].sem, 16 * (qt // 2))
                    if s_dveF.n:
                        v.wait_ge(s_dveF.sem, s_dveF.n)
                    s_dveF.inc(v.tensor_tensor(
                        out=outs_sb[:, (qt % 2) * D + og * 512:(qt % 2) * D + (og + 1) * 512],
                        in0=dps[i % 2][:],
                        in1=x1_sb[:, (qt % 2) * D + og * 512:(qt % 2) * D + (og + 1) * 512],
                        op=ALU.add))
        block.vector(p4b_vec)

        def p4b_sync(sy):
            def x1b_load(qt):
                s_x1b[qt % 2].dma(sy.dma_start(
                    out=x1_sb[:, (qt % 2) * D:(qt % 2) * D + D],
                    in_=x1_d.ap()[qt * P:(qt + 1) * P, :]))

            for qt in range(min(2, QT)):
                sy.wait_ge(s_dveL2.sem, 7 * (QT - 2 + qt) + 7)
                sy.wait_ge(s_actL2.sem, 3 * (QT - 2 + qt) + 1)
                x1b_load(qt)
            for qt in range(QT):
                sy.wait_ge(s_dveF.sem, VOG * qt + VOG)
                s_outF[qt % 2].dma(sy.dma_start(
                    out=out_d.ap()[qt * P:(qt + 1) * P, :],
                    in_=outs_sb[:, (qt % 2) * D:(qt % 2) * D + D]))
                if qt + 2 < QT:
                    x1b_load(qt + 2)
            for s in s_outF:
                if s.n:
                    sy.wait_ge(s.sem, s.n)
        block.sync(p4b_sync)

        def p4_gps_end(g):
            for s in s_outF:
                if s.n:
                    g.wait_ge(s.sem, s.n)
        block.gpsimd(p4_gps_end)

        p4.close()

    return nc, dict(D=D, S=S, T=T)


def _host_prep(D, S, x, qkvw, qkvb, ow, ob, A, B, Wr):
    KC = D // P
    OCH = 2 * D // P
    T = S // NCORES
    x2 = np.ascontiguousarray(np.asarray(x, dtype=np.float32).reshape(S, D))
    qkvw = np.asarray(qkvw, dtype=np.float32)
    w_qk = np.ascontiguousarray(
        qkvw[:, :2 * D].reshape(KC, P, OCH, P).transpose(2, 1, 0, 3).reshape(OCH, P, D))
    w_v = np.ascontiguousarray(qkvw[:, 2 * D:].reshape(KC, P, D))
    qkvb = np.asarray(qkvb, dtype=np.float32)
    b_qk = np.ascontiguousarray(qkvb[:2 * D].reshape(OCH, P).T)
    b_v = np.ascontiguousarray(qkvb[2 * D:].reshape(1, D))
    w_o = np.ascontiguousarray(np.asarray(ow, dtype=np.float32).reshape(KC, P, D))
    b_o = np.ascontiguousarray(np.asarray(ob, dtype=np.float32).reshape(1, D))
    A = np.asarray(A, dtype=np.float32)
    w_a = np.ascontiguousarray(A.transpose(2, 0, 1).reshape(KC, P, ER))
    B = np.asarray(B, dtype=np.float32)
    w_b = np.ascontiguousarray(B.transpose(0, 2, 1).reshape(ER, D))
    Wr = np.asarray(Wr, dtype=np.float32)
    w_r = np.ascontiguousarray(Wr.T.reshape(KC, P, NEXP))
    ident = np.eye(P, dtype=np.float32)
    ind4 = np.zeros((NEXP, ER), dtype=np.float32)
    for e in range(NEXP):
        ind4[e, e * LORA_R:(e + 1) * LORA_R] = LORA_SCALE
    shared = dict(w_qk=w_qk, w_v=w_v, w_o=w_o, b_qk=b_qk, b_v=b_v, b_o=b_o,
                  w_a=w_a, w_b=w_b, w_r=w_r, ident=ident, ind4=ind4)
    in_maps = []
    for c in range(NCORES):
        m = dict(shared)
        m["x"] = np.ascontiguousarray(x2[c * T:(c + 1) * T, :])
        in_maps.append(m)
    return in_maps


def kernel(L0_attn_x, L0_attn_attn_qkvw, L0_attn_attn_qkvb, L0_attn_attn_ow,
           L0_attn_attn_ob, L0_ffn_A, L0_ffn_B, L0_ffn_Wr):
    D = 4096
    S = 8192
    in_maps = _host_prep(D, S, L0_attn_x, L0_attn_attn_qkvw, L0_attn_attn_qkvb,
                         L0_attn_attn_ow, L0_attn_attn_ob, L0_ffn_A, L0_ffn_B,
                         L0_ffn_Wr)
    nc, meta = build_program(D, S)
    res = run_bass_kernel_spmd(nc, in_maps, core_ids=list(range(NCORES)))
    out = np.concatenate([np.asarray(r["out"], dtype=np.float32)
                          for r in res.results], axis=0)
    return out.reshape(1, S, D)


# version 13
# speedup vs baseline: 1.3979x; 1.0066x over previous
import math
from contextlib import ExitStack

import numpy as np
import concourse.bass as bass
import concourse.mybir as mybir
from concourse.bass_utils import run_bass_kernel_spmd

F32 = mybir.dt.float32
BF16 = mybir.dt.bfloat16
AF = mybir.ActivationFunctionType
ALU = mybir.AluOpType
AX = mybir.AxisListType

NCORES = 8
LORA_SCALE = 4.0
NEXP = 4
LORA_R = 16
ER = NEXP * LORA_R
EPS = 1e-5
P = 128


class Sem:

    def __init__(self, ctx, nc, name, registry):
        self.sem = ctx.enter_context(nc.semaphore(name))
        self.n = 0
        registry.append(self)

    def inc(self, instr, k=1):
        instr.then_inc(self.sem, k)
        self.n += k
        return self.n

    def dma(self, instr):
        return self.inc(instr, 16)

    def wait(self, eng, val=None):
        eng.wait_ge(self.sem, self.n if val is None else val)


def build_program(D, S):
    T = S // NCORES
    KC = D // P
    HEADS = D // P
    OCH = 2 * D // P
    VOG = D // 512
    QT = T // P
    NQG = T // 512
    KT = S // P
    NPAIR = KT // 2
    NG = KC // 8
    NQTG = QT // 4
    HPV = 4
    assert T % 512 == 0 and D % 1024 == 0 and QT % 4 == 0

    nc = bass.Bass("TRN2", target_bir_lowering=False)

    x_d = nc.dram_tensor("x", [T, D], F32, kind="ExternalInput")
    wqk_d = nc.dram_tensor("w_qk", [OCH, P, D], F32, kind="ExternalInput")
    wv_d = nc.dram_tensor("w_v", [KC, P, D], F32, kind="ExternalInput")
    wo_d = nc.dram_tensor("w_o", [KC, P, D], F32, kind="ExternalInput")
    bqk_d = nc.dram_tensor("b_qk", [P, OCH], F32, kind="ExternalInput")
    bv_d = nc.dram_tensor("b_v", [1, D], F32, kind="ExternalInput")
    bo_d = nc.dram_tensor("b_o", [1, D], F32, kind="ExternalInput")
    wa_d = nc.dram_tensor("w_a", [KC, P, ER], F32, kind="ExternalInput")
    wb_d = nc.dram_tensor("w_b", [ER, D], F32, kind="ExternalInput")
    wr_d = nc.dram_tensor("w_r", [KC, P, NEXP], F32, kind="ExternalInput")
    id_d = nc.dram_tensor("ident", [P, P], F32, kind="ExternalInput")
    ind_d = nc.dram_tensor("ind4", [NEXP, ER], F32, kind="ExternalInput")

    out_d = nc.dram_tensor("out", [T, D], F32, kind="ExternalOutput")

    cc_in = nc.dram_tensor("cc_in", [2 * D, T], BF16, kind="Internal")
    cck_out = nc.dram_tensor("cck_out", [NCORES * D, T], BF16,
                             kind="Internal", addr_space="Shared")
    ccv_out = nc.dram_tensor("ccv_out", [NCORES * D, T], BF16,
                             kind="Internal", addr_space="Shared")
    qs_d = nc.dram_tensor("qs", [D, T], BF16, kind="Internal")
    x1_d = nc.dram_tensor("x1", [T, D], F32, kind="Internal")

    gctx = ExitStack()
    REG = []

    def sb(name, shape, dtype):
        return gctx.enter_context(nc.sbuf_tensor(name, shape, dtype))

    ident_bf = sb("ident_bf", [P, P], BF16)
    ident_f = sb("ident_f", [P, P], F32)
    bqk_sb = sb("bqk_sb", [P, OCH], F32)
    bv_sb = sb("bv_sb", [1, D], BF16)
    bo_sb = sb("bo_sb", [1, D], BF16)
    wa_sb = sb("wa_sb", [P, KC * ER], BF16)
    wb_sb = sb("wb_sb", [ER, D], BF16)
    wr_sb = sb("wr_sb", [P, KC * NEXP], BF16)
    ind_sb = sb("ind_sb", [NEXP, ER], F32)
    ones_c = sb("ones_c", [P, P], BF16)
    ones_r = sb("ones_r", [1, P], BF16)
    ones_rf = sb("ones_rf", [1, P], F32)
    dump = sb("dump", [P, D], BF16)
    st = sb("st", [P, 32], F32)

    def mks(name):
        return Sem(gctx, nc, name, REG)

    def mkslots(name, k):
        return [mks(f"{name}{j}") for j in range(k)]

    s_small = mks("s_small")
    s_x = mkslots("s_x", 2)
    s_wq = mkslots("s_wq", 2)
    s_wv = mkslots("s_wv", 2)
    s_wo = mkslots("s_wo", 4)
    s_kv = mkslots("s_kv", 2)
    s_x1 = mkslots("s_x1", 2)
    s_x1b = mkslots("s_x1b", 2)
    s_xr = mkslots("s_xr", 4)
    s_cc = mks("s_cc")
    s_dveM = mks("s_dveM")
    s_dveLN = mks("s_dveLN")
    s_actLN = mks("s_actLN")
    s_peTR = mks("s_peTR")
    s_dveTR = mks("s_dveTR")
    s_peQK = mks("s_peQK")
    s_actQK = mks("s_actQK")
    s_stQK = mkslots("s_stQK", 4)
    s_peV = mks("s_peV")
    s_peV2 = mks("s_peV2")
    s_dveV = mks("s_dveV")
    s_stV = mkslots("s_stV", 2)
    s_peS = mks("s_peS")
    s_peP = mks("s_peP")
    s_peZ4 = mks("s_peZ4")
    s_dveQ = mks("s_dveQ")
    s_dveN = mks("s_dveN")
    s_actE = mks("s_actE")
    s_peH = mks("s_peH")
    s_dveZ = mks("s_dveZ")
    s_peZB = mks("s_peZB")
    s_dveC = mks("s_dveC")
    s_peO = mks("s_peO")
    s_dveX = mks("s_dveX")
    s_stX1 = mkslots("s_stX1", 4)
    s_dveL2 = mks("s_dveL2")
    s_actL2 = mks("s_actL2")
    s_peTR2 = mks("s_peTR2")
    s_dveTR2 = mks("s_dveTR2")
    s_peLG = mks("s_peLG")
    s_peMID = mks("s_peMID")
    s_peGT = mks("s_peGT")
    s_peGB = mks("s_peGB")
    s_peD = mks("s_peD")
    s_dveG = mks("s_dveG")
    s_actG = mks("s_actG")
    s_dveE2 = mks("s_dveE2")
    s_dveF = mks("s_dveF")
    s_outF = mkslots("s_outF", 2)

    block = gctx.enter_context(nc.Block())

    def barrier(eng, snap=None):
        vals = snap if snap is not None else [s.n for s in REG]
        for s, v_ in zip(REG, vals):
            if v_:
                eng.wait_ge(s.sem, v_)

    SC = float(1.0 / math.sqrt(P))

    def chain(eng, sem, instr_fn):
        if sem.n:
            eng.wait_ge(sem.sem, sem.n)
        return sem.inc(instr_fn())

    with gctx:
        def p0_gpsimd(g):
            s_small.dma(g.dma_start(out=ident_bf[:], in_=id_d.ap()))
            s_small.dma(g.dma_start(out=ident_f[:], in_=id_d.ap()))
            s_small.dma(g.dma_start(out=bqk_sb[:], in_=bqk_d.ap()))
            s_small.dma(g.dma_start(out=bv_sb[:], in_=bv_d.ap()))
            s_small.dma(g.dma_start(out=bo_sb[:], in_=bo_d.ap()))
            s_small.dma(g.dma_start(
                out=wa_sb.ap().rearrange("p (kc e) -> p kc e", kc=KC),
                in_=wa_d.ap().rearrange("kc p e -> p kc e")))
            s_small.dma(g.dma_start(out=wb_sb[:], in_=wb_d.ap()))
            s_small.dma(g.dma_start(
                out=wr_sb.ap().rearrange("p (kc e) -> p kc e", kc=KC),
                in_=wr_d.ap().rearrange("kc p e -> p kc e")))
            s_small.dma(g.dma_start(out=ind_sb[:], in_=ind_d.ap()))
        block.gpsimd(p0_gpsimd)

        def p0_vector(v):
            v.memset(ones_c[:], 1.0)
            v.memset(ones_r[:], 1.0)
            s_dveM.inc(v.memset(ones_rf[:], 1.0))
        block.vector(p0_vector)

        p1 = ExitStack()
        x_sb = p1.enter_context(nc.sbuf_tensor("x_sb", [P, 2 * D], F32))
        ln_tm = p1.enter_context(nc.sbuf_tensor("ln_tm", [P, D], BF16))
        ln_fm = p1.enter_context(nc.sbuf_tensor("ln_fm", [P, KC * T], BF16))
        wqk_sb = p1.enter_context(nc.sbuf_tensor("wqk_sb", [P, 2 * D], BF16))
        stage_qk = p1.enter_context(nc.sbuf_tensor("stage_qk", [P, 4 * 1024], BF16))
        wv_sb = p1.enter_context(nc.sbuf_tensor("wv_sb", [P, KC // 2 * 2 * 512], BF16))
        stage_v = p1.enter_context(nc.sbuf_tensor("stage_v", [P, 2 * 4 * 512], BF16))
        qkps = [p1.enter_context(nc.psum_tensor(f"qkps{i}", [P, 1024], F32))
                for i in range(2)]
        ps_ln = ExitStack()
        tps_ln = ps_ln.enter_context(nc.psum_tensor("tps_ln", [P, 2 * 1024], BF16))

        ln_fm_r = ln_fm.ap().rearrange("p (kc t) -> p kc t", kc=KC)

        def p1_sync(sy):
            for qt in range(QT):
                if qt >= 2:
                    sy.wait_ge(s_dveLN.sem, 7 * (qt - 2) + 7)
                    sy.wait_ge(s_actLN.sem, 3 * (qt - 2) + 1)
                s_x[qt % 2].dma(sy.dma_start(
                    out=x_sb[:, (qt % 2) * D:(qt % 2) * D + D],
                    in_=x_d.ap()[qt * P:(qt + 1) * P, :]))
        block.sync(p1_sync)

        def emit_ln_vec(v, qt, xap, sem_d, sem_a, s_petr, ln_tm_t, war_lag=2):
            b7 = 7 * qt
            chain(v, sem_d, lambda: v.reduce_sum(out=st[:, 0:1], in_=xap, axis=AX.X))
            chain(v, sem_d, lambda: v.tensor_scalar_mul(st[:, 1:2], st[:, 0:1], 1.0 / D))
            chain(v, sem_d, lambda: v.tensor_tensor(out=st[:, 2:3], in0=st[:, 1:2],
                                                    in1=st[:, 1:2], op=ALU.mult))
            v.wait_ge(sem_a.sem, 3 * qt + 1)
            chain(v, sem_d, lambda: v.tensor_scalar(out=st[:, 3:4], in0=st[:, 4:5],
                                                    scalar1=1.0 / D, scalar2=float(EPS),
                                                    op0=ALU.mult, op1=ALU.add))
            chain(v, sem_d, lambda: v.tensor_tensor(out=st[:, 5:6], in0=st[:, 3:4],
                                                    in1=st[:, 2:3], op=ALU.subtract))
            v.wait_ge(sem_a.sem, 3 * qt + 3)
            chain(v, sem_d, lambda: v.tensor_scalar(out=st[:, 7:8], in0=st[:, 1:2],
                                                    scalar1=st[:, 6:7], scalar2=-1.0,
                                                    op0=ALU.mult, op1=ALU.mult))
            if qt >= war_lag:
                v.wait_ge(s_petr.sem, NG * (qt - war_lag) + NG)
            nb = (qt % 2) * D if war_lag == 2 else 0
            chain(v, sem_d, lambda: v.tensor_scalar(
                out=ln_tm_t[:, nb:nb + D], in0=xap,
                scalar1=st[:, 6:7], scalar2=st[:, 7:8],
                op0=ALU.mult, op1=ALU.add))
            assert sem_d.n == b7 + 7

        def emit_ln_act(sc, qt, xap, sem_d, sem_a, wait_x):
            wait_x()
            if qt >= 1:
                sc.wait_ge(sem_d.sem, 7 * (qt - 1) + 4)
            chain(sc, sem_a, lambda: sc.activation(out=dump[:], in_=xap,
                                                   func=AF.Square,
                                                   accum_out=st[:, 4:5]))
            sc.wait_ge(sem_d.sem, 7 * qt + 5)
            chain(sc, sem_a, lambda: sc.activation(out=st[:, 8:9], in_=st[:, 5:6],
                                                   func=AF.Ln))
            chain(sc, sem_a, lambda: sc.activation(out=st[:, 6:7], in_=st[:, 8:9],
                                                   func=AF.Exp, scale=-0.5))
            assert sem_a.n == 3 * qt + 3

        def p1_vec_ln(v):
            for qt in range(QT):
                v.wait_ge(s_x[qt % 2].sem, 16 * (qt // 2 + 1))
                emit_ln_vec(v, qt, x_sb[:, (qt % 2) * D:(qt % 2) * D + D],
                            s_dveLN, s_actLN, s_peTR, ln_tm, war_lag=1)
                for g in range(NG):
                    G = NG * qt + g
                    v.wait_ge(s_peTR.sem, G + 1)
                    if s_dveTR.n:
                        v.wait_ge(s_dveTR.sem, s_dveTR.n)
                    s_dveTR.inc(v.tensor_copy(
                        out=ln_fm_r[:, g * 8:(g + 1) * 8, qt * P:(qt + 1) * P],
                        in_=tps_ln.ap()[:, (G % 2) * 1024:(G % 2) * 1024 + 1024]
                        .rearrange("p (kc t) -> p kc t", kc=8)))
        block.vector(p1_vec_ln)

        def p1_scalar_ln(sc):
            for qt in range(QT):
                emit_ln_act(sc, qt, x_sb[:, (qt % 2) * D:(qt % 2) * D + D],
                            s_dveLN, s_actLN,
                            lambda qt=qt: sc.wait_ge(s_x[qt % 2].sem,
                                                     16 * (qt // 2 + 1)))
        block.scalar(p1_scalar_ln)

        def p1_pe_tr(t):
            s_small.wait(t)
            t.wait_ge(s_dveM.sem, 1)
            for qt in range(QT):
                t.wait_ge(s_dveLN.sem, 7 * qt + 7)
                for g in range(NG):
                    G = NG * qt + g
                    if G >= 2:
                        t.wait_ge(s_dveTR.sem, G - 1)
                    for j in range(8):
                        kc = g * 8 + j
                        mm = t.transpose(
                            tps_ln[:, (G % 2) * 1024 + j * P:(G % 2) * 1024 + (j + 1) * P],
                            in_=ln_tm[:, kc * P:(kc + 1) * P],
                            identity=ident_bf[:])
                    s_peTR.inc(mm)
        block.tensor(p1_pe_tr)

        order = list(range(HEADS, OCH)) + list(range(0, HEADS))
        NKO = HEADS

        def emit_wqk_loads(g, i0, i1):
            for i in range(i0, i1):
                if i >= 2:
                    g.wait_ge(s_peQK.sem, i - 1)
                s_wq[i % 2].dma(g.dma_start(
                    out=wqk_sb[:, (i % 2) * D:(i % 2) * D + D],
                    in_=wqk_d.ap()[order[i]]))

        def emit_qk_compute(t, i0, i1):
            for i in range(i0, i1):
                t.wait_ge(s_wq[i % 2].sem, 16 * (i // 2 + 1))
                if i == 0:
                    t.wait_ge(s_dveTR.sem, NG * QT)
                if i >= 2:
                    t.wait_ge(s_actQK.sem, i - 1)
                for kc in range(KC):
                    for qg in range(NQG):
                        mm = t.matmul(
                            qkps[i % 2][:, qg * 512:(qg + 1) * 512],
                            lhsT=wqk_sb[:, (i % 2) * D + kc * P:(i % 2) * D + (kc + 1) * P],
                            rhs=ln_fm[:, kc * T + qg * 512:kc * T + qg * 512 + 512],
                            start=(kc == 0), stop=(kc == KC - 1))
                s_peQK.inc(mm)

        def emit_qk_evac(sc, i0, i1):
            for i in range(i0, i1):
                if i == 0:
                    s_small.wait(sc)
                sc.wait_ge(s_peQK.sem, i + 1)
                if i >= 4:
                    sc.wait_ge(s_stQK[i % 4].sem, 16 * (i // 4))
                och = order[i]
                chain(sc, s_actQK, lambda i=i, och=och: sc.activation(
                    out=stage_qk[:, (i % 4) * 1024:(i % 4) * 1024 + T],
                    in_=qkps[i % 2][:, 0:T], func=AF.Identity,
                    bias=bqk_sb[:, och:och + 1]))

        def emit_qk_store(sy, i0, i1):
            for i in range(i0, i1):
                sy.wait_ge(s_actQK.sem, i + 1)
                och = order[i]
                if och >= HEADS:
                    hk = och - HEADS
                    dst = cc_in.ap()[hk * P:(hk + 1) * P, :]
                else:
                    dst = qs_d.ap()[och * P:(och + 1) * P, :]
                s_stQK[i % 4].dma(sy.dma_start(
                    out=dst, in_=stage_qk[:, (i % 4) * 1024:(i % 4) * 1024 + T]))

        block.gpsimd(lambda g: emit_wqk_loads(g, 0, NKO))

        def p1_gps_cck(g):
            for s in s_stQK:
                if s.n:
                    g.wait_ge(s.sem, s.n)
            s_cc.inc(g.collective_compute(
                "AllGather", ALU.bypass,
                replica_groups=[list(range(NCORES))],
                ins=[cc_in.ap()[0:D, :]], outs=[cck_out.ap()]))

        block.tensor(lambda t: emit_qk_compute(t, 0, NKO))
        block.scalar(lambda sc: emit_qk_evac(sc, 0, NKO))
        block.sync(lambda sy: emit_qk_store(sy, 0, NKO))

        ps_ln.close()
        vps = [p1.enter_context(nc.psum_tensor(f"vps{i}", [P, 512], F32))
               for i in range(4)]
        KH = KC // 2

        def p1_gps_wv(g, v0, v1):
            for vog in range(v0, v1):
                for half in range(2):
                    if vog >= 1:
                        g.wait_ge(s_peV.sem,
                                  2 * NQTG * (vog - 1) + 2 * (NQTG - 1) + half + 1)
                    kc0 = half * KH
                    s_wv[half].dma(g.dma_start(
                        out=wv_sb.ap()[:, half * KH * 512:(half + 1) * KH * 512]
                        .rearrange("p (kc c) -> p kc c", kc=KH),
                        in_=wv_d.ap()[kc0:kc0 + KH, :, vog * 512:(vog + 1) * 512]
                        .rearrange("kc p c -> p kc c")))
        block.gpsimd(lambda g: p1_gps_wv(g, 0, 1))
        block.gpsimd(p1_gps_cck)
        block.gpsimd(lambda g: p1_gps_wv(g, 1, VOG))

        def p1_pe_v(t):
            for vog in range(VOG):
                for qtg in range(NQTG):
                    gi = NQTG * vog + qtg
                    if gi >= 1:
                        t.wait_ge(s_dveV.sem, 4 * gi)
                    for half in range(2):
                        t.wait_ge(s_wv[half].sem, 16 * (vog + 1))
                        for kcl in range(KH):
                            kc = half * KH + kcl
                            for q4 in range(4):
                                qt = qtg * 4 + q4
                                mm = t.matmul(
                                    vps[q4][:],
                                    lhsT=ln_fm[:, kc * T + qt * P:kc * T + (qt + 1) * P],
                                    rhs=wv_sb[:, (half * KH + kcl) * 512:
                                              (half * KH + kcl) * 512 + 512],
                                    start=(kc == 0), stop=False)
                        s_peV.inc(mm)
                    for q4 in range(4):
                        mm = t.matmul(vps[q4][:], lhsT=ones_r[:],
                                      rhs=bv_sb[0:1, vog * 512:(vog + 1) * 512],
                                      start=False, stop=True)
                        s_peV2.inc(mm)
        block.tensor(p1_pe_v)

        def p1_vec_v(v):
            for vog in range(VOG):
                for qtg in range(NQTG):
                    gi = NQTG * vog + qtg
                    for q4 in range(4):
                        v.wait_ge(s_peV2.sem, 4 * gi + q4 + 1)
                        if gi >= 2:
                            v.wait_ge(s_stV[gi % 2].sem, 16 * HPV * (gi // 2))
                        if s_dveV.n:
                            v.wait_ge(s_dveV.sem, s_dveV.n)
                        s_dveV.inc(v.tensor_copy(
                            out=stage_v[:, (gi % 2) * 2048 + q4 * 512:
                                        (gi % 2) * 2048 + (q4 + 1) * 512],
                            in_=vps[q4][:]))
        block.vector(p1_vec_v)

        def p1_sync_v(sy):
            for vog in range(VOG):
                for qtg in range(NQTG):
                    gi = NQTG * vog + qtg
                    sy.wait_ge(s_dveV.sem, 4 * gi + 4)
                    for j in range(HPV):
                        hv = vog * HPV + j
                        dst = cc_in.ap()[D + hv * P:D + (hv + 1) * P,
                                         qtg * 4 * P:(qtg + 1) * 4 * P] \
                            .rearrange("p (qt c) -> p qt c", qt=4)
                        src = stage_v.ap()[:, (gi % 2) * 2048:(gi % 2) * 2048 + 2048] \
                            .rearrange("p (qt c) -> p qt c", qt=4)[:, :, j * P:(j + 1) * P]
                        s_stV[gi % 2].dma(sy.dma_start(out=dst, in_=src))
        block.sync(p1_sync_v)

        def p1_gps_ccv(g):
            for s in s_stV:
                if s.n:
                    g.wait_ge(s.sem, s.n)
            s_cc.inc(g.collective_compute(
                "AllGather", ALU.bypass,
                replica_groups=[list(range(NCORES))],
                ins=[cc_in.ap()[D:2 * D, :]], outs=[ccv_out.ap()]))
        block.gpsimd(lambda g: emit_wqk_loads(g, NKO, NKO + 3))
        block.gpsimd(p1_gps_ccv)

        block.gpsimd(lambda g: emit_wqk_loads(g, NKO + 3, OCH))
        block.tensor(lambda t: emit_qk_compute(t, NKO, OCH))
        block.scalar(lambda sc: emit_qk_evac(sc, NKO, OCH))
        block.sync(lambda sy: emit_qk_store(sy, NKO, OCH))

        p1.close()

        p2 = ExitStack()
        ctx_sb = p2.enter_context(nc.sbuf_tensor("ctx_sb", [P, HEADS * T], BF16))
        sps = [p2.enter_context(nc.psum_tensor(f"sps{i}", [P, 1024], F32))
               for i in range(NQG)]
        ctxp = [p2.enter_context(nc.psum_tensor(f"ctxp{i}", [P, 512], F32))
                for i in range(NQG)]
        zzp = [p2.enter_context(nc.psum_tensor(f"zzp{i}", [P, 512], F32))
               for i in range(NQG)]
        p2a = ExitStack()
        ksb = p2a.enter_context(nc.sbuf_tensor("ksb", [P, 2 * S], BF16))
        vsb = p2a.enter_context(nc.sbuf_tensor("vsb", [P, 2 * S], BF16))
        qsb = p2a.enter_context(nc.sbuf_tensor("qsb", [P, 2 * T], BF16))
        esb = p2a.enter_context(nc.sbuf_tensor("esb", [P, 8 * 1024], BF16))
        eq_sb = p2a.enter_context(nc.sbuf_tensor("eq_sb", [P, NQG * 1024], BF16))
        e4_sb = p2a.enter_context(nc.sbuf_tensor("e4_sb", [P, NQG * 1024], BF16))
        zb_sb = p2a.enter_context(nc.sbuf_tensor("zb_sb", [P, NQG * 512], F32))

        NL = 1 + 2 * NCORES
        snap2 = [s.n for s in REG]

        def p2_sync(sy):
            barrier(sy, snap2)
            sy.wait_ge(s_cc.sem, 2)
            for h in range(HEADS):
                if h >= 2:
                    sy.wait_ge(s_peH.sem, h - 1)
                hb = h % 2
                s_kv[hb].dma(sy.dma_start(out=qsb[:, hb * T:hb * T + T],
                                          in_=qs_d.ap()[h * P:(h + 1) * P, :]))
                for r in range(NCORES):
                    s_kv[hb].dma(sy.dma_start(
                        out=ksb[:, hb * S + r * T:hb * S + (r + 1) * T],
                        in_=cck_out.ap()[r * D + h * P:r * D + (h + 1) * P, :]))
                for r in range(NCORES):
                    s_kv[hb].dma(sy.dma_start(
                        out=vsb.ap()[:, hb * S + r * T:hb * S + (r + 1) * T]
                        .rearrange("p (qt c) -> p qt c", qt=QT),
                        in_=ccv_out.ap()[r * D + h * P:r * D + (h + 1) * P, :]
                        .rearrange("p (qt c) -> p qt c", qt=QT)))
        block.sync(p2_sync)

        NQUAD = KT // 4
        QGS = list(range(NQG))

        def p2_tensor(t):
            barrier(t, snap2)
            for h in range(HEADS):
                hb = h % 2
                t.wait_ge(s_kv[h % 2].sem, 16 * NL * (h // 2 + 1))
                bE = NQG * NPAIR * h
                bP = NQG * NPAIR * h
                bZ = NQG * NQUAD * h
                if h >= 1:
                    t.wait_ge(s_actE.sem, bE)
                for p in range(NPAIR):
                    for qg in QGS:
                        if p >= 1:
                            t.wait_ge(s_actE.sem, bE + NQG * (p - 1) + qg + 1)
                        for j in range(2):
                            kt = 2 * p + j
                            mm = t.matmul(
                                sps[qg][:, j * 512:(j + 1) * 512],
                                lhsT=ksb[:, hb * S + kt * P:hb * S + (kt + 1) * P],
                                rhs=qsb[:, hb * T + qg * 512:hb * T + qg * 512 + 512],
                                start=True, stop=True)
                        s_peS.inc(mm)
                    if p >= 1:
                        pv = p - 1
                        for qg in QGS:
                            if pv == 0:
                                if h >= 1:
                                    t.wait_ge(s_dveC.sem, NQG * (h - 1) + qg + 1)
                            sl = (NQG * pv + qg) % 8
                            for j in range(2):
                                kt = 2 * pv + j
                                mm = t.matmul(
                                    ctxp[qg][:],
                                    lhsT=vsb[:, hb * S + kt * P:hb * S + (kt + 1) * P],
                                    rhs=esb[:, sl * 1024 + j * 512:sl * 1024 + (j + 1) * 512],
                                    start=(kt == 0), stop=False)
                            s_peP.inc(mm)
                            if pv % 2 == 1:
                                q = pv // 2
                                t.wait_ge(s_dveQ.sem, 3 * NQG * NQUAD * h + 3 * NQG * q + 3 * qg + 3)
                                if q == 0 and h >= 1:
                                    t.wait_ge(s_dveZ.sem, NQG * (h - 1) + qg + 1)
                                s_peZ4.inc(t.matmul(
                                    zzp[qg][:], lhsT=ones_c[:],
                                    rhs=e4_sb[:, qg * 1024 + (q % 2) * 512:
                                              qg * 1024 + (q % 2) * 512 + 512],
                                    start=(q == 0), stop=(q == NQUAD - 1)))
                pv = NPAIR - 1
                for qg in QGS:
                    t.wait_ge(s_actE.sem, bE + NQG * pv + qg + 1)
                    sl = (NQG * pv + qg) % 8
                    for j in range(2):
                        kt = 2 * pv + j
                        mm = t.matmul(
                            ctxp[qg][:],
                            lhsT=vsb[:, hb * S + kt * P:hb * S + (kt + 1) * P],
                            rhs=esb[:, sl * 1024 + j * 512:sl * 1024 + (j + 1) * 512],
                            start=False, stop=(j == 1))
                    s_peP.inc(mm)
                    q = pv // 2
                    t.wait_ge(s_dveQ.sem, 3 * NQG * NQUAD * h + 3 * NQG * q + 3 * qg + 3)
                    s_peZ4.inc(t.matmul(
                        zzp[qg][:], lhsT=ones_c[:],
                        rhs=e4_sb[:, qg * 1024 + (q % 2) * 512:
                                  qg * 1024 + (q % 2) * 512 + 512],
                        start=False, stop=True))
                t.wait_ge(s_peP.sem, bP + NQG * NPAIR)
                t.sem_inc(s_peH.sem, 1)
                s_peH.n += 1
        block.tensor(p2_tensor)

        def p2_scalar(sc):
            barrier(sc, snap2)
            for h in range(HEADS):
                bS = NQG * NPAIR * h
                bP = NQG * NPAIR * h
                bQ = 3 * NQG * NQUAD * h
                for p in range(NPAIR):
                    for qg in QGS:
                        e = NQG * p + qg
                        sc.wait_ge(s_peS.sem, bS + e + 1)
                        if p >= 8 // NQG:
                            pp = p - 8 // NQG
                            sc.wait_ge(s_peP.sem, bP + NQG * pp + qg + 1)
                            sc.wait_ge(s_dveQ.sem,
                                       bQ + 3 * NQG * (pp // 2) + 3 * qg + (pp % 2) + 1)
                        elif h >= 1:
                            pp = p + NPAIR - 8 // NQG
                            bPp = NQG * NPAIR * (h - 1)
                            bQp = 3 * NQG * NQUAD * (h - 1)
                            sc.wait_ge(s_peP.sem, bPp + NQG * pp + qg + 1)
                            sc.wait_ge(s_dveQ.sem,
                                       bQp + 3 * NQG * (pp // 2) + 3 * qg + (pp % 2) + 1)
                        chain(sc, s_actE, lambda qg=qg, e=e: sc.activation(
                            out=esb[:, (e % 8) * 1024:(e % 8) * 1024 + 1024],
                            in_=sps[qg][:], func=AF.Exp, scale=SC))
        block.scalar(p2_scalar)

        def p2_vector(v):
            barrier(v, snap2)
            for h in range(HEADS):
                bE = NQG * NPAIR * h
                bZ = NQG * NQUAD * h
                for q in range(NQUAD):
                    for qg in QGS:
                        p0, p1 = 2 * q, 2 * q + 1
                        e0 = (NQG * p0 + qg) % 8
                        e1 = (NQG * p1 + qg) % 8
                        v.wait_ge(s_actE.sem, bE + NQG * p0 + qg + 1)
                        war = bZ + NQG * (q - 2) + qg + 1
                        if war > 0:
                            v.wait_ge(s_peZ4.sem, war)
                        chain(v, s_dveQ, lambda qg=qg, e0=e0: v.tensor_add(
                            out=eq_sb[:, qg * 1024:qg * 1024 + 512],
                            in0=esb[:, e0 * 1024:e0 * 1024 + 512],
                            in1=esb[:, e0 * 1024 + 512:e0 * 1024 + 1024]))
                        v.wait_ge(s_actE.sem, bE + NQG * p1 + qg + 1)
                        chain(v, s_dveQ, lambda qg=qg, e1=e1: v.tensor_add(
                            out=eq_sb[:, qg * 1024 + 512:qg * 1024 + 1024],
                            in0=esb[:, e1 * 1024:e1 * 1024 + 512],
                            in1=esb[:, e1 * 1024 + 512:e1 * 1024 + 1024]))
                        chain(v, s_dveQ, lambda qg=qg, q=q: v.tensor_add(
                            out=e4_sb[:, qg * 1024 + (q % 2) * 512:
                                      qg * 1024 + (q % 2) * 512 + 512],
                            in0=eq_sb[:, qg * 1024:qg * 1024 + 512],
                            in1=eq_sb[:, qg * 1024 + 512:qg * 1024 + 1024]))
                for qg in QGS:
                    v.wait_ge(s_peZ4.sem, bZ + NQG * (NQUAD - 1) + qg + 1)
                    if s_dveZ.n:
                        v.wait_ge(s_dveZ.sem, s_dveZ.n)
                    s_dveZ.inc(v.reciprocal(
                        out=zb_sb[:, qg * 512:(qg + 1) * 512], in_=zzp[qg][:]))
                for qg in QGS:
                    v.wait_ge(s_peP.sem, NQG * NPAIR * h + NQG * (NPAIR - 1) + qg + 1)
                    chain(v, s_dveC, lambda h=h, qg=qg: v.tensor_copy(
                        out=ctx_sb[:, h * T + qg * 512:h * T + qg * 512 + 512],
                        in_=ctxp[qg][:]))
                for qg in QGS:
                    v.wait_ge(s_dveZ.sem, NQG * h + qg + 1)
                    v.wait_ge(s_dveC.sem, NQG * h + qg + 1)
                    chain(v, s_dveN, lambda qg=qg, h=h: v.tensor_tensor(
                        out=ctx_sb[:, h * T + qg * 512:h * T + qg * 512 + 512],
                        in0=ctx_sb[:, h * T + qg * 512:h * T + qg * 512 + 512],
                        in1=zb_sb[:, qg * 512:(qg + 1) * 512], op=ALU.mult))
        block.vector(p2_vector)

        p2a.close()
        wo_sb = p2.enter_context(nc.sbuf_tensor("wo_sb", [P, 2 * KH * 2 * 512], BF16))
        xr_sb = p2.enter_context(nc.sbuf_tensor("xr_sb", [P, 4 * 512], F32))
        x1s_sb = p2.enter_context(nc.sbuf_tensor("x1s_sb", [P, 4 * 512], F32))
        ops_extra = [p2.enter_context(nc.psum_tensor(f"ops_extra{i}", [P, 512], F32))
                     for i in range(4 * (2 - min(NQG, 2)))]
        if NQG >= 2:
            ops = [sps[0].ap()[:, 0:512], sps[0].ap()[:, 512:1024],
                   sps[1].ap()[:, 0:512], sps[1].ap()[:, 512:1024],
                   ctxp[0].ap()[:], ctxp[1].ap()[:], zzp[0].ap()[:], zzp[1].ap()[:]]
        else:
            ops = [sps[0].ap()[:, 0:512], sps[0].ap()[:, 512:1024],
                   ctxp[0].ap()[:], zzp[0].ap()[:]] + [o.ap()[:] for o in ops_extra]

        snap3 = [s.n for s in REG]

        def p3_gps(g):
            barrier(g, snap3)
            for og in range(VOG):
                for half in range(2):
                    if og >= 2:
                        g.wait_ge(s_peO.sem, QT * (og - 1))
                    kc0 = half * KH
                    base = (og % 2) * 2 * KH * 512
                    s_wo[(og % 2) * 2 + half].dma(g.dma_start(
                        out=wo_sb.ap()[:, base + half * KH * 512:base + (half + 1) * KH * 512]
                        .rearrange("p (kc c) -> p kc c", kc=KH),
                        in_=wo_d.ap()[kc0:kc0 + KH, :, og * 512:(og + 1) * 512]
                        .rearrange("kc p c -> p kc c")))
        block.gpsimd(p3_gps)

        def p3_sync(sy):
            barrier(sy, snap3)
            NI = VOG * QT

            def xr_load(i):
                og, qt = divmod(i, QT)
                s_xr[i % 4].dma(sy.dma_start(
                    out=xr_sb[:, (i % 4) * 512:(i % 4) * 512 + 512],
                    in_=x_d.ap()[qt * P:(qt + 1) * P, og * 512:(og + 1) * 512]))

            for i in range(min(4, NI)):
                xr_load(i)
            for i in range(NI):
                og, qt = divmod(i, QT)
                sy.wait_ge(s_dveX.sem, i + 1)
                s_stX1[i % 4].dma(sy.dma_start(
                    out=x1_d.ap()[qt * P:(qt + 1) * P, og * 512:(og + 1) * 512],
                    in_=x1s_sb[:, (i % 4) * 512:(i % 4) * 512 + 512]))
                if i + 4 < NI:
                    xr_load(i + 4)
        block.sync(p3_sync)

        def p3_tensor(t):
            barrier(t, snap3)
            for og in range(VOG):
                if og >= 1:
                    t.wait_ge(s_dveX.sem, QT * og)
                for half in range(2):
                    t.wait_ge(s_wo[(og % 2) * 2 + half].sem, 16 * (og // 2 + 1))
                    base = (og % 2) * 2 * KH * 512
                    for kcl in range(KH):
                        hc = half * KH + kcl
                        for qt in range(QT):
                            t.matmul(ops[qt],
                                     lhsT=ctx_sb[:, hc * T + qt * P:hc * T + (qt + 1) * P],
                                     rhs=wo_sb[:, base + (half * KH + kcl) * 512:
                                               base + (half * KH + kcl) * 512 + 512],
                                     start=(hc == 0), stop=False)
                for qt in range(QT):
                    mm = t.matmul(ops[qt], lhsT=ones_r[:],
                                  rhs=bo_sb[0:1, og * 512:(og + 1) * 512],
                                  start=False, stop=True)
                    s_peO.inc(mm)
        block.tensor(p3_tensor)

        def p3_vector(v):
            barrier(v, snap3)
            for og in range(VOG):
                for qt in range(QT):
                    i = og * QT + qt
                    v.wait_ge(s_peO.sem, i + 1)
                    v.wait_ge(s_xr[i % 4].sem, 16 * (i // 4 + 1))
                    if i >= 4:
                        v.wait_ge(s_stX1[i % 4].sem, 16 * (i // 4))
                    if s_dveX.n:
                        v.wait_ge(s_dveX.sem, s_dveX.n)
                    s_dveX.inc(v.tensor_tensor(
                        out=x1s_sb[:, (i % 4) * 512:(i % 4) * 512 + 512],
                        in0=ops[qt],
                        in1=xr_sb[:, (i % 4) * 512:(i % 4) * 512 + 512],
                        op=ALU.add))
        block.vector(p3_vector)

        p2.close()

        snap4 = [s.n for s in REG]
        p4 = ExitStack()
        ln2_fm = p4.enter_context(nc.sbuf_tensor("ln2_fm", [P, KC * T], BF16))
        x1_sb = p4.enter_context(nc.sbuf_tensor("x1_sb", [P, 2 * D], F32))
        ln2_tm = p4.enter_context(nc.sbuf_tensor("ln2_tm", [P, 2 * D], BF16))
        outs_sb = p4.enter_context(nc.sbuf_tensor("outs_sb", [P, 2 * D], F32))
        GA = QT * NEXP
        ga_sb = p4.enter_context(nc.sbuf_tensor("ga_sb", [P, GA], F32))
        gl_sb = p4.enter_context(nc.sbuf_tensor("gl_sb", [P, GA], F32))
        gx_sb = p4.enter_context(nc.sbuf_tensor("gx_sb", [P, GA], F32))
        gm_sb = p4.enter_context(nc.sbuf_tensor("gm_sb", [P, 4 * QT], F32))
        grep_sb = p4.enter_context(nc.sbuf_tensor("grep_sb", [P, 2 * GA], F32))
        gt_sb = p4.enter_context(nc.sbuf_tensor("gt_sb", [NEXP, QT * P], F32))
        mid_sb = p4.enter_context(nc.sbuf_tensor("mid_sb", [ER, QT * P], BF16))
        gbs_sb = p4.enter_context(nc.sbuf_tensor("gbs_sb", [ER, QT * P], F32))
        p4ps = ExitStack()
        tps2 = p4ps.enter_context(nc.psum_tensor("tps2", [P, 1024], BF16))
        ln2_fm_r = ln2_fm.ap().rearrange("p (kc t) -> p kc t", kc=KC)

        def p4a_sync(sy):
            barrier(sy, snap4)

            def x1_load(qt):
                s_x1[qt % 2].dma(sy.dma_start(
                    out=x1_sb[:, (qt % 2) * D:(qt % 2) * D + D],
                    in_=x1_d.ap()[qt * P:(qt + 1) * P, :]))

            for qt in range(min(2, QT)):
                x1_load(qt)
            for qt in range(2, QT):
                sy.wait_ge(s_dveL2.sem, 7 * (qt - 2) + 7)
                sy.wait_ge(s_actL2.sem, 3 * (qt - 2) + 1)
                x1_load(qt)
        block.sync(p4a_sync)

        def p4a_vec(v):
            barrier(v, snap4)
            for qt in range(QT):
                v.wait_ge(s_x1[qt % 2].sem, 16 * (qt // 2 + 1))
                emit_ln_vec(v, qt, x1_sb[:, (qt % 2) * D:(qt % 2) * D + D],
                            s_dveL2, s_actL2, s_peTR2, ln2_tm)
                for g in range(NG):
                    G = NG * qt + g
                    v.wait_ge(s_peTR2.sem, G + 1)
                    if s_dveTR2.n:
                        v.wait_ge(s_dveTR2.sem, s_dveTR2.n)
                    s_dveTR2.inc(v.tensor_copy(
                        out=ln2_fm_r[:, g * 8:(g + 1) * 8, qt * P:(qt + 1) * P],
                        in_=tps2.ap()[:, 0:1024]
                        .rearrange("p (kc t) -> p kc t", kc=8)))
        block.vector(p4a_vec)

        def p4a_scalar(sc):
            barrier(sc, snap4)
            for qt in range(QT):
                emit_ln_act(sc, qt, x1_sb[:, (qt % 2) * D:(qt % 2) * D + D],
                            s_dveL2, s_actL2,
                            lambda qt=qt: sc.wait_ge(s_x1[qt % 2].sem,
                                                     16 * (qt // 2 + 1)))
        block.scalar(p4a_scalar)

        def p4a_tensor(t):
            barrier(t, snap4)
            for qt in range(QT):
                t.wait_ge(s_dveL2.sem, 7 * qt + 7)
                for g in range(NG):
                    G = NG * qt + g
                    if G >= 1:
                        t.wait_ge(s_dveTR2.sem, G)
                    for j in range(8):
                        kc = g * 8 + j
                        mm = t.transpose(
                            tps2[:, j * P:(j + 1) * P],
                            in_=ln2_tm[:, (qt % 2) * D + kc * P:(qt % 2) * D + (kc + 1) * P],
                            identity=ident_bf[:])
                    s_peTR2.inc(mm)
        block.tensor(p4a_tensor)

        p4ps.close()
        lgps = p4.enter_context(nc.psum_tensor("lgps", [P, GA], F32))
        mid_all = p4.enter_context(nc.psum_tensor("mid_all", [ER, QT * P], F32))
        gtps = p4.enter_context(nc.psum_tensor("gtps", [NEXP, 2 * P], F32))
        gb_all = p4.enter_context(nc.psum_tensor("gb_all", [ER, QT * P], F32))
        dps = [p4.enter_context(nc.psum_tensor(f"dps{i}", [P, 512], F32))
               for i in range(2)]
        lg3 = lgps.ap().rearrange("p (qt e) -> p qt e", e=NEXP)

        def p4b_tensor(t):
            t.wait_ge(s_dveTR2.sem, NG * QT)
            for qt in range(QT):
                for kc in range(KC):
                    mm = t.matmul(lgps[:, qt * NEXP:(qt + 1) * NEXP],
                                  lhsT=ln2_fm[:, kc * T + qt * P:kc * T + (qt + 1) * P],
                                  rhs=wr_sb[:, kc * NEXP:(kc + 1) * NEXP],
                                  start=(kc == 0), stop=(kc == KC - 1))
            s_peLG.inc(mm)
            for qt in range(QT):
                for kc in range(KC):
                    mm = t.matmul(mid_all[:, qt * P:(qt + 1) * P],
                                  lhsT=wa_sb[:, kc * ER:(kc + 1) * ER],
                                  rhs=ln2_fm[:, kc * T + qt * P:kc * T + (qt + 1) * P],
                                  start=(kc == 0), stop=(kc == KC - 1))
            s_peMID.inc(mm)
            t.wait_ge(s_dveG.sem, 22)
            for qt in range(QT):
                if qt >= 1:
                    t.wait_ge(s_dveE2.sem, qt)
                s_peGT.inc(t.transpose(
                    gtps[:, 0:P],
                    in_=ga_sb[:, qt * NEXP:(qt + 1) * NEXP],
                    identity=ident_f[:]))
            for qt in range(QT):
                t.wait_ge(s_dveE2.sem, qt + 1)
                s_peGB.inc(t.matmul(
                    gb_all[:, qt * P:(qt + 1) * P], lhsT=ind_sb[:],
                    rhs=gt_sb[0:NEXP, qt * P:(qt + 1) * P],
                    start=True, stop=True))
            t.wait_ge(s_dveE2.sem, QT + 2)
            for qt in range(QT):
                for og in range(VOG):
                    i = VOG * qt + og
                    if i >= 2:
                        t.wait_ge(s_dveF.sem, i - 1)
                    s_peD.inc(t.matmul(dps[i % 2][:],
                                       lhsT=mid_sb[:, qt * P:(qt + 1) * P],
                                       rhs=wb_sb[:, og * 512:(og + 1) * 512],
                                       start=True, stop=True))
        block.tensor(p4b_tensor)

        def p4b_scalar(sc):
            sc.wait_ge(s_dveG.sem, 14)
            chain(sc, s_actG, lambda: sc.activation(out=gx_sb[:], in_=gl_sb[:],
                                                    func=AF.Exp))
        block.scalar(p4b_scalar)

        def p4b_vec(v):
            v.wait_ge(s_peLG.sem, 1)
            ch = lambda fn: chain(v, s_dveG, fn)
            ch(lambda: v.reduce_max(out=gm_sb[:, 0:QT], in_=lg3, axis=AX.X))
            for e in range(NEXP):
                ch(lambda e=e: v.tensor_copy(
                    out=bass.AP(grep_sb, e, [[2 * GA, P], [NEXP, QT]]),
                    in_=gm_sb[:, 0:QT]))
            ch(lambda: v.tensor_tensor(out=ga_sb[:], in0=lgps[:],
                                       in1=grep_sb[:, 0:GA], op=ALU.is_ge))
            ch(lambda: v.scalar_tensor_tensor(out=gl_sb[:], in0=ga_sb[:],
                                              scalar=-1e30, in1=lgps[:],
                                              op0=ALU.mult, op1=ALU.add))
            ch(lambda: v.reduce_max(out=gm_sb[:, QT:2 * QT],
                                    in_=gl_sb.ap().rearrange("p (qt e) -> p qt e", e=NEXP),
                                    axis=AX.X))
            for e in range(NEXP):
                ch(lambda e=e: v.tensor_copy(
                    out=bass.AP(grep_sb, GA + e, [[2 * GA, P], [NEXP, QT]]),
                    in_=gm_sb[:, QT:2 * QT]))
            ch(lambda: v.tensor_tensor(out=ga_sb[:], in0=lgps[:],
                                       in1=grep_sb[:, GA:2 * GA], op=ALU.is_ge))
            ch(lambda: v.tensor_tensor(out=gl_sb[:], in0=lgps[:],
                                       in1=grep_sb[:, 0:GA], op=ALU.subtract))
            assert s_dveG.n == 14, s_dveG.n
            v.wait_ge(s_actG.sem, 1)
            ch(lambda: v.tensor_tensor(out=gx_sb[:], in0=gx_sb[:], in1=ga_sb[:],
                                       op=ALU.mult))
            ch(lambda: v.reduce_sum(out=gm_sb[:, 2 * QT:3 * QT],
                                    in_=gx_sb.ap().rearrange("p (qt e) -> p qt e", e=NEXP),
                                    axis=AX.X))
            ch(lambda: v.reciprocal(out=gm_sb[:, 3 * QT:4 * QT],
                                    in_=gm_sb[:, 2 * QT:3 * QT]))
            for e in range(NEXP):
                ch(lambda e=e: v.tensor_copy(
                    out=bass.AP(grep_sb, e, [[2 * GA, P], [NEXP, QT]]),
                    in_=gm_sb[:, 3 * QT:4 * QT]))
            ch(lambda: v.tensor_tensor(out=ga_sb[:], in0=gx_sb[:],
                                       in1=grep_sb[:, 0:GA], op=ALU.mult))
            assert s_dveG.n == 22, s_dveG.n
            for qt in range(QT):
                v.wait_ge(s_peGT.sem, qt + 1)
                chain(v, s_dveE2, lambda qt=qt: v.tensor_copy(
                    out=gt_sb[:, qt * P:(qt + 1) * P],
                    in_=gtps[:, 0:P]))
            v.wait_ge(s_peGB.sem, QT)
            chain(v, s_dveE2, lambda: v.tensor_copy(out=gbs_sb[:], in_=gb_all[:]))
            v.wait_ge(s_peMID.sem, 1)
            chain(v, s_dveE2, lambda: v.tensor_tensor(
                out=mid_sb[:], in0=mid_all[:], in1=gbs_sb[:], op=ALU.mult))
            for qt in range(QT):
                for og in range(VOG):
                    i = VOG * qt + og
                    v.wait_ge(s_peD.sem, i + 1)
                    v.wait_ge(s_x1b[qt % 2].sem, 16 * (qt // 2 + 1))
                    if qt >= 2 and og == 0:
                        v.wait_ge(s_outF[qt % 2].sem, 16 * (qt // 2))
                    if s_dveF.n:
                        v.wait_ge(s_dveF.sem, s_dveF.n)
                    s_dveF.inc(v.tensor_tensor(
                        out=outs_sb[:, (qt % 2) * D + og * 512:(qt % 2) * D + (og + 1) * 512],
                        in0=dps[i % 2][:],
                        in1=x1_sb[:, (qt % 2) * D + og * 512:(qt % 2) * D + (og + 1) * 512],
                        op=ALU.add))
        block.vector(p4b_vec)

        def p4b_sync(sy):
            def x1b_load(qt):
                s_x1b[qt % 2].dma(sy.dma_start(
                    out=x1_sb[:, (qt % 2) * D:(qt % 2) * D + D],
                    in_=x1_d.ap()[qt * P:(qt + 1) * P, :]))

            for qt in range(min(2, QT)):
                sy.wait_ge(s_dveL2.sem, 7 * (QT - 2 + qt) + 7)
                sy.wait_ge(s_actL2.sem, 3 * (QT - 2 + qt) + 1)
                x1b_load(qt)
            for qt in range(QT):
                sy.wait_ge(s_dveF.sem, VOG * qt + VOG)
                s_outF[qt % 2].dma(sy.dma_start(
                    out=out_d.ap()[qt * P:(qt + 1) * P, :],
                    in_=outs_sb[:, (qt % 2) * D:(qt % 2) * D + D]))
                if qt + 2 < QT:
                    x1b_load(qt + 2)
            for s in s_outF:
                if s.n:
                    sy.wait_ge(s.sem, s.n)
        block.sync(p4b_sync)

        def p4_gps_end(g):
            for s in s_outF:
                if s.n:
                    g.wait_ge(s.sem, s.n)
        block.gpsimd(p4_gps_end)

        p4.close()

    return nc, dict(D=D, S=S, T=T)


def _host_prep(D, S, x, qkvw, qkvb, ow, ob, A, B, Wr):
    KC = D // P
    OCH = 2 * D // P
    T = S // NCORES
    x2 = np.ascontiguousarray(np.asarray(x, dtype=np.float32).reshape(S, D))
    qkvw = np.asarray(qkvw, dtype=np.float32)
    w_qk = np.ascontiguousarray(
        qkvw[:, :2 * D].reshape(KC, P, OCH, P).transpose(2, 1, 0, 3).reshape(OCH, P, D))
    w_v = np.ascontiguousarray(qkvw[:, 2 * D:].reshape(KC, P, D))
    qkvb = np.asarray(qkvb, dtype=np.float32)
    b_qk = np.ascontiguousarray(qkvb[:2 * D].reshape(OCH, P).T)
    b_v = np.ascontiguousarray(qkvb[2 * D:].reshape(1, D))
    w_o = np.ascontiguousarray(np.asarray(ow, dtype=np.float32).reshape(KC, P, D))
    b_o = np.ascontiguousarray(np.asarray(ob, dtype=np.float32).reshape(1, D))
    A = np.asarray(A, dtype=np.float32)
    w_a = np.ascontiguousarray(A.transpose(2, 0, 1).reshape(KC, P, ER))
    B = np.asarray(B, dtype=np.float32)
    w_b = np.ascontiguousarray(B.transpose(0, 2, 1).reshape(ER, D))
    Wr = np.asarray(Wr, dtype=np.float32)
    w_r = np.ascontiguousarray(Wr.T.reshape(KC, P, NEXP))
    ident = np.eye(P, dtype=np.float32)
    ind4 = np.zeros((NEXP, ER), dtype=np.float32)
    for e in range(NEXP):
        ind4[e, e * LORA_R:(e + 1) * LORA_R] = LORA_SCALE
    shared = dict(w_qk=w_qk, w_v=w_v, w_o=w_o, b_qk=b_qk, b_v=b_v, b_o=b_o,
                  w_a=w_a, w_b=w_b, w_r=w_r, ident=ident, ind4=ind4)
    in_maps = []
    for c in range(NCORES):
        m = dict(shared)
        m["x"] = np.ascontiguousarray(x2[c * T:(c + 1) * T, :])
        in_maps.append(m)
    return in_maps


def kernel(L0_attn_x, L0_attn_attn_qkvw, L0_attn_attn_qkvb, L0_attn_attn_ow,
           L0_attn_attn_ob, L0_ffn_A, L0_ffn_B, L0_ffn_Wr):
    D = 4096
    S = 8192
    in_maps = _host_prep(D, S, L0_attn_x, L0_attn_attn_qkvw, L0_attn_attn_qkvb,
                         L0_attn_attn_ow, L0_attn_attn_ob, L0_ffn_A, L0_ffn_B,
                         L0_ffn_Wr)
    nc, meta = build_program(D, S)
    res = run_bass_kernel_spmd(nc, in_maps, core_ids=list(range(NCORES)))
    out = np.concatenate([np.asarray(r["out"], dtype=np.float32)
                          for r in res.results], axis=0)
    return out.reshape(1, S, D)


# version 15
# speedup vs baseline: 1.4228x; 1.0178x over previous
import math
from contextlib import ExitStack

import numpy as np
import concourse.bass as bass
import concourse.mybir as mybir
from concourse import dve_ops
from concourse.bass_utils import run_bass_kernel_spmd

F32 = mybir.dt.float32
BF16 = mybir.dt.bfloat16
AF = mybir.ActivationFunctionType
ALU = mybir.AluOpType
AX = mybir.AxisListType

NCORES = 8
LORA_SCALE = 4.0
NEXP = 4
LORA_R = 16
ER = NEXP * LORA_R
EPS = 1e-5
P = 128


class Sem:

    def __init__(self, ctx, nc, name, registry):
        self.sem = ctx.enter_context(nc.semaphore(name))
        self.n = 0
        registry.append(self)

    def inc(self, instr, k=1):
        instr.then_inc(self.sem, k)
        self.n += k
        return self.n

    def dma(self, instr):
        return self.inc(instr, 16)

    def wait(self, eng, val=None):
        eng.wait_ge(self.sem, self.n if val is None else val)


def build_program(D, S):
    T = S // NCORES
    KC = D // P
    HEADS = D // P
    OCH = 2 * D // P
    VOG = D // 512
    QT = T // P
    NQG = T // 512
    KT = S // P
    NPAIR = KT // 2
    NG = KC // 8
    NQTG = QT // 4
    HPV = 4
    assert T % 512 == 0 and D % 1024 == 0 and QT % 4 == 0

    nc = bass.Bass("TRN2", target_bir_lowering=False)

    x_d = nc.dram_tensor("x", [T, D], F32, kind="ExternalInput")
    wqk_d = nc.dram_tensor("w_qk", [OCH, P, D], F32, kind="ExternalInput")
    wv_d = nc.dram_tensor("w_v", [KC, P, D], F32, kind="ExternalInput")
    wo_d = nc.dram_tensor("w_o", [KC, P, D], F32, kind="ExternalInput")
    bqk_d = nc.dram_tensor("b_qk", [P, OCH], F32, kind="ExternalInput")
    bv_d = nc.dram_tensor("b_v", [1, D], F32, kind="ExternalInput")
    bo_d = nc.dram_tensor("b_o", [1, D], F32, kind="ExternalInput")
    wa_d = nc.dram_tensor("w_a", [KC, P, ER], F32, kind="ExternalInput")
    wb_d = nc.dram_tensor("w_b", [ER, D], F32, kind="ExternalInput")
    wr_d = nc.dram_tensor("w_r", [KC, P, NEXP], F32, kind="ExternalInput")
    id_d = nc.dram_tensor("ident", [P, P], F32, kind="ExternalInput")
    ind_d = nc.dram_tensor("ind4", [NEXP, ER], F32, kind="ExternalInput")

    out_d = nc.dram_tensor("out", [T, D], F32, kind="ExternalOutput")

    cc_in = nc.dram_tensor("cc_in", [2 * D, T], BF16, kind="Internal")
    cck_out = nc.dram_tensor("cck_out", [NCORES * D, T], BF16,
                             kind="Internal", addr_space="Shared")
    ccv_out = nc.dram_tensor("ccv_out", [NCORES * D, T], BF16,
                             kind="Internal", addr_space="Shared")
    qs_d = nc.dram_tensor("qs", [D, T], BF16, kind="Internal")
    x1_d = nc.dram_tensor("x1", [T, D], F32, kind="Internal")

    gctx = ExitStack()
    REG = []

    def sb(name, shape, dtype):
        return gctx.enter_context(nc.sbuf_tensor(name, shape, dtype))

    ident_bf = sb("ident_bf", [P, P], BF16)
    ident_f = sb("ident_f", [P, P], F32)
    bqk_sb = sb("bqk_sb", [P, OCH], F32)
    bv_sb = sb("bv_sb", [1, D], BF16)
    bo_sb = sb("bo_sb", [1, D], BF16)
    wa_sb = sb("wa_sb", [P, KC * ER], BF16)
    wb_sb = sb("wb_sb", [ER, D], BF16)
    wr_sb = sb("wr_sb", [P, KC * NEXP], BF16)
    ind_sb = sb("ind_sb", [NEXP, ER], F32)
    ones_c = sb("ones_c", [P, P], BF16)
    ones_r = sb("ones_r", [1, P], BF16)
    ones_rf = sb("ones_rf", [1, P], F32)
    dump = sb("dump", [P, D], BF16)
    st = sb("st", [P, 32], F32)

    def mks(name):
        return Sem(gctx, nc, name, REG)

    def mkslots(name, k):
        return [mks(f"{name}{j}") for j in range(k)]

    s_small = mks("s_small")
    s_x = mkslots("s_x", 2)
    s_wq = mkslots("s_wq", 2)
    s_wv = mkslots("s_wv", 2)
    s_wo = mkslots("s_wo", 4)
    s_kv = mkslots("s_kv", 2)
    s_x1 = mkslots("s_x1", 2)
    s_x1b = mkslots("s_x1b", 2)
    s_xr = mkslots("s_xr", 4)
    s_cc = mks("s_cc")
    s_dveM = mks("s_dveM")
    s_dveLN = mks("s_dveLN")
    s_actLN = mks("s_actLN")
    s_peTR = mks("s_peTR")
    s_dveTR = mks("s_dveTR")
    s_peQK = mks("s_peQK")
    s_actQK = mks("s_actQK")
    s_stQK = mkslots("s_stQK", 4)
    s_peV = mks("s_peV")
    s_peV2 = mks("s_peV2")
    s_dveV = mks("s_dveV")
    s_stV = mkslots("s_stV", 2)
    s_peS = mks("s_peS")
    s_peP = mks("s_peP")
    s_peZ4 = mks("s_peZ4")
    s_dveQ = mks("s_dveQ")
    s_dveN = mks("s_dveN")
    s_actE = mks("s_actE")
    s_peH = mks("s_peH")
    s_dveZ = mks("s_dveZ")
    s_peZB = mks("s_peZB")
    s_dveC = mks("s_dveC")
    s_peO = mks("s_peO")
    s_dveX = mks("s_dveX")
    s_stX1 = mkslots("s_stX1", 4)
    s_dveL2 = mks("s_dveL2")
    s_actL2 = mks("s_actL2")
    s_peTR2 = mks("s_peTR2")
    s_dveTR2 = mks("s_dveTR2")
    s_peLG = mks("s_peLG")
    s_peMID = mks("s_peMID")
    s_peGT = mks("s_peGT")
    s_peGB = mks("s_peGB")
    s_peD = mks("s_peD")
    s_dveG = mks("s_dveG")
    s_actG = mks("s_actG")
    s_dveE2 = mks("s_dveE2")
    s_dveF = mks("s_dveF")
    s_outF = mkslots("s_outF", 2)

    block = gctx.enter_context(nc.Block())

    def barrier(eng, snap=None):
        vals = snap if snap is not None else [s.n for s in REG]
        for s, v_ in zip(REG, vals):
            if v_:
                eng.wait_ge(s.sem, v_)

    SC = float(1.0 / math.sqrt(P))

    def chain(eng, sem, instr_fn):
        if sem.n:
            eng.wait_ge(sem.sem, sem.n)
        return sem.inc(instr_fn())

    with gctx:
        def p0_gpsimd(g):
            s_small.dma(g.dma_start(out=ident_bf[:], in_=id_d.ap()))
            s_small.dma(g.dma_start(out=ident_f[:], in_=id_d.ap()))
            s_small.dma(g.dma_start(out=bqk_sb[:], in_=bqk_d.ap()))
            s_small.dma(g.dma_start(out=bv_sb[:], in_=bv_d.ap()))
            s_small.dma(g.dma_start(out=bo_sb[:], in_=bo_d.ap()))
            s_small.dma(g.dma_start(
                out=wa_sb.ap().rearrange("p (kc e) -> p kc e", kc=KC),
                in_=wa_d.ap().rearrange("kc p e -> p kc e")))
            s_small.dma(g.dma_start(out=wb_sb[:], in_=wb_d.ap()))
            s_small.dma(g.dma_start(
                out=wr_sb.ap().rearrange("p (kc e) -> p kc e", kc=KC),
                in_=wr_d.ap().rearrange("kc p e -> p kc e")))
            s_small.dma(g.dma_start(out=ind_sb[:], in_=ind_d.ap()))
        block.gpsimd(p0_gpsimd)

        def p0_vector(v):
            v.memset(ones_c[:], 1.0)
            v.memset(ones_r[:], 1.0)
            s_dveM.inc(v.memset(ones_rf[:], 1.0))
        block.vector(p0_vector)

        p1 = ExitStack()
        x_sb = p1.enter_context(nc.sbuf_tensor("x_sb", [P, 2 * D], F32))
        ln_tm = p1.enter_context(nc.sbuf_tensor("ln_tm", [P, D], BF16))
        ln_fm = p1.enter_context(nc.sbuf_tensor("ln_fm", [P, KC * T], BF16))
        wqk_sb = p1.enter_context(nc.sbuf_tensor("wqk_sb", [P, 2 * D], BF16))
        stage_qk = p1.enter_context(nc.sbuf_tensor("stage_qk", [P, 4 * 1024], BF16))
        wv_sb = p1.enter_context(nc.sbuf_tensor("wv_sb", [P, KC // 2 * 2 * 512], BF16))
        stage_v = p1.enter_context(nc.sbuf_tensor("stage_v", [P, 2 * 4 * 512], BF16))
        qkps = [p1.enter_context(nc.psum_tensor(f"qkps{i}", [P, 1024], F32))
                for i in range(2)]
        ps_ln = ExitStack()
        tps_ln = ps_ln.enter_context(nc.psum_tensor("tps_ln", [P, 2 * 1024], BF16))

        ln_fm_r = ln_fm.ap().rearrange("p (kc t) -> p kc t", kc=KC)

        def p1_sync(sy):
            for qt in range(QT):
                if qt >= 2:
                    sy.wait_ge(s_dveLN.sem, 7 * (qt - 2) + 7)
                    sy.wait_ge(s_actLN.sem, 3 * (qt - 2) + 1)
                s_x[qt % 2].dma(sy.dma_start(
                    out=x_sb[:, (qt % 2) * D:(qt % 2) * D + D],
                    in_=x_d.ap()[qt * P:(qt + 1) * P, :]))
        block.sync(p1_sync)

        def emit_ln_vec(v, qt, xap, sem_d, sem_a, s_petr, ln_tm_t, war_lag=2):
            b7 = 7 * qt
            chain(v, sem_d, lambda: v.reduce_sum(out=st[:, 0:1], in_=xap, axis=AX.X))
            chain(v, sem_d, lambda: v.tensor_scalar_mul(st[:, 1:2], st[:, 0:1], 1.0 / D))
            chain(v, sem_d, lambda: v.tensor_tensor(out=st[:, 2:3], in0=st[:, 1:2],
                                                    in1=st[:, 1:2], op=ALU.mult))
            v.wait_ge(sem_a.sem, 3 * qt + 1)
            chain(v, sem_d, lambda: v.tensor_scalar(out=st[:, 3:4], in0=st[:, 4:5],
                                                    scalar1=1.0 / D, scalar2=float(EPS),
                                                    op0=ALU.mult, op1=ALU.add))
            chain(v, sem_d, lambda: v.tensor_tensor(out=st[:, 5:6], in0=st[:, 3:4],
                                                    in1=st[:, 2:3], op=ALU.subtract))
            v.wait_ge(sem_a.sem, 3 * qt + 3)
            chain(v, sem_d, lambda: v.tensor_scalar(out=st[:, 7:8], in0=st[:, 1:2],
                                                    scalar1=st[:, 6:7], scalar2=-1.0,
                                                    op0=ALU.mult, op1=ALU.mult))
            if qt >= war_lag:
                v.wait_ge(s_petr.sem, NG * (qt - war_lag) + NG)
            nb = (qt % 2) * D if war_lag == 2 else 0
            chain(v, sem_d, lambda: v.tensor_scalar(
                out=ln_tm_t[:, nb:nb + D], in0=xap,
                scalar1=st[:, 6:7], scalar2=st[:, 7:8],
                op0=ALU.mult, op1=ALU.add))
            assert sem_d.n == b7 + 7

        def emit_ln_act(sc, qt, xap, sem_d, sem_a, wait_x):
            wait_x()
            if qt >= 1:
                sc.wait_ge(sem_d.sem, 7 * (qt - 1) + 4)
            chain(sc, sem_a, lambda: sc.activation(out=dump[:], in_=xap,
                                                   func=AF.Square,
                                                   accum_out=st[:, 4:5]))
            sc.wait_ge(sem_d.sem, 7 * qt + 5)
            chain(sc, sem_a, lambda: sc.activation(out=st[:, 8:9], in_=st[:, 5:6],
                                                   func=AF.Ln))
            chain(sc, sem_a, lambda: sc.activation(out=st[:, 6:7], in_=st[:, 8:9],
                                                   func=AF.Exp, scale=-0.5))
            assert sem_a.n == 3 * qt + 3

        def p1_vec_ln(v):
            for qt in range(QT):
                v.wait_ge(s_x[qt % 2].sem, 16 * (qt // 2 + 1))
                emit_ln_vec(v, qt, x_sb[:, (qt % 2) * D:(qt % 2) * D + D],
                            s_dveLN, s_actLN, s_peTR, ln_tm, war_lag=1)
                for g in range(NG):
                    G = NG * qt + g
                    v.wait_ge(s_peTR.sem, G + 1)
                    if s_dveTR.n:
                        v.wait_ge(s_dveTR.sem, s_dveTR.n)
                    s_dveTR.inc(v.tensor_copy(
                        out=ln_fm_r[:, g * 8:(g + 1) * 8, qt * P:(qt + 1) * P],
                        in_=tps_ln.ap()[:, (G % 2) * 1024:(G % 2) * 1024 + 1024]
                        .rearrange("p (kc t) -> p kc t", kc=8)))
        block.vector(p1_vec_ln)

        def p1_scalar_ln(sc):
            for qt in range(QT):
                emit_ln_act(sc, qt, x_sb[:, (qt % 2) * D:(qt % 2) * D + D],
                            s_dveLN, s_actLN,
                            lambda qt=qt: sc.wait_ge(s_x[qt % 2].sem,
                                                     16 * (qt // 2 + 1)))
        block.scalar(p1_scalar_ln)

        def p1_pe_tr(t):
            s_small.wait(t)
            t.wait_ge(s_dveM.sem, 1)
            for qt in range(QT):
                t.wait_ge(s_dveLN.sem, 7 * qt + 7)
                for g in range(NG):
                    G = NG * qt + g
                    if G >= 2:
                        t.wait_ge(s_dveTR.sem, G - 1)
                    for j in range(8):
                        kc = g * 8 + j
                        mm = t.transpose(
                            tps_ln[:, (G % 2) * 1024 + j * P:(G % 2) * 1024 + (j + 1) * P],
                            in_=ln_tm[:, kc * P:(kc + 1) * P],
                            identity=ident_bf[:])
                    s_peTR.inc(mm)
        block.tensor(p1_pe_tr)

        order = list(range(HEADS, OCH)) + list(range(0, HEADS))
        NKO = HEADS

        def emit_wqk_loads(g, i0, i1):
            for i in range(i0, i1):
                if i >= 2:
                    g.wait_ge(s_peQK.sem, i - 1)
                s_wq[i % 2].dma(g.dma_start(
                    out=wqk_sb[:, (i % 2) * D:(i % 2) * D + D],
                    in_=wqk_d.ap()[order[i]]))

        def emit_qk_compute(t, i0, i1):
            for i in range(i0, i1):
                t.wait_ge(s_wq[i % 2].sem, 16 * (i // 2 + 1))
                if i == 0:
                    t.wait_ge(s_dveTR.sem, NG * QT)
                if i >= 2:
                    t.wait_ge(s_actQK.sem, i - 1)
                for kc in range(KC):
                    for qg in range(NQG):
                        mm = t.matmul(
                            qkps[i % 2][:, qg * 512:(qg + 1) * 512],
                            lhsT=wqk_sb[:, (i % 2) * D + kc * P:(i % 2) * D + (kc + 1) * P],
                            rhs=ln_fm[:, kc * T + qg * 512:kc * T + qg * 512 + 512],
                            start=(kc == 0), stop=(kc == KC - 1))
                s_peQK.inc(mm)

        def emit_qk_evac(sc, i0, i1):
            for i in range(i0, i1):
                if i == 0:
                    s_small.wait(sc)
                sc.wait_ge(s_peQK.sem, i + 1)
                if i >= 4:
                    sc.wait_ge(s_stQK[i % 4].sem, 16 * (i // 4))
                och = order[i]
                chain(sc, s_actQK, lambda i=i, och=och: sc.activation(
                    out=stage_qk[:, (i % 4) * 1024:(i % 4) * 1024 + T],
                    in_=qkps[i % 2][:, 0:T], func=AF.Identity,
                    bias=bqk_sb[:, och:och + 1]))

        def emit_qk_store(sy, i0, i1):
            for i in range(i0, i1):
                sy.wait_ge(s_actQK.sem, i + 1)
                och = order[i]
                if och >= HEADS:
                    hk = och - HEADS
                    dst = cc_in.ap()[hk * P:(hk + 1) * P, :]
                else:
                    dst = qs_d.ap()[och * P:(och + 1) * P, :]
                s_stQK[i % 4].dma(sy.dma_start(
                    out=dst, in_=stage_qk[:, (i % 4) * 1024:(i % 4) * 1024 + T]))

        block.gpsimd(lambda g: emit_wqk_loads(g, 0, NKO))

        def p1_gps_cck(g):
            for s in s_stQK:
                if s.n:
                    g.wait_ge(s.sem, s.n)
            s_cc.inc(g.collective_compute(
                "AllGather", ALU.bypass,
                replica_groups=[list(range(NCORES))],
                ins=[cc_in.ap()[0:D, :]], outs=[cck_out.ap()]))

        block.tensor(lambda t: emit_qk_compute(t, 0, NKO))
        block.scalar(lambda sc: emit_qk_evac(sc, 0, NKO))
        block.sync(lambda sy: emit_qk_store(sy, 0, NKO))

        ps_ln.close()
        vps = [p1.enter_context(nc.psum_tensor(f"vps{i}", [P, 512], F32))
               for i in range(4)]
        KH = KC // 2

        def p1_gps_wv(g, v0, v1):
            for vog in range(v0, v1):
                for half in range(2):
                    if vog >= 1:
                        g.wait_ge(s_peV.sem,
                                  2 * NQTG * (vog - 1) + 2 * (NQTG - 1) + half + 1)
                    kc0 = half * KH
                    s_wv[half].dma(g.dma_start(
                        out=wv_sb.ap()[:, half * KH * 512:(half + 1) * KH * 512]
                        .rearrange("p (kc c) -> p kc c", kc=KH),
                        in_=wv_d.ap()[kc0:kc0 + KH, :, vog * 512:(vog + 1) * 512]
                        .rearrange("kc p c -> p kc c")))
        block.gpsimd(lambda g: p1_gps_wv(g, 0, 1))
        block.gpsimd(p1_gps_cck)
        block.gpsimd(lambda g: p1_gps_wv(g, 1, VOG))

        def p1_pe_v(t):
            for vog in range(VOG):
                for qtg in range(NQTG):
                    gi = NQTG * vog + qtg
                    if gi >= 1:
                        t.wait_ge(s_dveV.sem, 4 * gi)
                    for half in range(2):
                        t.wait_ge(s_wv[half].sem, 16 * (vog + 1))
                        for kcl in range(KH):
                            kc = half * KH + kcl
                            for q4 in range(4):
                                qt = qtg * 4 + q4
                                mm = t.matmul(
                                    vps[q4][:],
                                    lhsT=ln_fm[:, kc * T + qt * P:kc * T + (qt + 1) * P],
                                    rhs=wv_sb[:, (half * KH + kcl) * 512:
                                              (half * KH + kcl) * 512 + 512],
                                    start=(kc == 0), stop=False)
                        s_peV.inc(mm)
                    for q4 in range(4):
                        mm = t.matmul(vps[q4][:], lhsT=ones_r[:],
                                      rhs=bv_sb[0:1, vog * 512:(vog + 1) * 512],
                                      start=False, stop=True)
                        s_peV2.inc(mm)
        block.tensor(p1_pe_v)

        def p1_vec_v(v):
            for vog in range(VOG):
                for qtg in range(NQTG):
                    gi = NQTG * vog + qtg
                    for q4 in range(4):
                        v.wait_ge(s_peV2.sem, 4 * gi + q4 + 1)
                        if gi >= 2:
                            v.wait_ge(s_stV[gi % 2].sem, 16 * HPV * (gi // 2))
                        if s_dveV.n:
                            v.wait_ge(s_dveV.sem, s_dveV.n)
                        s_dveV.inc(v.tensor_copy(
                            out=stage_v[:, (gi % 2) * 2048 + q4 * 512:
                                        (gi % 2) * 2048 + (q4 + 1) * 512],
                            in_=vps[q4][:]))
        block.vector(p1_vec_v)

        def p1_sync_v(sy):
            for vog in range(VOG):
                for qtg in range(NQTG):
                    gi = NQTG * vog + qtg
                    sy.wait_ge(s_dveV.sem, 4 * gi + 4)
                    for j in range(HPV):
                        hv = vog * HPV + j
                        dst = cc_in.ap()[D + hv * P:D + (hv + 1) * P,
                                         qtg * 4 * P:(qtg + 1) * 4 * P] \
                            .rearrange("p (qt c) -> p qt c", qt=4)
                        src = stage_v.ap()[:, (gi % 2) * 2048:(gi % 2) * 2048 + 2048] \
                            .rearrange("p (qt c) -> p qt c", qt=4)[:, :, j * P:(j + 1) * P]
                        s_stV[gi % 2].dma(sy.dma_start(out=dst, in_=src))
        block.sync(p1_sync_v)

        def p1_gps_ccv(g):
            for s in s_stV:
                if s.n:
                    g.wait_ge(s.sem, s.n)
            s_cc.inc(g.collective_compute(
                "AllGather", ALU.bypass,
                replica_groups=[list(range(NCORES))],
                ins=[cc_in.ap()[D:2 * D, :]], outs=[ccv_out.ap()]))
        block.gpsimd(lambda g: emit_wqk_loads(g, NKO, NKO + 3))
        block.gpsimd(p1_gps_ccv)

        block.gpsimd(lambda g: emit_wqk_loads(g, NKO + 3, OCH))
        block.tensor(lambda t: emit_qk_compute(t, NKO, OCH))
        block.scalar(lambda sc: emit_qk_evac(sc, NKO, OCH))
        block.sync(lambda sy: emit_qk_store(sy, NKO, OCH))

        p1.close()

        p2 = ExitStack()
        ctx_sb = p2.enter_context(nc.sbuf_tensor("ctx_sb", [P, HEADS * T], BF16))
        sps = [p2.enter_context(nc.psum_tensor(f"sps{i}", [P, 1024], F32))
               for i in range(NQG)]
        ctxp = [p2.enter_context(nc.psum_tensor(f"ctxp{i}", [P, 512], F32))
                for i in range(NQG)]
        zzp = [p2.enter_context(nc.psum_tensor(f"zzp{i}", [P, 512], F32))
               for i in range(NQG)]
        p2a = ExitStack()
        ksb = p2a.enter_context(nc.sbuf_tensor("ksb", [P, 2 * S], BF16))
        vsb = p2a.enter_context(nc.sbuf_tensor("vsb", [P, 2 * S], BF16))
        qsb = p2a.enter_context(nc.sbuf_tensor("qsb", [P, 2 * T], BF16))
        esb = p2a.enter_context(nc.sbuf_tensor("esb", [P, 8 * 1024], BF16))
        eq_sb = p2a.enter_context(nc.sbuf_tensor("eq_sb", [P, NQG * 1024], BF16))
        e4_sb = p2a.enter_context(nc.sbuf_tensor("e4_sb", [P, NQG * 1024], BF16))
        zb_sb = p2a.enter_context(nc.sbuf_tensor("zb_sb", [P, NQG * 512], F32))
        zscr = p2a.enter_context(nc.sbuf_tensor("zscr", [P, 512], F32))

        NL = 1 + 2 * NCORES
        snap2 = [s.n for s in REG]

        def p2_sync(sy):
            barrier(sy, snap2)
            sy.wait_ge(s_cc.sem, 2)
            for h in range(HEADS):
                if h >= 2:
                    sy.wait_ge(s_peH.sem, h - 1)
                hb = h % 2
                s_kv[hb].dma(sy.dma_start(out=qsb[:, hb * T:hb * T + T],
                                          in_=qs_d.ap()[h * P:(h + 1) * P, :]))
                for r in range(NCORES):
                    s_kv[hb].dma(sy.dma_start(
                        out=ksb[:, hb * S + r * T:hb * S + (r + 1) * T],
                        in_=cck_out.ap()[r * D + h * P:r * D + (h + 1) * P, :]))
                for r in range(NCORES):
                    s_kv[hb].dma(sy.dma_start(
                        out=vsb.ap()[:, hb * S + r * T:hb * S + (r + 1) * T]
                        .rearrange("p (qt c) -> p qt c", qt=QT),
                        in_=ccv_out.ap()[r * D + h * P:r * D + (h + 1) * P, :]
                        .rearrange("p (qt c) -> p qt c", qt=QT)))
        block.sync(p2_sync)

        NQUAD = KT // 4
        QGS = list(range(NQG))

        def p2_tensor(t):
            barrier(t, snap2)
            for h in range(HEADS):
                hb = h % 2
                t.wait_ge(s_kv[h % 2].sem, 16 * NL * (h // 2 + 1))
                bE = NQG * NPAIR * h
                bP = NQG * NPAIR * h
                bZ = NQG * NQUAD * h
                if h >= 1:
                    t.wait_ge(s_actE.sem, bE)
                for p in range(NPAIR):
                    for qg in QGS:
                        if p >= 1:
                            t.wait_ge(s_actE.sem, bE + NQG * (p - 1) + qg + 1)
                        for j in range(2):
                            kt = 2 * p + j
                            mm = t.matmul(
                                sps[qg][:, j * 512:(j + 1) * 512],
                                lhsT=ksb[:, hb * S + kt * P:hb * S + (kt + 1) * P],
                                rhs=qsb[:, hb * T + qg * 512:hb * T + qg * 512 + 512],
                                start=True, stop=True)
                        s_peS.inc(mm)
                    if p >= 1:
                        pv = p - 1
                        for qg in QGS:
                            if pv == 0:
                                if h >= 1:
                                    t.wait_ge(s_dveC.sem, NQG * (h - 1) + qg + 1)
                            sl = (NQG * pv + qg) % 8
                            for j in range(2):
                                kt = 2 * pv + j
                                mm = t.matmul(
                                    ctxp[qg][:],
                                    lhsT=vsb[:, hb * S + kt * P:hb * S + (kt + 1) * P],
                                    rhs=esb[:, sl * 1024 + j * 512:sl * 1024 + (j + 1) * 512],
                                    start=(kt == 0), stop=False)
                            s_peP.inc(mm)
                            if pv % 2 == 1:
                                q = pv // 2
                                t.wait_ge(s_dveQ.sem, 3 * NQG * NQUAD * h + 3 * NQG * q + 3 * qg + 3)
                                if q == 0 and h >= 1:
                                    t.wait_ge(s_dveZ.sem, 2 * (NQG * (h - 1) + qg) + 2)
                                s_peZ4.inc(t.matmul(
                                    zzp[qg][:], lhsT=ones_c[:],
                                    rhs=e4_sb[:, qg * 1024 + (q % 2) * 512:
                                              qg * 1024 + (q % 2) * 512 + 512],
                                    start=(q == 0), stop=(q == NQUAD - 1)))
                pv = NPAIR - 1
                for qg in QGS:
                    t.wait_ge(s_actE.sem, bE + NQG * pv + qg + 1)
                    sl = (NQG * pv + qg) % 8
                    for j in range(2):
                        kt = 2 * pv + j
                        mm = t.matmul(
                            ctxp[qg][:],
                            lhsT=vsb[:, hb * S + kt * P:hb * S + (kt + 1) * P],
                            rhs=esb[:, sl * 1024 + j * 512:sl * 1024 + (j + 1) * 512],
                            start=False, stop=(j == 1))
                    s_peP.inc(mm)
                    q = pv // 2
                    t.wait_ge(s_dveQ.sem, 3 * NQG * NQUAD * h + 3 * NQG * q + 3 * qg + 3)
                    s_peZ4.inc(t.matmul(
                        zzp[qg][:], lhsT=ones_c[:],
                        rhs=e4_sb[:, qg * 1024 + (q % 2) * 512:
                                  qg * 1024 + (q % 2) * 512 + 512],
                        start=False, stop=True))
                t.wait_ge(s_peP.sem, bP + NQG * NPAIR)
                t.sem_inc(s_peH.sem, 1)
                s_peH.n += 1
        block.tensor(p2_tensor)

        def p2_scalar(sc):
            barrier(sc, snap2)
            for h in range(HEADS):
                bS = NQG * NPAIR * h
                bP = NQG * NPAIR * h
                bQ = 3 * NQG * NQUAD * h
                for p in range(NPAIR):
                    for qg in QGS:
                        e = NQG * p + qg
                        sc.wait_ge(s_peS.sem, bS + e + 1)
                        if p >= 8 // NQG:
                            pp = p - 8 // NQG
                            sc.wait_ge(s_peP.sem, bP + NQG * pp + qg + 1)
                            sc.wait_ge(s_dveQ.sem,
                                       bQ + 3 * NQG * (pp // 2) + 3 * qg + (pp % 2) + 1)
                        elif h >= 1:
                            pp = p + NPAIR - 8 // NQG
                            bPp = NQG * NPAIR * (h - 1)
                            bQp = 3 * NQG * NQUAD * (h - 1)
                            sc.wait_ge(s_peP.sem, bPp + NQG * pp + qg + 1)
                            sc.wait_ge(s_dveQ.sem,
                                       bQp + 3 * NQG * (pp // 2) + 3 * qg + (pp % 2) + 1)
                        chain(sc, s_actE, lambda qg=qg, e=e: sc.activation(
                            out=esb[:, (e % 8) * 1024:(e % 8) * 1024 + 1024],
                            in_=sps[qg][:], func=AF.Exp, scale=SC))
        block.scalar(p2_scalar)

        def p2_vector(v):
            barrier(v, snap2)
            for h in range(HEADS):
                bE = NQG * NPAIR * h
                bZ = NQG * NQUAD * h
                for q in range(NQUAD):
                    for qg in QGS:
                        p0, p1 = 2 * q, 2 * q + 1
                        e0 = (NQG * p0 + qg) % 8
                        e1 = (NQG * p1 + qg) % 8
                        v.wait_ge(s_actE.sem, bE + NQG * p0 + qg + 1)
                        war = bZ + NQG * (q - 2) + qg + 1
                        if war > 0:
                            v.wait_ge(s_peZ4.sem, war)
                        chain(v, s_dveQ, lambda qg=qg, e0=e0: v.tensor_add(
                            out=eq_sb[:, qg * 1024:qg * 1024 + 512],
                            in0=esb[:, e0 * 1024:e0 * 1024 + 512],
                            in1=esb[:, e0 * 1024 + 512:e0 * 1024 + 1024]))
                        v.wait_ge(s_actE.sem, bE + NQG * p1 + qg + 1)
                        chain(v, s_dveQ, lambda qg=qg, e1=e1: v.tensor_add(
                            out=eq_sb[:, qg * 1024 + 512:qg * 1024 + 1024],
                            in0=esb[:, e1 * 1024:e1 * 1024 + 512],
                            in1=esb[:, e1 * 1024 + 512:e1 * 1024 + 1024]))
                        chain(v, s_dveQ, lambda qg=qg, q=q: v.tensor_add(
                            out=e4_sb[:, qg * 1024 + (q % 2) * 512:
                                      qg * 1024 + (q % 2) * 512 + 512],
                            in0=eq_sb[:, qg * 1024:qg * 1024 + 512],
                            in1=eq_sb[:, qg * 1024 + 512:qg * 1024 + 1024]))
                for qg in QGS:
                    v.wait_ge(s_peZ4.sem, bZ + NQG * (NQUAD - 1) + qg + 1)
                    if s_dveZ.n:
                        v.wait_ge(s_dveZ.sem, s_dveZ.n)
                    chain(v, s_dveZ, lambda qg=qg: v.engine_nop())
                    chain(v, s_dveZ, lambda qg=qg: v.reciprocal(
                        out=zb_sb[:, qg * 512:(qg + 1) * 512], in_=zzp[qg][:]))
                for qg in QGS:
                    v.wait_ge(s_peP.sem, NQG * NPAIR * h + NQG * (NPAIR - 1) + qg + 1)
                    chain(v, s_dveC, lambda h=h, qg=qg: v.tensor_copy(
                        out=ctx_sb[:, h * T + qg * 512:h * T + qg * 512 + 512],
                        in_=ctxp[qg][:]))
                for qg in QGS:
                    v.wait_ge(s_dveZ.sem, 2 * (NQG * h + qg) + 2)
                    v.wait_ge(s_dveC.sem, NQG * h + qg + 1)
                    chain(v, s_dveN, lambda qg=qg, h=h: v.tensor_tensor(
                        out=ctx_sb[:, h * T + qg * 512:h * T + qg * 512 + 512],
                        in0=ctx_sb[:, h * T + qg * 512:h * T + qg * 512 + 512],
                        in1=zb_sb[:, qg * 512:(qg + 1) * 512], op=ALU.mult))
        block.vector(p2_vector)

        p2a.close()
        wo_sb = p2.enter_context(nc.sbuf_tensor("wo_sb", [P, 2 * KH * 2 * 512], BF16))
        xr_sb = p2.enter_context(nc.sbuf_tensor("xr_sb", [P, 4 * 512], F32))
        x1s_sb = p2.enter_context(nc.sbuf_tensor("x1s_sb", [P, 4 * 512], F32))
        ops_extra = [p2.enter_context(nc.psum_tensor(f"ops_extra{i}", [P, 512], F32))
                     for i in range(4 * (2 - min(NQG, 2)))]
        if NQG >= 2:
            ops = [sps[0].ap()[:, 0:512], sps[0].ap()[:, 512:1024],
                   sps[1].ap()[:, 0:512], sps[1].ap()[:, 512:1024],
                   ctxp[0].ap()[:], ctxp[1].ap()[:], zzp[0].ap()[:], zzp[1].ap()[:]]
        else:
            ops = [sps[0].ap()[:, 0:512], sps[0].ap()[:, 512:1024],
                   ctxp[0].ap()[:], zzp[0].ap()[:]] + [o.ap()[:] for o in ops_extra]

        snap3 = [s.n for s in REG]

        def p3_gps(g):
            barrier(g, snap3)
            for og in range(VOG):
                for half in range(2):
                    if og >= 2:
                        g.wait_ge(s_peO.sem, QT * (og - 1))
                    kc0 = half * KH
                    base = (og % 2) * 2 * KH * 512
                    s_wo[(og % 2) * 2 + half].dma(g.dma_start(
                        out=wo_sb.ap()[:, base + half * KH * 512:base + (half + 1) * KH * 512]
                        .rearrange("p (kc c) -> p kc c", kc=KH),
                        in_=wo_d.ap()[kc0:kc0 + KH, :, og * 512:(og + 1) * 512]
                        .rearrange("kc p c -> p kc c")))
        block.gpsimd(p3_gps)

        def p3_sync(sy):
            barrier(sy, snap3)
            NI = VOG * QT

            def xr_load(i):
                og, qt = divmod(i, QT)
                s_xr[i % 4].dma(sy.dma_start(
                    out=xr_sb[:, (i % 4) * 512:(i % 4) * 512 + 512],
                    in_=x_d.ap()[qt * P:(qt + 1) * P, og * 512:(og + 1) * 512]))

            for i in range(min(4, NI)):
                xr_load(i)
            for i in range(NI):
                og, qt = divmod(i, QT)
                sy.wait_ge(s_dveX.sem, i + 1)
                s_stX1[i % 4].dma(sy.dma_start(
                    out=x1_d.ap()[qt * P:(qt + 1) * P, og * 512:(og + 1) * 512],
                    in_=x1s_sb[:, (i % 4) * 512:(i % 4) * 512 + 512]))
                if i + 4 < NI:
                    xr_load(i + 4)
        block.sync(p3_sync)

        def p3_tensor(t):
            barrier(t, snap3)
            for og in range(VOG):
                if og >= 1:
                    t.wait_ge(s_dveX.sem, QT * og)
                for half in range(2):
                    t.wait_ge(s_wo[(og % 2) * 2 + half].sem, 16 * (og // 2 + 1))
                    base = (og % 2) * 2 * KH * 512
                    for kcl in range(KH):
                        hc = half * KH + kcl
                        for qt in range(QT):
                            t.matmul(ops[qt],
                                     lhsT=ctx_sb[:, hc * T + qt * P:hc * T + (qt + 1) * P],
                                     rhs=wo_sb[:, base + (half * KH + kcl) * 512:
                                               base + (half * KH + kcl) * 512 + 512],
                                     start=(hc == 0), stop=False)
                for qt in range(QT):
                    mm = t.matmul(ops[qt], lhsT=ones_r[:],
                                  rhs=bo_sb[0:1, og * 512:(og + 1) * 512],
                                  start=False, stop=True)
                    s_peO.inc(mm)
        block.tensor(p3_tensor)

        def p3_vector(v):
            barrier(v, snap3)
            for og in range(VOG):
                for qt in range(QT):
                    i = og * QT + qt
                    v.wait_ge(s_peO.sem, i + 1)
                    v.wait_ge(s_xr[i % 4].sem, 16 * (i // 4 + 1))
                    if i >= 4:
                        v.wait_ge(s_stX1[i % 4].sem, 16 * (i // 4))
                    if s_dveX.n:
                        v.wait_ge(s_dveX.sem, s_dveX.n)
                    s_dveX.inc(v.tensor_tensor(
                        out=x1s_sb[:, (i % 4) * 512:(i % 4) * 512 + 512],
                        in0=ops[qt],
                        in1=xr_sb[:, (i % 4) * 512:(i % 4) * 512 + 512],
                        op=ALU.add))
        block.vector(p3_vector)

        p2.close()

        snap4 = [s.n for s in REG]
        p4 = ExitStack()
        ln2_fm = p4.enter_context(nc.sbuf_tensor("ln2_fm", [P, KC * T], BF16))
        x1_sb = p4.enter_context(nc.sbuf_tensor("x1_sb", [P, 2 * D], F32))
        ln2_tm = p4.enter_context(nc.sbuf_tensor("ln2_tm", [P, 2 * D], BF16))
        outs_sb = p4.enter_context(nc.sbuf_tensor("outs_sb", [P, 2 * D], F32))
        GA = QT * NEXP
        ga_sb = p4.enter_context(nc.sbuf_tensor("ga_sb", [P, GA], F32))
        gl_sb = p4.enter_context(nc.sbuf_tensor("gl_sb", [P, GA], F32))
        gx_sb = p4.enter_context(nc.sbuf_tensor("gx_sb", [P, GA], F32))
        gm_sb = p4.enter_context(nc.sbuf_tensor("gm_sb", [P, 4 * QT], F32))
        grep_sb = p4.enter_context(nc.sbuf_tensor("grep_sb", [P, 2 * GA], F32))
        gt_sb = p4.enter_context(nc.sbuf_tensor("gt_sb", [NEXP, QT * P], F32))
        mid_sb = p4.enter_context(nc.sbuf_tensor("mid_sb", [ER, QT * P], BF16))
        gbs_sb = p4.enter_context(nc.sbuf_tensor("gbs_sb", [ER, QT * P], F32))
        p4ps = ExitStack()
        tps2 = p4ps.enter_context(nc.psum_tensor("tps2", [P, 1024], BF16))
        ln2_fm_r = ln2_fm.ap().rearrange("p (kc t) -> p kc t", kc=KC)

        def p4a_sync(sy):
            barrier(sy, snap4)

            def x1_load(qt):
                s_x1[qt % 2].dma(sy.dma_start(
                    out=x1_sb[:, (qt % 2) * D:(qt % 2) * D + D],
                    in_=x1_d.ap()[qt * P:(qt + 1) * P, :]))

            for qt in range(min(2, QT)):
                x1_load(qt)
            for qt in range(2, QT):
                sy.wait_ge(s_dveL2.sem, 7 * (qt - 2) + 7)
                sy.wait_ge(s_actL2.sem, 3 * (qt - 2) + 1)
                x1_load(qt)
        block.sync(p4a_sync)

        def p4a_vec(v):
            barrier(v, snap4)
            for qt in range(QT):
                v.wait_ge(s_x1[qt % 2].sem, 16 * (qt // 2 + 1))
                emit_ln_vec(v, qt, x1_sb[:, (qt % 2) * D:(qt % 2) * D + D],
                            s_dveL2, s_actL2, s_peTR2, ln2_tm)
                for g in range(NG):
                    G = NG * qt + g
                    v.wait_ge(s_peTR2.sem, G + 1)
                    if s_dveTR2.n:
                        v.wait_ge(s_dveTR2.sem, s_dveTR2.n)
                    s_dveTR2.inc(v.tensor_copy(
                        out=ln2_fm_r[:, g * 8:(g + 1) * 8, qt * P:(qt + 1) * P],
                        in_=tps2.ap()[:, 0:1024]
                        .rearrange("p (kc t) -> p kc t", kc=8)))
        block.vector(p4a_vec)

        def p4a_scalar(sc):
            barrier(sc, snap4)
            for qt in range(QT):
                emit_ln_act(sc, qt, x1_sb[:, (qt % 2) * D:(qt % 2) * D + D],
                            s_dveL2, s_actL2,
                            lambda qt=qt: sc.wait_ge(s_x1[qt % 2].sem,
                                                     16 * (qt // 2 + 1)))
        block.scalar(p4a_scalar)

        def p4a_tensor(t):
            barrier(t, snap4)
            for qt in range(QT):
                t.wait_ge(s_dveL2.sem, 7 * qt + 7)
                for g in range(NG):
                    G = NG * qt + g
                    if G >= 1:
                        t.wait_ge(s_dveTR2.sem, G)
                    for j in range(8):
                        kc = g * 8 + j
                        mm = t.transpose(
                            tps2[:, j * P:(j + 1) * P],
                            in_=ln2_tm[:, (qt % 2) * D + kc * P:(qt % 2) * D + (kc + 1) * P],
                            identity=ident_bf[:])
                    s_peTR2.inc(mm)
        block.tensor(p4a_tensor)

        p4ps.close()
        lgps = p4.enter_context(nc.psum_tensor("lgps", [P, GA], F32))
        mid_all = p4.enter_context(nc.psum_tensor("mid_all", [ER, QT * P], F32))
        gtps = p4.enter_context(nc.psum_tensor("gtps", [NEXP, 2 * P], F32))
        gb_all = p4.enter_context(nc.psum_tensor("gb_all", [ER, QT * P], F32))
        dps = [p4.enter_context(nc.psum_tensor(f"dps{i}", [P, 512], F32))
               for i in range(2)]
        lg3 = lgps.ap().rearrange("p (qt e) -> p qt e", e=NEXP)

        def p4b_tensor(t):
            t.wait_ge(s_dveTR2.sem, NG * QT)
            for qt in range(QT):
                for kc in range(KC):
                    mm = t.matmul(lgps[:, qt * NEXP:(qt + 1) * NEXP],
                                  lhsT=ln2_fm[:, kc * T + qt * P:kc * T + (qt + 1) * P],
                                  rhs=wr_sb[:, kc * NEXP:(kc + 1) * NEXP],
                                  start=(kc == 0), stop=(kc == KC - 1))
            s_peLG.inc(mm)
            for qt in range(QT):
                for kc in range(KC):
                    mm = t.matmul(mid_all[:, qt * P:(qt + 1) * P],
                                  lhsT=wa_sb[:, kc * ER:(kc + 1) * ER],
                                  rhs=ln2_fm[:, kc * T + qt * P:kc * T + (qt + 1) * P],
                                  start=(kc == 0), stop=(kc == KC - 1))
            s_peMID.inc(mm)
            t.wait_ge(s_dveG.sem, 22)
            for qt in range(QT):
                if qt >= 1:
                    t.wait_ge(s_dveE2.sem, qt)
                s_peGT.inc(t.transpose(
                    gtps[:, 0:P],
                    in_=ga_sb[:, qt * NEXP:(qt + 1) * NEXP],
                    identity=ident_f[:]))
            for qt in range(QT):
                t.wait_ge(s_dveE2.sem, qt + 1)
                s_peGB.inc(t.matmul(
                    gb_all[:, qt * P:(qt + 1) * P], lhsT=ind_sb[:],
                    rhs=gt_sb[0:NEXP, qt * P:(qt + 1) * P],
                    start=True, stop=True))
            t.wait_ge(s_dveE2.sem, QT + 2)
            for qt in range(QT):
                for og in range(VOG):
                    i = VOG * qt + og
                    if i >= 2:
                        t.wait_ge(s_dveF.sem, i - 1)
                    s_peD.inc(t.matmul(dps[i % 2][:],
                                       lhsT=mid_sb[:, qt * P:(qt + 1) * P],
                                       rhs=wb_sb[:, og * 512:(og + 1) * 512],
                                       start=True, stop=True))
        block.tensor(p4b_tensor)

        def p4b_scalar(sc):
            sc.wait_ge(s_dveG.sem, 14)
            chain(sc, s_actG, lambda: sc.activation(out=gx_sb[:], in_=gl_sb[:],
                                                    func=AF.Exp))
        block.scalar(p4b_scalar)

        def p4b_vec(v):
            v.wait_ge(s_peLG.sem, 1)
            ch = lambda fn: chain(v, s_dveG, fn)
            ch(lambda: v.reduce_max(out=gm_sb[:, 0:QT], in_=lg3, axis=AX.X))
            for e in range(NEXP):
                ch(lambda e=e: v.tensor_copy(
                    out=bass.AP(grep_sb, e, [[2 * GA, P], [NEXP, QT]]),
                    in_=gm_sb[:, 0:QT]))
            ch(lambda: v.tensor_tensor(out=ga_sb[:], in0=lgps[:],
                                       in1=grep_sb[:, 0:GA], op=ALU.is_ge))
            ch(lambda: v.scalar_tensor_tensor(out=gl_sb[:], in0=ga_sb[:],
                                              scalar=-1e30, in1=lgps[:],
                                              op0=ALU.mult, op1=ALU.add))
            ch(lambda: v.reduce_max(out=gm_sb[:, QT:2 * QT],
                                    in_=gl_sb.ap().rearrange("p (qt e) -> p qt e", e=NEXP),
                                    axis=AX.X))
            for e in range(NEXP):
                ch(lambda e=e: v.tensor_copy(
                    out=bass.AP(grep_sb, GA + e, [[2 * GA, P], [NEXP, QT]]),
                    in_=gm_sb[:, QT:2 * QT]))
            ch(lambda: v.tensor_tensor(out=ga_sb[:], in0=lgps[:],
                                       in1=grep_sb[:, GA:2 * GA], op=ALU.is_ge))
            ch(lambda: v.tensor_tensor(out=gl_sb[:], in0=lgps[:],
                                       in1=grep_sb[:, 0:GA], op=ALU.subtract))
            assert s_dveG.n == 14, s_dveG.n
            v.wait_ge(s_actG.sem, 1)
            ch(lambda: v.tensor_tensor(out=gx_sb[:], in0=gx_sb[:], in1=ga_sb[:],
                                       op=ALU.mult))
            ch(lambda: v.reduce_sum(out=gm_sb[:, 2 * QT:3 * QT],
                                    in_=gx_sb.ap().rearrange("p (qt e) -> p qt e", e=NEXP),
                                    axis=AX.X))
            ch(lambda: v.reciprocal(out=gm_sb[:, 3 * QT:4 * QT],
                                    in_=gm_sb[:, 2 * QT:3 * QT]))
            for e in range(NEXP):
                ch(lambda e=e: v.tensor_copy(
                    out=bass.AP(grep_sb, e, [[2 * GA, P], [NEXP, QT]]),
                    in_=gm_sb[:, 3 * QT:4 * QT]))
            ch(lambda: v.tensor_tensor(out=ga_sb[:], in0=gx_sb[:],
                                       in1=grep_sb[:, 0:GA], op=ALU.mult))
            assert s_dveG.n == 22, s_dveG.n
            for qt in range(QT):
                v.wait_ge(s_peGT.sem, qt + 1)
                chain(v, s_dveE2, lambda qt=qt: v.tensor_copy(
                    out=gt_sb[:, qt * P:(qt + 1) * P],
                    in_=gtps[:, 0:P]))
            v.wait_ge(s_peGB.sem, QT)
            chain(v, s_dveE2, lambda: v.tensor_copy(out=gbs_sb[:], in_=gb_all[:]))
            v.wait_ge(s_peMID.sem, 1)
            chain(v, s_dveE2, lambda: v.tensor_tensor(
                out=mid_sb[:], in0=mid_all[:], in1=gbs_sb[:], op=ALU.mult))
            for qt in range(QT):
                for og in range(VOG):
                    i = VOG * qt + og
                    v.wait_ge(s_peD.sem, i + 1)
                    v.wait_ge(s_x1b[qt % 2].sem, 16 * (qt // 2 + 1))
                    if qt >= 2 and og == 0:
                        v.wait_ge(s_outF[qt % 2].sem, 16 * (qt // 2))
                    if s_dveF.n:
                        v.wait_ge(s_dveF.sem, s_dveF.n)
                    s_dveF.inc(v.tensor_tensor(
                        out=outs_sb[:, (qt % 2) * D + og * 512:(qt % 2) * D + (og + 1) * 512],
                        in0=dps[i % 2][:],
                        in1=x1_sb[:, (qt % 2) * D + og * 512:(qt % 2) * D + (og + 1) * 512],
                        op=ALU.add))
        block.vector(p4b_vec)

        def p4b_sync(sy):
            def x1b_load(qt):
                s_x1b[qt % 2].dma(sy.dma_start(
                    out=x1_sb[:, (qt % 2) * D:(qt % 2) * D + D],
                    in_=x1_d.ap()[qt * P:(qt + 1) * P, :]))

            for qt in range(min(2, QT)):
                sy.wait_ge(s_dveL2.sem, 7 * (QT - 2 + qt) + 7)
                sy.wait_ge(s_actL2.sem, 3 * (QT - 2 + qt) + 1)
                x1b_load(qt)
            for qt in range(QT):
                sy.wait_ge(s_dveF.sem, VOG * qt + VOG)
                s_outF[qt % 2].dma(sy.dma_start(
                    out=out_d.ap()[qt * P:(qt + 1) * P, :],
                    in_=outs_sb[:, (qt % 2) * D:(qt % 2) * D + D]))
                if qt + 2 < QT:
                    x1b_load(qt + 2)
            for s in s_outF:
                if s.n:
                    sy.wait_ge(s.sem, s.n)
        block.sync(p4b_sync)

        def p4_gps_end(g):
            for s in s_outF:
                if s.n:
                    g.wait_ge(s.sem, s.n)
        block.gpsimd(p4_gps_end)

        p4.close()

    return nc, dict(D=D, S=S, T=T)


def _host_prep(D, S, x, qkvw, qkvb, ow, ob, A, B, Wr):
    KC = D // P
    OCH = 2 * D // P
    T = S // NCORES
    x2 = np.ascontiguousarray(np.asarray(x, dtype=np.float32).reshape(S, D))
    qkvw = np.asarray(qkvw, dtype=np.float32)
    w_qk = np.ascontiguousarray(
        qkvw[:, :2 * D].reshape(KC, P, OCH, P).transpose(2, 1, 0, 3).reshape(OCH, P, D))
    w_v = np.ascontiguousarray(qkvw[:, 2 * D:].reshape(KC, P, D))
    qkvb = np.asarray(qkvb, dtype=np.float32)
    b_qk = np.ascontiguousarray(qkvb[:2 * D].reshape(OCH, P).T)
    b_v = np.ascontiguousarray(qkvb[2 * D:].reshape(1, D))
    w_o = np.ascontiguousarray(np.asarray(ow, dtype=np.float32).reshape(KC, P, D))
    b_o = np.ascontiguousarray(np.asarray(ob, dtype=np.float32).reshape(1, D))
    A = np.asarray(A, dtype=np.float32)
    w_a = np.ascontiguousarray(A.transpose(2, 0, 1).reshape(KC, P, ER))
    B = np.asarray(B, dtype=np.float32)
    w_b = np.ascontiguousarray(B.transpose(0, 2, 1).reshape(ER, D))
    Wr = np.asarray(Wr, dtype=np.float32)
    w_r = np.ascontiguousarray(Wr.T.reshape(KC, P, NEXP))
    ident = np.eye(P, dtype=np.float32)
    ind4 = np.zeros((NEXP, ER), dtype=np.float32)
    for e in range(NEXP):
        ind4[e, e * LORA_R:(e + 1) * LORA_R] = LORA_SCALE
    shared = dict(w_qk=w_qk, w_v=w_v, w_o=w_o, b_qk=b_qk, b_v=b_v, b_o=b_o,
                  w_a=w_a, w_b=w_b, w_r=w_r, ident=ident, ind4=ind4)
    in_maps = []
    for c in range(NCORES):
        m = dict(shared)
        m["x"] = np.ascontiguousarray(x2[c * T:(c + 1) * T, :])
        in_maps.append(m)
    return in_maps


def kernel(L0_attn_x, L0_attn_attn_qkvw, L0_attn_attn_qkvb, L0_attn_attn_ow,
           L0_attn_attn_ob, L0_ffn_A, L0_ffn_B, L0_ffn_Wr):
    D = 4096
    S = 8192
    in_maps = _host_prep(D, S, L0_attn_x, L0_attn_attn_qkvw, L0_attn_attn_qkvb,
                         L0_attn_attn_ow, L0_attn_attn_ob, L0_ffn_A, L0_ffn_B,
                         L0_ffn_Wr)
    nc, meta = build_program(D, S)
    res = run_bass_kernel_spmd(nc, in_maps, core_ids=list(range(NCORES)))
    out = np.concatenate([np.asarray(r["out"], dtype=np.float32)
                          for r in res.results], axis=0)
    return out.reshape(1, S, D)
